# version 17
# baseline (speedup 1.0000x reference)
import numpy as np
import ml_dtypes
from contextlib import ExitStack
import concourse.bass as bass
import concourse.mybir as mybir
from concourse.bass_utils import run_bass_kernel_spmd

F32 = mybir.dt.float32
BF16 = mybir.dt.bfloat16
AF = mybir.ActivationFunctionType
ALU = mybir.AluOpType

T = 2048
DM = 2048
DI = 4096
NH = 32
QC = 128
NCH = T // QC
NTT = T // 128
KD = DM // 128
KI = DI // 128
PROJ = 8480
NE = 8
TH = T // 2
CLAMP = 30.0
LN_EPS = 1e-5
RMS_EPS = 1e-6
NEG = -1e30


class Prog:
    ENGINES = ("tensor", "vector", "scalar", "sync", "gpsimd")

    def __init__(self, nc, ctx):
        self.nc = nc
        self.q = {e: [] for e in self.ENGINES}
        self.cnt = {}
        self.sems = {}
        self._lw = {e: {} for e in self.ENGINES}
        for s in ("wd", "ld", "tr", "pe", "dv", "sc", "st"):
            self.sems[s] = ctx.enter_context(nc.semaphore("sem_" + s))
            self.cnt[s] = 0

    def v(self, s):
        return self.cnt[s]

    def on(self, eng, fn, wait=(), inc=()):
        w = []
        for s, val in wait:
            if val <= 0 or self._lw[eng].get(s, 0) >= val:
                continue
            self._lw[eng][s] = val
            w.append((s, val))
        for s, a in inc:
            self.cnt[s] += a
        self.q[eng].append((fn, tuple(w), tuple(inc)))

    def fence_all(self):
        vals = [(x, self.cnt[x]) for x in
                ("pe", "dv", "sc", "ld", "tr", "wd", "st")]
        for eng in self.ENGINES:
            self.on(eng, None, wait=vals)

    def flush(self, block):
        for eng in self.ENGINES:
            ops = self.q[eng]
            if not ops:
                continue

            def body(e, ops=ops):
                for fn, waits, incs in ops:
                    for s, val in waits:
                        e.wait_ge(self.sems[s], val)
                    inst = fn(e) if fn is not None else None
                    for s, a in incs:
                        inst.then_inc(self.sems[s], a)

            getattr(block, eng)(body)


class SbufPlan:
    def __init__(self, nc):
        self.nc = nc
        self.items = []

    def add(self, name, shape, dtype, p0, p1):
        nb = int(np.prod(shape[1:])) * mybir.dt.size(dtype)
        self.items.append((name, shape, dtype, p0, p1, nb))

    def alloc(self, base, budget):
        placed, offs = [], {}
        for name, shape, dtype, p0, p1, nb in sorted(self.items,
                                                     key=lambda x: -x[5]):
            nba = (nb + 31) // 32 * 32
            off = 0
            while True:
                hit = None
                for o, e, q0, q1 in placed:
                    if o < off + nba and off < e and not (q1 < p0 or p1 < q0):
                        hit = e
                        break
                if hit is None:
                    break
                off = (hit + 31) // 32 * 32
            placed.append((off, off + nba, p0, p1))
            offs[name] = (off, shape, dtype)
        self.top = max(e for _, e, _, _ in placed)
        assert self.top <= budget, f"sbuf over budget: {self.top} > {budget}"
        return {n: self.nc.alloc_sbuf_tensor_at(n, sh, dt, offset=base + o)
                for n, (o, sh, dt) in offs.items()}


def build(taps=()):
    taps = set(taps)
    nc = bass.Bass("TRN2", target_bir_lowering=False)
    dr = {}

    def din(name, shape, dtype=F32):
        dr[name] = nc.dram_tensor(name, shape, dtype, kind="ExternalInput").ap()

    din("x_in", [T, DM]); din("w_in", [DM, PROJ]); din("conv_w", [4, DI])
    din("conv_b", [DI]); din("a_log", [NH]); din("dt_bias", [NH])
    din("d_vec", [DI]); din("w_out", [DI, DM]); din("nsc", [DM])
    din("r_w", [DM, NE]); din("r_b", [NE]); din("e_w", [NE, DM, DM])
    din("e_b", [NE, DM]); din("p_w", [DM, DM]); din("p_b", [DM]); din("n_w", [DM])
    din("c_iden_bf", [128, 128], BF16); din("c_iden_f", [128, 128])
    din("c_tri", [128, 128]); din("c_sel32", [NH, NH * 128])
    din("c_sel8", [NE, NE * 128]); din("c_ones", [1, 128])
    din("c_sel32b", [NH, NH * 128], BF16)
    dr["out"] = nc.dram_tensor("out", [T, DM], F32, kind="ExternalOutput").ap()
    for nm, shape, dt in [("scr_gate", [128, NH * T], BF16),
                          ("scr_ssm", [128, NH * T], BF16),
                          ("scr_y", [128, NH * T], BF16),
                          ("scr_x1", [T, DM], F32),
                          ("scr_w1t", [128, KD * T], BF16),
                          ("scr_wo", [128, KI * DM], BF16),
                          ("scr_pw", [128, 4 * KD * 512], BF16),
                          ("scr_ew", [128, NE * 4 * KD * 512], BF16)]:
        dr[nm] = nc.dram_tensor(nm, shape, dt, kind="Internal").ap()

    tap_specs = {}

    def mktap(name, shape, dtype=F32):
        if name not in taps:
            return None
        ap = nc.dram_tensor("tap_" + name, shape, dtype,
                            kind="ExternalOutput").ap()
        tap_specs[name] = ap
        return ap

    with ExitStack() as ctx:
        pl = SbufPlan(nc)
        A = pl.add
        A("iden_bf", [128, 128], BF16, 0, 5)
        A("iden_f", [128, 128], F32, 0, 5)
        A("tri", [128, 128], F32, 0, 5)
        A("ones_f", [1, 128], F32, 0, 5)
        A("sel32b", [NH, NH * 128], BF16, 0, 2)
        A("sel8", [NE, NE * 128], F32, 0, 5)
        A("cwcol", [128, 128], F32, 0, 1)
        A("cbcol", [128, 32], F32, 0, 1)
        A("dcol", [128, 32], F32, 0, 2)
        A("ebcol", [128, 128], F32, 0, 5)
        A("dtbcol", [32, 1], F32, 0, 1)
        A("acol", [32, 1], F32, 0, 1)
        A("wr_bf", [128, 128], BF16, 0, 3)
        A("rb_b", [128, 8], F32, 0, 3)
        A("astats", [128, 2 * NTT], F32, 0, 3)
        A("x1stats", [128, 2 * NTT], F32, 3, 5)
        A("nscb", [128, DM], F32, 0, 0)
        A("fwt", [8, T], F32, 3, 5)
        A("x0t", [128, 2 * DM], F32, 0, 0)
        A("zrst", [128, DM], F32, 0, 0)
        A("h2s", [128, 2 * DM], BF16, 0, 0)
        A("h2T", [128, KD * T], BF16, 0, 1)
        A("wpanB", [128, 2 * KD * 512], BF16, 0, 1)
        A("convrow", [128, T], F32, 0, 1)
        A("convtmp", [128, T], F32, 0, 1)
        A("ssmrow", [128, 2 * T], BF16, 0, 1)
        A("sgrow", [128, 2 * T], BF16, 1, 1)
        A("bc_bf", [128, T], BF16, 1, 2)
        A("cc_bf", [128, T], BF16, 1, 2)
        A("cumv", [32, T], F32, 1, 2)
        A("cumh", [32, T], BF16, 1, 2)
        A("cumm", [32, T], BF16, 1, 2)
        A("cuml", [32, T], BF16, 1, 2)
        A("dtv", [32, T], F32, 1, 2)
        A("excum", [32, T], BF16, 1, 2)
        A("t32a", [32, T], F32, 1, 1)
        A("t32b", [32, T], F32, 1, 1)
        A("wst2", [32, 128], F32, 2, 2)
        A("acol2", [32, 1], F32, 0, 1)
        A("ywin", [128, 2 * NH * QC], BF16, 2, 2)
        A("gwin", [128, 2 * NH * QC], BF16, 2, 2)
        A("xtch", [128, NH * 128], BF16, 2, 2)
        A("s_st", [128, NH * 128], F32, 2, 2)
        A("sbf", [128, NH * 128], BF16, 2, 2)
        A("cjlt", [128, 32], F32, 2, 2)
        A("wcolt", [128, 32], F32, 2, 2)
        A("cdbc", [128, 32], F32, 2, 2)
        A("cdrow", [1, 32], F32, 2, 2)
        A("cdrow2", [1, 32], F32, 2, 2)
        A("cbtri", [128, 128], F32, 2, 2)
        A("e_t", [128, 2 * 128], F32, 2, 2)
        A("et2", [128, 2 * 128], F32, 2, 2)
        A("dt_t", [128, 2 * 128], F32, 2, 2)
        A("mt_bf", [128, 2 * 128], BF16, 2, 2)
        A("ce_bf", [128, 2 * 128], BF16, 2, 2)
        A("bw_bf", [128, 2 * 128], BF16, 2, 2)
        A("bct_bf", [128, 128], BF16, 2, 2)
        A("lnst", [32, 128], F32, 2, 2)
        A("cjlst", [32, 128], F32, 2, 2)
        A("wst", [32, 128], F32, 2, 2)
        A("ylhs", [128, 4 * KI * 128], BF16, 3, 3)
        A("wpanD", [128, 2 * KI * 512], BF16, 3, 3)
        A("dstage", [128, 4 * DM], F32, 3, 3)
        A("x0t2", [128, 2 * DM], F32, 3, 3)
        A("lnt2", [128, DM], F32, 3, 3)
        A("w1bf", [128, 6 * DM], BF16, 3, 3)
        A("w1tst", [128, 2 * DM], BF16, 3, 3)
        A("m8", [128, 8], F32, 3, 3)
        A("mv8", [128, 8], F32, 3, 3)
        A("mr8", [128, 8], F32, 3, 3)
        A("lg8", [128, 8], F32, 3, 3)
        A("fw8", [128, 16], F32, 3, 3)
        A("comb", [128, KD * TH], F32, 4, 5)
        A("web", [128, 2 * KD * 512], BF16, 4, 4)
        A("w1blk", [128, KD * TH], BF16, 4, 4)
        A("fwb", [128, TH], F32, 4, 4)
        A("silu1", [128, 512], F32, 4, 4)
        A("silu2", [128, 512], F32, 4, 4)
        A("prodt", [128, 512], F32, 4, 4)
        A("cbstage", [128, 4 * KD * 128], BF16, 5, 5)
        A("wpanF", [128, 2 * KD * 512], BF16, 5, 5)
        A("x1t", [128, 2 * DM], F32, 5, 5)
        A("fstage", [128, 4 * DM], F32, 5, 5)
        A("fyt", [128, DM], F32, 5, 5)
        A("pbb", [128, DM], F32, 4, 5)
        A("nwb", [128, DM], F32, 4, 5)
        A("cst4", [128, 4], F32, 0, 5)
        A("bn24", [128, 48], F32, 0, 5)
        A("mv2", [128, 4], F32, 0, 5)
        A("rc", [128, 8], F32, 0, 5)
        base = (nc.sbuf_base + 4096 + 31) // 32 * 32
        sb = pl.alloc(base, nc.sbuf_base + 212000 - base)

        p_mm = [ctx.enter_context(nc.psum_tensor("p_mm0", [128, 512], F32)),
                ctx.enter_context(nc.psum_tensor("p_mm1", [128, 512], F32))]
        p_cb = ctx.enter_context(nc.psum_tensor("p_cb", [128, 512], F32))
        p_bc = ctx.enter_context(nc.psum_tensor("p_bc", [128, 512], F32))
        p_y = ctx.enter_context(nc.psum_tensor("p_y", [128, 512], F32))
        p_sl = ctx.enter_context(nc.psum_tensor("p_sl", [128, 512], F32))
        p_tr = ctx.enter_context(nc.psum_tensor("p_tr", [128, 1024], BF16))
        p_x = ctx.enter_context(nc.psum_tensor("p_x", [128, 512], F32))
        psd = dict(mm=p_mm, cb=p_cb, bc=p_bc, y=p_y, sl=p_sl, tr=p_tr, x=p_x)

        P = Prog(nc, ctx)
        blk = ctx.enter_context(nc.Block())
        _emit(nc, P, sb, psd, dr, mktap)
        P.flush(blk)
    return nc, tap_specs


def _emit(nc, P, sb, psd, dr, mktap):
    S = lambda n: sb[n]
    p_mm, p_cb, p_bc = psd["mm"], psd["cb"], psd["bc"]
    p_y, p_sl, p_tr, p_x = psd["y"], psd["sl"], psd["tr"], psd["x"]

    DMA = lambda dst, src: (lambda e: e.dma_start(dst, src))
    DMAS = lambda dst, src: (lambda e: e.dma_start(
        dst, src, allow_slow_non_contiguous=True))
    DMAT = lambda dst, src: (lambda e: e.dma_start_transpose(dst, src))
    MM = lambda o, l, r, s0, s1: (lambda e: e.matmul(o, l, r, start=s0, stop=s1))
    TRN = lambda o, i, idn: (lambda e: e.transpose(o, i, idn))
    CPY = lambda o, i: (lambda e: e.tensor_copy(o, i))
    MEMS = lambda o, v: (lambda e: e.memset(o, v))

    def SCPY(o, i, bias=0.0, scale=1.0, func=AF.Copy):
        if func == AF.Copy and not (isinstance(bias, float)
                                    and isinstance(scale, float)):
            func = AF.Identity
        return lambda e: e.activation(o, i, func, bias=bias, scale=scale)

    TT = lambda o, a, b, op: (lambda e: e.tensor_tensor(o, a, b, op))

    def TS(o, a, s1, s2, op0, op1=ALU.bypass):
        return lambda e: e.tensor_scalar(o, a, s1, s2, op0=op0, op1=op1)

    def STT(o, a, s, b, op0, op1):
        return lambda e: e.scalar_tensor_tensor(o, a, s, b, op0=op0, op1=op1)

    P.on("vector", MEMS(S("cst4")[:, 0:1], LN_EPS), inc=[("dv", 1)])
    P.on("vector", MEMS(S("cst4")[:, 1:2], RMS_EPS), inc=[("dv", 1)])
    for dst, srcn in [(S("iden_bf")[:], "c_iden_bf"), (S("iden_f")[:], "c_iden_f"),
                      (S("tri")[:], "c_tri"), (S("ones_f")[:], "c_ones"),
                      (S("sel32b")[:], "c_sel32b"), (S("sel8")[:], "c_sel8")]:
        P.on("sync", DMA(dst, dr[srcn]), inc=[("ld", 16)])
    P.on("sync", DMA(S("dtbcol")[:, 0:1], dr["dt_bias"].rearrange("(h o) -> h o", o=1)),
         inc=[("ld", 16)])
    P.on("sync", DMA(S("acol")[:, 0:1], dr["a_log"].rearrange("(h o) -> h o", o=1)),
         inc=[("ld", 16)])
    P.on("sync", DMAS(S("dcol")[:], dr["d_vec"].rearrange("(h p) -> p h", p=128)),
         inc=[("ld", 16)])
    P.on("sync", DMAS(S("cbcol")[:], dr["conv_b"].rearrange("(f p) -> p f", p=128)),
         inc=[("ld", 16)])
    P.on("sync", DMAS(S("ebcol")[:].rearrange("p (e f) -> p e f", f=16),
                     dr["e_b"].rearrange("e (f p) -> p e f", p=128)),
         inc=[("ld", 16)])
    P.on("sync", DMAS(S("cwcol")[:].rearrange("p (k f) -> p k f", f=32),
                      dr["conv_w"].rearrange("k (f p) -> p k f", p=128)),
         inc=[("ld", 16)])
    P.on("gpsimd", DMA(S("wr_bf")[:].rearrange("p (kt e) -> p kt e", kt=KD),
                       dr["r_w"].rearrange("(kt p) e -> p kt e", p=128)),
         inc=[("wd", 16)])
    wr_wd = P.v("wd")
    P.on("sync", DMA(S("nscb")[0:1, :], dr["nsc"].rearrange("(o d) -> o d", o=1)),
         inc=[("ld", 16)])
    P.on("sync", DMA(S("rb_b")[0:1, :], dr["r_b"].rearrange("(o e) -> o e", o=1)),
         inc=[("ld", 16)])
    ld0 = P.v("ld")

    P.on("scalar", SCPY(S("acol2")[:, 0:1], S("acol")[:, 0:1], func=AF.Exp),
         wait=[("ld", ld0)], inc=[("sc", 1)])
    P.on("vector", TS(S("acol")[:, 0:1], S("acol2")[:, 0:1], -1.0, None, ALU.mult),
         wait=[("sc", P.v("sc"))], inc=[("dv", 1)])
    fb_bc = [0]
    for b in range(4):
        P.on("tensor", MM(p_bc[:, 0:512], S("ones_f")[:],
                          S("nscb")[0:1, b * 512:(b + 1) * 512], True, True),
             wait=[("ld", ld0), ("dv", fb_bc[0])], inc=[("pe", 1)])
        P.on("vector", CPY(S("nscb")[:, b * 512:(b + 1) * 512], p_bc[:, 0:512]),
             wait=[("pe", P.v("pe"))], inc=[("dv", 1)])
        fb_bc[0] = P.v("dv")
    P.on("tensor", MM(p_bc[:, 0:8], S("ones_f")[:], S("rb_b")[0:1, :], True, True),
         wait=[("dv", fb_bc[0])], inc=[("pe", 1)])
    P.on("vector", CPY(S("rb_b")[:], p_bc[:, 0:8]),
         wait=[("pe", P.v("pe"))], inc=[("dv", 1)])
    fb_bc[0] = P.v("dv")
    init_dv = P.v("dv")

    def ln_stats(src, par, eps, mean_sq=False, pre_wait=()):
        eps_ap = S("cst4")[:, 0:1] if eps == LN_EPS else S("cst4")[:, 1:2]
        b0 = par * 24
        for i in range(4):
            P.on("vector", lambda e, i=i, src=src, b0=b0: e.bn_stats(
                S("bn24")[:, b0 + i * 6:b0 + (i + 1) * 6],
                src[:, i * 512:(i + 1) * 512]),
                 wait=(list(pre_wait) if i == 0 else []), inc=[("dv", 1)])
        P.on("vector", lambda e, b0=b0, par=par: e.bn_aggr(
            S("mv2")[:, par * 2:par * 2 + 2], S("bn24")[:, b0:b0 + 24]),
             wait=[("dv", P.v("dv"))], inc=[("dv", 1)])
        c = par * 4
        if mean_sq:
            P.on("vector", STT(S("rc")[:, c + 2:c + 3],
                               S("mv2")[:, par * 2:par * 2 + 1],
                               S("mv2")[:, par * 2:par * 2 + 1],
                               S("mv2")[:, par * 2 + 1:par * 2 + 2],
                               ALU.mult, ALU.add),
                 wait=[("dv", P.v("dv"))], inc=[("dv", 1)])
            var_ap = S("rc")[:, c + 2:c + 3]
        else:
            var_ap = S("mv2")[:, par * 2 + 1:par * 2 + 2]
        dvv = P.v("dv")
        P.on("scalar", SCPY(S("rc")[:, c + 3:c + 4], var_ap, bias=eps_ap,
                            func=AF.Sqrt),
             wait=[("dv", dvv)], inc=[("sc", 1)])
        P.on("vector", lambda e, c=c: e.reciprocal(S("rc")[:, c:c + 1],
                                                   S("rc")[:, c + 3:c + 4]),
             wait=[("sc", P.v("sc"))], inc=[("dv", 1)])
        P.on("vector", TS(S("rc")[:, c + 1:c + 2],
                          S("mv2")[:, par * 2:par * 2 + 1],
                          S("rc")[:, c:c + 1], -1.0, ALU.mult, ALU.mult),
             wait=[("dv", P.v("dv"))], inc=[("dv", 1)])
        return P.v("dv"), c

    h2T = S("h2T")
    ptr_free = [0]
    a_free = {0: (0, 0), 1: (0, 0)}
    astat_sc = {0: 0, 1: 0}
    a_done_sc = {}
    for tt in range(NTT):
        par = tt % 2
        x0 = S("x0t")[:, par * DM:(par + 1) * DM]
        fsc, ftr = a_free[par]
        P.on("sync", DMA(x0, dr["x_in"][tt * 128:(tt + 1) * 128, :]),
             wait=[("ld", ld0), ("sc", fsc), ("sc", ftr)], inc=[("ld", 16)])
        ldv = P.v("ld")
        b0 = par * 24
        for i in range(4):
            P.on("vector", lambda e, i=i, x0=x0, b0=b0: e.bn_stats(
                S("bn24")[:, b0 + i * 6:b0 + (i + 1) * 6],
                x0[:, i * 512:(i + 1) * 512]),
                 wait=([("ld", ldv), ("dv", init_dv),
                        ("sc", astat_sc[par])] if i == 0 else []),
                 inc=[("dv", 1)])
        P.on("vector", lambda e, b0=b0, par=par: e.bn_aggr(
            S("mv2")[:, par * 2:par * 2 + 2], S("bn24")[:, b0:b0 + 24]),
             wait=[("dv", P.v("dv"))], inc=[("dv", 1)])
        agg_dv = P.v("dv")
        c = par * 4
        mean_ap = S("mv2")[:, par * 2:par * 2 + 1]
        var_ap = S("mv2")[:, par * 2 + 1:par * 2 + 2]
        P.on("vector", TS(S("rc")[:, c + 3:c + 4], var_ap,
                          1.0 + RMS_EPS, RMS_EPS * LN_EPS, ALU.mult, ALU.add),
             wait=[("dv", agg_dv)], inc=[("dv", 1)])
        P.on("scalar", SCPY(S("rc")[:, c + 2:c + 3], S("rc")[:, c + 3:c + 4],
                            func=AF.Sqrt),
             wait=[("dv", P.v("dv"))], inc=[("sc", 1)])
        P.on("vector", lambda e, c=c: e.reciprocal(S("rc")[:, c:c + 1],
                                                   S("rc")[:, c + 2:c + 3]),
             wait=[("sc", P.v("sc"))], inc=[("dv", 1)])
        P.on("vector", TS(S("rc")[:, c + 1:c + 2], mean_ap,
                          S("rc")[:, c:c + 1], -1.0, ALU.mult, ALU.mult),
             wait=[("dv", P.v("dv"))], inc=[("dv", 1)])
        main_dv = P.v("dv")
        ac = S("astats")[:, 2 * tt:2 * tt + 1]
        ac1 = S("astats")[:, 2 * tt + 1:2 * tt + 2]
        P.on("scalar", SCPY(ac, var_ap, bias=S("cst4")[:, 0:1], func=AF.Sqrt),
             wait=[("dv", agg_dv)], inc=[("sc", 1)])
        P.on("vector", lambda e, ac=ac, ac1=ac1: e.reciprocal(ac1, ac),
             wait=[("sc", P.v("sc"))], inc=[("dv", 1)])
        P.on("vector", TS(ac, mean_ap, ac1, -1.0, ALU.mult, ALU.mult),
             wait=[("dv", P.v("dv"))], inc=[("dv", 1)])
        astat_sc[par] = P.v("sc")
        P.on("scalar", SCPY(S("zrst")[:], x0, bias=S("rc")[:, c + 1:c + 2],
                            scale=S("rc")[:, c:c + 1]),
             wait=[("dv", main_dv), ("dv", P.v("dv"))], inc=[("sc", 1)])
        z_sc = P.v("sc")
        a_sc = P.v("sc")
        h2sl = S("h2s")[:, par * DM:(par + 1) * DM]
        P.on("vector", TT(h2sl, S("zrst")[:], S("nscb")[:], ALU.mult),
             wait=[("sc", a_sc)], inc=[("dv", 1)])
        hdv = P.v("dv")
        h2T_r = h2T[:].rearrange("p (kt t) -> p kt t", kt=KD)
        for bi in range(2):
            for j in range(8):
                kt = bi * 8 + j
                P.on("tensor", TRN(p_tr[:, j * 128:(j + 1) * 128],
                                   h2sl[:, kt * 128:(kt + 1) * 128],
                                   S("iden_bf")[:]),
                     wait=([("dv", hdv), ("sc", ptr_free[0]), ("ld", ld0)]
                           if j == 0 else []),
                     inc=[("pe", 1)])
            pev = P.v("pe")
            P.on("scalar", lambda e, bi=bi, tt=tt, h2T_r=h2T_r: e.copy(
                h2T_r[:, bi * 8:(bi + 1) * 8, tt * 128:(tt + 1) * 128],
                p_tr[:, 0:1024].rearrange("p (j t) -> p j t", j=8)),
                 wait=[("pe", pev)], inc=[("sc", 1)])
            ptr_free[0] = P.v("sc")
        a_free[par] = (z_sc, P.v("sc"))
        a_done_sc[tt] = P.v("sc")
    A_tr = P.v("sc")

    tp = mktap("h2T", [128, KD * T], BF16)
    if tp is not None:
        P.on("sync", DMA(tp, h2T[:]), wait=[("sc", A_tr)], inc=[("st", 16)])

    wpan = S("wpanB")
    PANB = KD * 512
    wpan_free = [0, 0]
    mm_free = [dict(dv=0, sc=0), dict(dv=0, sc=0)]
    unit = [0]

    def load_panel(buf, f0, ncols):
        dst = wpan[:, buf * PANB:buf * PANB + KD * ncols].rearrange(
            "p (kt f) -> p kt f", kt=KD)
        src = dr["w_in"].rearrange("(kt p) f -> p kt f", p=128)[:, :, f0:f0 + ncols]
        for pc in range(4):
            k0, k1 = pc * 4, pc * 4 + 4
            P.on("gpsimd", DMA(dst[:, k0:k1], src[:, k0:k1]),
                 wait=[("pe", wpan_free[buf])], inc=[("wd", 16)])
        return P.v("wd")

    def acc_tile(buf, ncols, fcol, tb, mrows, wd_v, extra=(), a_sc=None):
        par = unit[0] % 2
        unit[0] += 1
        pp = p_mm[par]
        fr = mm_free[par]
        for kt in range(KD):
            w = ([("wd", wd_v), ("sc", A_tr if a_sc is None else a_sc),
                  ("dv", fr["dv"]),
                  ("sc", fr["sc"])] + list(extra)) if kt == 0 else []
            P.on("tensor", MM(pp[0:mrows, 0:512],
                              wpan[:, buf * PANB + kt * ncols + fcol:
                                   buf * PANB + kt * ncols + fcol + mrows],
                              h2T[:, kt * T + tb * 512:kt * T + (tb + 1) * 512],
                              kt == 0, kt == KD - 1),
                 wait=w, inc=[("pe", 1)] if kt == KD - 1 else ())
        return pp, par, P.v("pe")

    spill_free = {0: 0, 1: 0}
    sgrow_free = {0: 0, 1: 0}
    prev_conv = [0, 0]
    prev_silu = [0, 0]

    for fcb in range(8):
        buf = fcb % 2
        wd_v = load_panel(buf, DI + fcb * 512, 512)
        for ft in range(4):
            frow = fcb * 4 + ft
            rbuf = frow % 2
            cw = S("cwcol")
            last_silu = 0
            for tb in range(4):
                pp, par, pe_v = acc_tile(buf, 512, ft * 128, tb, 128, wd_v,
                                         a_sc=a_done_sc[4 * tb + 3])
                P.on("scalar", SCPY(S("convrow")[:, tb * 512:(tb + 1) * 512],
                                    pp[:, 0:512]),
                     wait=[("pe", pe_v),
                           ("dv", prev_conv[0] if tb == 0 else prev_conv[1])],
                     inc=[("sc", 1)])
                mm_free[par]["sc"] = P.v("sc")
                if tb in (1, 3):
                    half = tb // 2
                    a, b2 = half * 1024, (half + 1) * 1024
                    evac_sc = P.v("sc")
                    P.on("vector", TS(S("convtmp")[:, a:b2],
                                      S("convrow")[:, a:b2],
                                      cw[:, 3 * 32 + frow:3 * 32 + frow + 1],
                                      None, ALU.mult),
                         wait=[("sc", evac_sc), ("sc", prev_silu[half]),
                               ("ld", ld0)],
                         inc=[("dv", 1)])
                    for k, sh in ((2, 1), (1, 2), (0, 3)):
                        lo = max(a, sh)
                        P.on("vector", STT(S("convtmp")[:, lo:b2],
                                           S("convrow")[:, lo - sh:b2 - sh],
                                           cw[:, k * 32 + frow:k * 32 + frow + 1],
                                           S("convtmp")[:, lo:b2],
                                           ALU.mult, ALU.add),
                             wait=[("dv", P.v("dv"))], inc=[("dv", 1)])
                    prev_conv[half] = P.v("dv")
                    P.on("scalar", SCPY(
                        S("ssmrow")[:, rbuf * T + a:rbuf * T + b2],
                        S("convtmp")[:, a:b2], func=AF.Silu,
                        bias=S("cbcol")[:, frow:frow + 1]),
                         wait=[("dv", prev_conv[half]),
                               ("ld", spill_free[rbuf])],
                         inc=[("sc", 1)])
                    prev_silu[half] = P.v("sc")
                    last_silu = P.v("sc")
            P.on("sync", DMA(dr["scr_ssm"][:, frow * T:(frow + 1) * T],
                             S("ssmrow")[:, rbuf * T:(rbuf + 1) * T]),
                 wait=[("sc", last_silu)], inc=[("ld", 16)])
            spill_free[rbuf] = P.v("ld")
        wpan_free[buf] = P.v("pe")
    ssm_ld = P.v("ld")

    buf = 0
    wd_v = load_panel(buf, 2 * DI, 288)
    for tb in range(4):
        pp, par, pe_v = acc_tile(buf, 288, 0, tb, 32, wd_v)
        P.on("vector", CPY(S("dtv")[:, tb * 512:(tb + 1) * 512],
                           pp[0:32, 0:512]),
             wait=[("pe", pe_v)], inc=[("dv", 1)])
        mm_free[par]["dv"] = P.v("dv")
    for name, fcol in (("bc_bf", 32), ("cc_bf", 160)):
        for tb in range(4):
            pp, par, pe_v = acc_tile(buf, 288, fcol, tb, 128, wd_v)
            P.on("scalar", SCPY(S(name)[:, tb * 512:(tb + 1) * 512], pp[:, 0:512]),
                 wait=[("pe", pe_v)], inc=[("sc", 1)])
            mm_free[par]["sc"] = P.v("sc")
    wpan_free[buf] = P.v("pe")
    tail_dv, tail_sc = P.v("dv"), P.v("sc")

    dts = S("dtv")[:, :]
    P.on("vector", TS(dts, dts, S("dtbcol")[:, 0:1], None, ALU.add),
         wait=[("dv", tail_dv), ("ld", ld0)], inc=[("dv", 1)])
    P.on("scalar", SCPY(S("t32a")[:], dts, func=AF.Abs),
         wait=[("dv", P.v("dv"))], inc=[("sc", 1)])
    P.on("scalar", SCPY(S("t32b")[:], S("t32a")[:], scale=-1.0, func=AF.Exp),
         wait=[("sc", P.v("sc"))], inc=[("sc", 1)])
    P.on("scalar", SCPY(S("t32a")[:], S("t32b")[:], bias=1.0, func=AF.Ln),
         wait=[("sc", P.v("sc"))], inc=[("sc", 1)])
    sp_sc = P.v("sc")
    P.on("vector", TS(dts, dts, 0.0, None, ALU.max),
         wait=[("sc", sp_sc)], inc=[("dv", 1)])
    P.on("vector", TT(dts, dts, S("t32a")[:], ALU.add),
         wait=[("dv", P.v("dv"))], inc=[("dv", 1)])
    P.on("vector", TS(S("t32a")[:], dts, S("acol")[:, 0:1], None, ALU.mult),
         wait=[("dv", P.v("dv"))], inc=[("dv", 1)])
    for ch in range(NCH):
        sl = slice(ch * QC, (ch + 1) * QC)
        P.on("vector", lambda e, sl=sl: e.tensor_tensor_scan(
            S("cumv")[:, sl], S("t32a")[:, sl], S("t32a")[:, sl], 0.0,
            op0=ALU.add, op1=ALU.bypass),
             wait=[("dv", P.v("dv"))], inc=[("dv", 1)])
    cum_dv = P.v("dv")
    P.on("scalar", SCPY(S("excum")[:], S("cumv")[:, :], func=AF.Exp),
         wait=[("dv", cum_dv)], inc=[("sc", 1)])
    P.on("vector", CPY(S("cumh")[:], S("cumv")[:]),
         wait=[("dv", cum_dv)], inc=[("dv", 1)])
    P.on("vector", TT(S("t32b")[:], S("cumv")[:], S("cumh")[:], ALU.subtract),
         wait=[("dv", P.v("dv"))], inc=[("dv", 1)])
    P.on("vector", CPY(S("cumm")[:], S("t32b")[:]),
         wait=[("dv", P.v("dv"))], inc=[("dv", 1)])
    P.on("vector", TT(S("t32b")[:], S("t32b")[:], S("cumm")[:], ALU.subtract),
         wait=[("dv", P.v("dv"))], inc=[("dv", 1)])
    P.on("vector", CPY(S("cuml")[:], S("t32b")[:]),
         wait=[("dv", P.v("dv"))], inc=[("dv", 1)])
    prep_dv, prep_sc = P.v("dv"), P.v("sc")

    for fcb in range(8):
        buf = fcb % 2
        wd_v = load_panel(buf, fcb * 512, 512)
        for ft in range(4):
            frow = fcb * 4 + ft
            rbuf = frow % 2
            grow = S("sgrow")[:, rbuf * T:(rbuf + 1) * T]
            for tb in range(4):
                pp, par, pe_v = acc_tile(buf, 512, ft * 128, tb, 128, wd_v)
                P.on("scalar", SCPY(grow[:, tb * 512:(tb + 1) * 512],
                                    pp[:, 0:512], func=AF.Silu),
                     wait=[("pe", pe_v), ("ld", sgrow_free[rbuf])],
                     inc=[("sc", 1)])
                mm_free[par]["sc"] = P.v("sc")
            P.on("sync", DMA(dr["scr_gate"][:, frow * T:(frow + 1) * T], grow),
                 wait=[("sc", P.v("sc"))], inc=[("ld", 16)])
            sgrow_free[rbuf] = P.v("ld")
        wpan_free[buf] = P.v("pe")
    gate_ld = P.v("ld")

    tp = mktap("cum", [32, T])
    if tp is not None:
        P.on("sync", DMA(tp, S("cumv")[:]),
             wait=[("sc", prep_sc), ("dv", prep_dv)], inc=[("st", 16)])
    tp = mktap("dt", [32, T])
    if tp is not None:
        P.on("sync", DMA(tp, S("dtv")[:]),
             wait=[("sc", prep_sc), ("dv", prep_dv)], inc=[("st", 16)])
    tp = mktap("bc", [128, T], BF16)
    if tp is not None:
        P.on("sync", DMA(tp, S("bc_bf")[:]), wait=[("sc", tail_sc)],
             inc=[("st", 16)])
    tp = mktap("cc", [128, T], BF16)
    if tp is not None:
        P.on("sync", DMA(tp, S("cc_bf")[:]), wait=[("sc", tail_sc)],
             inc=[("st", 16)])
    tp = mktap("scr_ssm", [128, NH * T], BF16)
    if tp is not None:
        P.on("sync", DMA(tp, dr["scr_ssm"]), wait=[("ld", ssm_ld)],
             inc=[("st", 16)])
    tp = mktap("scr_gate", [128, NH * T], BF16)
    if tp is not None:
        P.on("sync", DMA(tp, dr["scr_gate"]), wait=[("ld", gate_ld)],
             inc=[("st", 16)])

    ywin, xtch = S("ywin"), S("xtch")
    sst, sbf = S("s_st"), S("sbf")
    P.fence_all()
    for db in range(4):
        for kh in range(2):
            k0, k1 = kh * 16, kh * 16 + 16
            P.on("gpsimd", DMA(
                dr["scr_wo"].rearrange("p (db kt f) -> p db kt f",
                                       db=4, kt=KI)[:, db, k0:k1, :],
                dr["w_out"].rearrange("(kt p) f -> p kt f", p=128)[
                    :, k0:k1, db * 512:(db + 1) * 512]),
                 inc=[("wd", 16)])
    wo_wd = P.v("wd")
    for db in range(4):
        P.on("gpsimd", DMA(
            dr["scr_pw"].rearrange("p (db kt f) -> p db kt f",
                                   db=4, kt=KD)[:, db],
            dr["p_w"].rearrange("(kt p) f -> p kt f", p=128)[
                :, :, db * 512:(db + 1) * 512]),
             inc=[("wd", 16)])
    pw_wd = P.v("wd")
    ew_wd = {}
    for e_i in range(NE):
        for fcb in range(4):
            P.on("gpsimd", DMA(
                dr["scr_ew"].rearrange("p (e b kt f) -> p e b kt f",
                                       e=NE, b=4, kt=KD)[:, e_i, fcb],
                dr["e_w"].rearrange("e (kt p) f -> e p kt f", p=128)[
                    e_i][:, :, fcb * 512:(fcb + 1) * 512]),
                 inc=[("wd", 16)])
            ew_wd[(e_i, fcb)] = P.v("wd")
    P.on("vector", MEMS(sst[:], 0.0), inc=[("dv", 1)])
    P.on("vector", MEMS(sbf[:], 0.0), inc=[("dv", 1)])
    sinit_dv = P.v("dv")

    pair_ld = {}
    chunk_done_dv = {-1: 0}
    yspill_ld = {}
    gwin_ld = {}
    gmul_dv = [0]

    def load_pair(cp):
        gate_free = chunk_done_dv.get(2 * cp - 1, 0)
        xw = [("ld", ssm_ld), ("ld", yspill_ld.get(cp - 1, 0)),
              ("dv", gate_free)]
        for pc in range(4):
            h0, h1 = pc * 8, pc * 8 + 8
            for ci in range(2):
                src = dr["scr_ssm"].rearrange("p (h t) -> p h t", h=NH)[
                    :, h0:h1, (2 * cp + ci) * QC:(2 * cp + ci + 1) * QC]
                dst = ywin[:].rearrange("p (c h i) -> p c h i", c=2, h=NH)[
                    :, ci, h0:h1]
                P.on("sync", DMA(dst, src), wait=xw, inc=[("ld", 16)])
        pair_ld[cp] = P.v("ld")
        gw = [("ld", gate_ld), ("dv", gmul_dv[0])]
        for pc in range(4):
            h0, h1 = pc * 8, pc * 8 + 8
            for ci in range(2):
                src = dr["scr_gate"].rearrange("p (h t) -> p h t", h=NH)[
                    :, h0:h1, (2 * cp + ci) * QC:(2 * cp + ci + 1) * QC]
                dst = S("gwin")[:].rearrange("p (c h i) -> p c h i",
                                             c=2, h=NH)[:, ci, h0:h1]
                P.on("sync", DMA(dst, src), wait=gw, inc=[("ld", 16)])
        gwin_ld[cp] = P.v("ld")

    def yw(cpar, h):
        return ywin[:, cpar * NH * QC + h * QC:cpar * NH * QC + (h + 1) * QC]

    load_pair(0)
    fr_bc = fb_bc
    fr_cb = [0]
    fr_y = [0]
    fr_sl = [0]
    fr_x = [0]
    sbf_rd = {h: 0 for h in range(NH)}
    xt_last_sc = [A_tr * 0]
    prev = dict(s1=0, t3=0, t4=0, t5=0, d1=0, d4=0, d7=0, s2=0)
    prevu = [dict(s1=0, t3=0, t4=0, t5=0, d2=0, d4=0),
             dict(s1=0, t3=0, t4=0, t5=0, d2=0, d4=0)]
    chunk_rd = dict(cjlt=0, wcolt=0, cdbc=0, cbtri=0, bct=0)

    for c in range(NCH):
        cpar = c % 2
        csl = slice(c * QC, (c + 1) * QC)
        if c % 2 == 0 and c > 0:
            load_pair(c // 2)
        ld_need = pair_ld[c // 2]
        P.on("tensor", MM(p_cb[:, 0:128], S("bc_bf")[:, csl],
                          S("cc_bf")[:, csl], True, True),
             wait=[("sc", tail_sc), ("dv", fr_cb[0])], inc=[("pe", 1)])
        P.on("vector", TT(S("cbtri")[:], p_cb[:, 0:128], S("tri")[:], ALU.mult),
             wait=[("pe", P.v("pe")), ("dv", chunk_rd["cbtri"]), ("ld", ld0)],
             inc=[("dv", 1)])
        fr_cb[0] = P.v("dv")
        cbtri_dv = P.v("dv")
        P.on("tensor", TRN(p_tr[:, 0:128], S("bc_bf")[:, csl], S("iden_bf")[:]),
             wait=[("sc", xt_last_sc[0])], inc=[("pe", 1)])
        P.on("scalar", SCPY(S("bct_bf")[:], p_tr[:, 0:128]),
             wait=[("pe", P.v("pe")), ("dv", chunk_rd["bct"])], inc=[("sc", 1)])
        bct_sc = P.v("sc")
        P.on("scalar", SCPY(S("lnst")[:], S("dtv")[:, csl], func=AF.Ln),
             wait=[("sc", prep_sc), ("dv", prep_dv)], inc=[("sc", 1)])
        P.on("vector", TT(S("cjlst")[:], S("cumv")[:, csl], S("lnst")[:],
                          ALU.subtract),
             wait=[("sc", P.v("sc"))], inc=[("dv", 1)])
        cjl_dv = P.v("dv")
        P.on("vector", TS(S("wst")[:], S("cumv")[:, csl],
                          S("cumv")[:, c * QC + QC - 1:c * QC + QC],
                          None, ALU.subtract),
             inc=[("dv", 1)])
        P.on("scalar", SCPY(S("wst2")[:], S("wst")[:], scale=-1.0, func=AF.Exp),
             wait=[("dv", P.v("dv"))], inc=[("sc", 1)])
        P.on("vector", TT(S("wst")[:], S("wst2")[:], S("dtv")[:, csl],
                          ALU.mult),
             wait=[("sc", P.v("sc"))], inc=[("dv", 1)])
        w_dv = P.v("dv")
        P.on("tensor", TRN(p_x[:, 0:32], S("cjlst")[:], S("iden_f")[0:32, 0:32]),
             wait=[("dv", cjl_dv), ("dv", fr_x[0])], inc=[("pe", 1)])
        P.on("tensor", TRN(p_x[:, 32:64], S("wst")[:], S("iden_f")[0:32, 0:32]),
             wait=[("dv", w_dv)], inc=[("pe", 1)])
        P.on("tensor", TRN(p_x[0:1, 64:96],
                           S("cumv")[:, c * QC + QC - 1:c * QC + QC],
                           S("iden_f")[0:32, 0:32]),
             wait=[("dv", cum_dv)], inc=[("pe", 1)])
        pev = P.v("pe")
        P.on("vector", CPY(S("cjlt")[:], p_x[:, 0:32]),
             wait=[("pe", pev), ("dv", chunk_rd["cjlt"])], inc=[("dv", 1)])
        P.on("vector", CPY(S("wcolt")[:], p_x[:, 32:64]),
             wait=[("dv", chunk_rd["wcolt"])], inc=[("dv", 1)])
        P.on("scalar", SCPY(S("cdrow")[:], p_x[0:1, 64:96]),
             wait=[("pe", pev)], inc=[("sc", 1)])
        P.on("scalar", SCPY(S("cdrow2")[:], S("cdrow")[:], func=AF.Exp),
             wait=[("sc", P.v("sc"))], inc=[("sc", 1)])
        fr_x[0] = P.v("dv")
        cjl_ev = P.v("dv")
        P.on("tensor", MM(p_bc[:, 0:32], S("ones_f")[:], S("cdrow2")[:],
                          True, True),
             wait=[("sc", P.v("sc")), ("dv", fr_bc[0])], inc=[("pe", 1)])
        P.on("vector", CPY(S("cdbc")[:], p_bc[:, 0:32]),
             wait=[("pe", P.v("pe")), ("dv", chunk_rd["cdbc"])], inc=[("dv", 1)])
        fr_bc[0] = P.v("dv")
        shared_dv = P.v("dv")

        xt_sc = {}
        for bi in range(4):
            for j in range(8):
                h = bi * 8 + j
                P.on("tensor", TRN(p_tr[:, j * 128:(j + 1) * 128],
                                   yw(cpar, h), S("iden_bf")[:]),
                     wait=([("ld", ld_need), ("sc", bct_sc),
                            ("pe", prev["t5"])] if (bi == 0 and j == 0)
                           else ([("sc", xt_sc[bi - 1])] if j == 0 else [])),
                     inc=[("pe", 1)])
            pev = P.v("pe")
            P.on("scalar", SCPY(xtch[:, bi * 1024:(bi + 1) * 1024],
                                p_tr[:, 0:1024]),
                 wait=[("pe", pev)], inc=[("sc", 1)])
            xt_sc[bi] = P.v("sc")
        xt_last_sc[0] = xt_sc[3]

        stash = {}

        def stage1(h):
            up = h % 2
            usl = slice(up * 128, (up + 1) * 128)
            e_t = S("e_t")[:, usl]
            et2 = S("et2")[:, usl]
            dt_t = S("dt_t")[:, usl]
            mt_bf = S("mt_bf")[:, usl]
            ce_bf = S("ce_bf")[:, usl]
            bw_bf = S("bw_bf")[:, usl]
            prev2 = prevu[up]
            sel = S("sel32b")[:, h * 128:(h + 1) * 128]
            P.on("tensor", MM(p_bc[:, 0:128], sel, S("cumh")[:, csl],
                              True, False),
                 wait=[("dv", fr_bc[0]), ("dv", prep_dv)], inc=[("pe", 1)])
            P.on("tensor", MM(p_bc[:, 0:128], sel, S("cumm")[:, csl],
                              False, False), inc=[("pe", 1)])
            P.on("tensor", MM(p_bc[:, 0:128], sel, S("cuml")[:, csl],
                              False, True), inc=[("pe", 1)])
            t1 = P.v("pe")
            P.on("tensor", MM(p_cb[:, 0:128], sel, S("excum")[:, csl],
                              True, True),
                 wait=[("dv", fr_cb[0]), ("sc", prep_sc)], inc=[("pe", 1)])
            t2 = P.v("pe")
            P.on("vector", TS(e_t, p_bc[:, 0:128],
                              S("cjlt")[:, h:h + 1], CLAMP,
                              ALU.subtract, ALU.min),
                 wait=[("pe", t1), ("dv", cjl_ev), ("sc", prev2["s1"])],
                 inc=[("dv", 1)])
            fr_bc[0] = P.v("dv")
            d1 = P.v("dv")
            P.on("scalar", SCPY(et2, e_t, func=AF.Exp),
                 wait=[("dv", d1), ("dv", prev2["d2"])], inc=[("sc", 1)])
            s1 = P.v("sc")
            P.on("vector", TT(ce_bf, p_cb[:, 0:128], S("cc_bf")[:, csl],
                              ALU.mult),
                 wait=[("pe", t2), ("pe", prev2["t4"])], inc=[("dv", 1)])
            fr_cb[0] = P.v("dv")
            d3 = P.v("dv")
            P.on("vector", TS(bw_bf, S("bct_bf")[:],
                              S("wcolt")[:, h:h + 1], None, ALU.mult),
                 wait=[("sc", bct_sc), ("dv", shared_dv), ("pe", prev2["t5"])],
                 inc=[("dv", 1)])
            d6 = P.v("dv")
            P.on("vector", TT(mt_bf, et2, S("cbtri")[:], ALU.mult),
                 wait=[("sc", s1), ("dv", cbtri_dv), ("pe", prev2["t3"])],
                 inc=[("dv", 1)])
            d2 = P.v("dv")
            P.on("scalar", SCPY(dt_t, yw(cpar, h),
                                scale=S("dcol")[:, h:h + 1]),
                 wait=[("ld", ld_need), ("ld", ld0), ("dv", prev2["d4"])],
                 inc=[("sc", 1)])
            s2 = P.v("sc")
            stash[h] = dict(mt=mt_bf, ce=ce_bf, bw=bw_bf, dt_t=dt_t,
                            d2=d2, d3=d3, d6=d6, s1=s1, s2=s2, up=up)

        def stage2(h):
            st2 = stash.pop(h)
            up = st2["up"]
            xt = xtch[:, h * 128:(h + 1) * 128]
            P.on("tensor", MM(p_y[:, 0:128], xt, st2["mt"], True, False),
                 wait=[("dv", st2["d2"]), ("sc", xt_sc[h // 8]),
                       ("dv", fr_y[0])],
                 inc=[("pe", 1)])
            t3 = P.v("pe")
            P.on("tensor", MM(p_y[:, 0:128], sbf[:, h * 128:(h + 1) * 128],
                              st2["ce"], False, True),
                 wait=[("dv", st2["d3"]), ("sc", sbf_rd[h]),
                       ("dv", sinit_dv)],
                 inc=[("pe", 1)])
            t4 = P.v("pe")
            P.on("tensor", MM(p_sl[:, 0:128], st2["bw"], xt, True, True),
                 wait=[("dv", st2["d6"]), ("dv", fr_sl[0])], inc=[("pe", 1)])
            t5 = P.v("pe")
            P.on("vector", TT(yw(cpar, h), st2["dt_t"], p_y[:, 0:128],
                              ALU.add),
                 wait=[("pe", t4), ("sc", st2["s2"])], inc=[("dv", 1)])
            fr_y[0] = P.v("dv")
            d4 = P.v("dv")
            P.on("vector", STT(sst[:, h * 128:(h + 1) * 128],
                               sst[:, h * 128:(h + 1) * 128],
                               S("cdbc")[:, h:h + 1], p_sl[:, 0:128],
                               ALU.mult, ALU.add),
                 wait=[("pe", t5), ("dv", shared_dv), ("sc", sbf_rd[h])],
                 inc=[("dv", 1)])
            fr_sl[0] = P.v("dv")
            d7 = P.v("dv")
            P.on("scalar", SCPY(sbf[:, h * 128:(h + 1) * 128],
                                sst[:, h * 128:(h + 1) * 128]),
                 wait=[("dv", d7)], inc=[("sc", 1)])
            sbf_rd[h] = P.v("sc")
            prev["t5"] = t5
            prevu[up] = dict(s1=st2["s1"], t3=t3, t4=t4, t5=t5,
                             d2=st2["d2"], d4=d4)

        for h in range(NH):
            stage1(h)
            if h > 0:
                stage2(h - 1)
        stage2(NH - 1)
        chunk_done_dv[c] = P.v("dv")
        chunk_rd = dict(cjlt=P.v("dv"), wcolt=P.v("dv"), cdbc=P.v("dv"),
                        cbtri=P.v("dv"), bct=P.v("dv"))
        if c % 2 == 1:
            P.on("vector", TT(ywin[:], ywin[:], S("gwin")[:], ALU.mult),
                 wait=[("dv", P.v("dv")), ("ld", gwin_ld[c // 2])],
                 inc=[("dv", 1)])
            gmul_dv[0] = P.v("dv")
            for ci in range(2):
                P.on("sync", DMA(
                    dr["scr_y"][:, (c - 1 + ci) * NH * QC:(c + ci) * NH * QC],
                    ywin[:, ci * NH * QC:(ci + 1) * NH * QC]),
                     wait=[("dv", P.v("dv"))], inc=[("ld", 16)])
            yspill_ld[c // 2] = P.v("ld")
    y_ld = P.v("ld")
    C_pe = P.v("pe")

    tp = mktap("scr_y", [128, NH * T], BF16)
    if tp is not None:
        P.on("sync", DMA(tp, dr["scr_y"]), wait=[("ld", y_ld)], inc=[("st", 16)])

    P.fence_all()
    wpanD = S("wpanD")
    PAND = KI * 512
    wD_free = [C_pe, C_pe]
    ylhs = S("ylhs")
    ylhs_free = [C_pe]
    mmD = [dict(sc=0), dict(sc=0)]
    x0b_free = {0: 0, 1: 0}
    ptrD_free = [P.v("sc")]
    w1_free = {0: (0, 0, 0), 1: (0, 0, 0)}
    dstage_free = {q: 0 for q in range(4)}
    fwt_sc = 0

    def load_ylhs(tblk):
        for sidx in range(4):
            tt = tblk * 4 + sidx
            P.on("sync", DMA(
                ylhs[:, sidx * KI * 128:(sidx + 1) * KI * 128],
                dr["scr_y"][:, tt * NH * QC:(tt + 1) * NH * QC]),
                 wait=[("ld", y_ld), ("pe", ylhs_free[0])],
                 inc=[("wd", 16)])
        return P.v("wd")

    def yl(slot, ktf):
        return ylhs[:, slot * KI * 128 + ktf * 128:slot * KI * 128 + (ktf + 1) * 128]

    def load_wD(buf, d0):
        db = d0 // 512
        P.on("sync", DMA(
            wpanD[:, buf * PAND:(buf + 1) * PAND],
            dr["scr_wo"][:, db * KI * 512:(db + 1) * KI * 512]),
             wait=[("pe", wD_free[buf]), ("wd", wo_wd)], inc=[("wd", 16)])
        return P.v("wd")

    lnt2_free = [0, 0]
    w1b_free = {s: 0 for s in range(6)}
    pend_p2 = []
    pend_fw = []
    fwt_scL = [0]

    def flush_fw():
        for tt, fwdv, fpar in pend_fw:
            P.on("tensor", TRN(p_x[0:8, 128:256],
                               S("fw8")[:, fpar * 8:fpar * 8 + 8],
                               S("iden_f")[:]),
                 wait=[("dv", fwdv), ("sc", fwt_scL[0])], inc=[("pe", 1)])
            P.on("scalar", SCPY(S("fwt")[0:8, tt * 128:(tt + 1) * 128],
                                p_x[0:8, 128:256]),
                 wait=[("pe", P.v("pe"))], inc=[("sc", 1)])
            fwt_scL[0] = P.v("sc")
        del pend_fw[:]

    def emit_pass2(info):
        tt, slot, wbsc, w1b = info["tt"], info["slot"], info["wbsc"], info["w1b"]
        par = tt % 2
        flush_fw()
        w1t = S("w1tst")[:, par * DM:(par + 1) * DM]
        w1t_r = w1t.rearrange("p (kt i) -> p kt i", kt=KD)
        ftr, fld, fpe = w1_free[par]
        for bi in range(2):
            for j in range(8):
                kt = bi * 8 + j
                P.on("tensor", TRN(p_tr[:, j * 128:(j + 1) * 128],
                                   w1b[:, kt * 128:(kt + 1) * 128],
                                   S("iden_bf")[:]),
                     wait=([("sc", wbsc), ("sc", ptrD_free[0]),
                            ("ld", fld), ("pe", fpe)] if j == 0 else []),
                     inc=[("pe", 1)])
            pev = P.v("pe")
            P.on("scalar", lambda e, bi=bi, w1t_r=w1t_r: e.copy(
                w1t_r[:, bi * 8:(bi + 1) * 8, :],
                p_tr[:, 0:1024].rearrange("p (j t) -> p j t", j=8)),
                 wait=[("pe", pev)], inc=[("sc", 1)])
            ptrD_free[0] = P.v("sc")
        w1b_free[slot] = P.v("pe")
        trv = P.v("sc")
        P.on("sync", DMA(
            dr["scr_w1t"].rearrange("p (kt t) -> p kt t", kt=KD)[
                :, :, tt * 128:(tt + 1) * 128],
            w1t.rearrange("p (kt i) -> p kt i", kt=KD)),
             wait=[("sc", trv)], inc=[("ld", 16)])
        for kt in range(KD):
            P.on("tensor", MM(p_x[0:128, 0:8],
                              w1t[:, kt * 128:(kt + 1) * 128],
                              S("wr_bf")[:, kt * 8:(kt + 1) * 8],
                              kt == 0, kt == KD - 1),
                 wait=([("sc", trv), ("wd", wr_wd), ("dv", fr_x[0])]
                       if kt == 0 else []),
                 inc=[("pe", 1)] if kt == KD - 1 else ())
        pev = P.v("pe")
        w1_free[par] = (trv, P.v("ld"), pev)
        P.on("vector", TT(S("lg8")[:], p_x[:, 0:8], S("rb_b")[:], ALU.add),
             wait=[("pe", pev)], inc=[("dv", 1)])
        fr_x[0] = P.v("dv")
        P.on("vector", lambda e: e.max(out=S("m8")[:], in_=S("lg8")[:]),
             wait=[("dv", P.v("dv"))], inc=[("dv", 1)])
        P.on("vector", MEMS(S("mv8")[:], NEG), inc=[("dv", 1)])
        P.on("vector", CPY(S("mv8")[:, 0:2], S("m8")[:, 0:2]),
             wait=[("dv", P.v("dv"))], inc=[("dv", 1)])
        P.on("vector", lambda e: e.match_replace(
            out=S("mr8")[:], in_to_replace=S("mv8")[:],
            in_values=S("lg8")[:], imm_value=NEG),
             wait=[("dv", P.v("dv"))], inc=[("dv", 1)])
        P.on("vector", TT(S("rc")[:, 6:7], S("m8")[:, 0:1], S("m8")[:, 1:2],
                          ALU.subtract),
             wait=[("dv", P.v("dv"))], inc=[("dv", 1)])
        dvv = P.v("dv")
        P.on("scalar", SCPY(S("rc")[:, 7:8], S("rc")[:, 6:7],
                            func=AF.Sigmoid),
             wait=[("dv", dvv)], inc=[("sc", 1)])
        P.on("scalar", SCPY(S("rc")[:, 6:7], S("rc")[:, 6:7], scale=-1.0,
                            func=AF.Sigmoid),
             wait=[("sc", P.v("sc"))], inc=[("sc", 1)])
        rsc = P.v("sc")
        P.on("vector", TT(S("rc")[:, 7:8], S("rc")[:, 7:8], S("rc")[:, 6:7],
                          ALU.subtract),
             wait=[("sc", rsc)], inc=[("dv", 1)])
        fpar = tt % 2
        fw8 = S("fw8")[:, fpar * 8:fpar * 8 + 8]
        P.on("vector", TS(fw8, S("lg8")[:], S("m8")[:, 0:1], None,
                          ALU.is_equal),
             inc=[("dv", 1)])
        P.on("vector", TS(fw8, fw8, S("rc")[:, 7:8],
                          S("rc")[:, 6:7], ALU.mult, ALU.add),
             wait=[("dv", P.v("dv"))], inc=[("dv", 1)])
        P.on("vector", TT(S("mv8")[:], S("mr8")[:], S("lg8")[:],
                          ALU.not_equal),
             inc=[("dv", 1)])
        P.on("vector", TT(fw8, fw8, S("mv8")[:], ALU.mult),
             wait=[("dv", P.v("dv"))], inc=[("dv", 1)])
        pend_fw.append((tt, P.v("dv"), fpar))

    pending = load_ylhs(0)
    pend_wd = {}
    for tblk in range(4):
        blk_wd = pending
        for db in range(4):
            buf = db % 2
            wd_v = pend_wd.pop((tblk, db), None)
            if wd_v is None:
                wd_v = load_wD(buf, db * 512)
            for q in range(4):
                tt = tblk * 4 + q
                par = (q + db) % 2
                pp = p_mm[par]
                for ktf in range(KI):
                    w = ([("wd", wd_v), ("wd", blk_wd),
                          ("sc", mmD[par]["sc"]),
                          ("sc", dstage_free[q])] if ktf == 0 else [])
                    P.on("tensor", MM(pp[:, 0:512], yl(q, ktf),
                                      wpanD[:, buf * PAND + ktf * 512:
                                            buf * PAND + (ktf + 1) * 512],
                                      ktf == 0, ktf == KI - 1),
                         wait=w, inc=[("pe", 1)] if ktf == KI - 1 else ())
                pe_v = P.v("pe")
                P.on("scalar", SCPY(
                    S("dstage")[:, q * DM + db * 512:
                                q * DM + (db + 1) * 512],
                    pp[:, 0:512]),
                     wait=[("pe", pe_v)], inc=[("sc", 1)])
                mmD[par]["sc"] = P.v("sc")
            wD_free[buf] = P.v("pe")
        ylhs_free[0] = P.v("pe")
        evD_sc = P.v("sc")
        if tblk < 3:
            pend_wd[(tblk + 1, 0)] = load_wD(0, 0)
            pending = load_ylhs(tblk + 1)
            pend_wd[(tblk + 1, 1)] = load_wD(1, 512)
        for info in pend_p2:
            emit_pass2(info)
        pend_p2 = []
        for q in range(4):
            tt = tblk * 4 + q
            slot = tt % 6
            par = tt % 2
            stg = S("dstage")[:, q * DM:(q + 1) * DM]
            x0 = S("x0t2")[:, par * DM:(par + 1) * DM]
            P.on("sync", DMA(x0, dr["x_in"][tt * 128:(tt + 1) * 128, :]),
                 wait=[("dv", x0b_free[par])], inc=[("ld", 16)])
            ldv = P.v("ld")
            P.on("scalar", SCPY(S("lnt2")[:], x0,
                                bias=S("astats")[:, 2 * tt:2 * tt + 1],
                                scale=S("astats")[:, 2 * tt + 1:2 * tt + 2]),
                 wait=[("ld", ldv), ("dv", lnt2_free[0]),
                       ("ld", lnt2_free[1])],
                 inc=[("sc", 1)])
            P.on("vector", TT(stg, stg, S("lnt2")[:], ALU.add),
                 wait=[("sc", P.v("sc")), ("sc", evD_sc)], inc=[("dv", 1)])
            attn_dv = P.v("dv")
            dvv, cc = ln_stats(stg, par, LN_EPS, pre_wait=[("dv", attn_dv)])
            P.on("scalar", SCPY(S("lnt2")[:], stg, bias=S("rc")[:, cc + 1:cc + 2],
                                scale=S("rc")[:, cc:cc + 1]),
                 wait=[("dv", dvv)], inc=[("sc", 1)])
            stg_sc = P.v("sc")
            dstage_free[q] = stg_sc
            P.on("vector", TT(S("lnt2")[:], S("lnt2")[:], x0, ALU.add),
                 wait=[("sc", stg_sc)], inc=[("dv", 1)])
            x1dv = P.v("dv")
            x0b_free[par] = x1dv
            P.on("sync", DMA(dr["scr_x1"][tt * 128:(tt + 1) * 128, :],
                             S("lnt2")[:]),
                 wait=[("dv", x1dv)], inc=[("ld", 16)])
            x1_ld = P.v("ld")
            dvv, cc = ln_stats(S("lnt2"), par, LN_EPS,
                               pre_wait=[("dv", x1dv)])
            P.on("vector", CPY(S("x1stats")[:, 2 * tt:2 * tt + 2],
                               S("rc")[:, cc:cc + 2]),
                 wait=[("dv", dvv)], inc=[("dv", 1)])
            w1b = S("w1bf")[:, slot * DM:(slot + 1) * DM]
            P.on("scalar", SCPY(w1b, S("lnt2")[:],
                                bias=S("rc")[:, cc + 1:cc + 2],
                                scale=S("rc")[:, cc:cc + 1]),
                 wait=[("dv", dvv), ("ld", x1_ld),
                       ("pe", w1b_free[slot])],
                 inc=[("sc", 1)])
            wbsc = P.v("sc")
            lnt2_free = [P.v("dv"), x1_ld]
            pend_p2.append(dict(tt=tt, slot=slot, wbsc=wbsc, w1b=w1b))
    for info in pend_p2:
        emit_pass2(info)
    flush_fw()
    w1t_ld = P.v("ld")
    D_sc = P.v("sc")
    D_pe = P.v("pe")

    tp = mktap("x1", [T, DM])
    if tp is not None:
        P.on("sync", DMA(tp, dr["scr_x1"]), wait=[("ld", w1t_ld)],
             inc=[("st", 16)])
    tp = mktap("fwt", [8, T])
    if tp is not None:
        P.on("sync", DMA(tp, S("fwt")[:]), wait=[("sc", D_sc)], inc=[("st", 16)])
    tp = mktap("scr_w1t", [128, KD * T], BF16)
    if tp is not None:
        P.on("sync", DMA(tp, dr["scr_w1t"]), wait=[("ld", w1t_ld)],
             inc=[("st", 16)])

    web, w1blk, comb = S("web"), S("w1blk"), S("comb")
    PANE = KD * 512
    wE_free = [D_pe, D_pe]
    mmE = [dict(sc=0), dict(sc=0)]
    silu_free = [0, 0]
    comb_free = [0]
    w1blk_free = [D_pe]
    fwb_free = [0]

    wpanF = S("wpanF")
    PANF = KD * 512
    wF_free = [0, 0]
    mmF = [dict(sc=0, cb=0), dict(sc=0, cb=0)]
    cb_sc_map = {}
    fstage_free = {q: 0 for q in range(4)}
    fyt_free = [0]
    x1b_free = {0: 0, 1: 0}
    cbst_free = [0]

    def load_we(e_i, fcb, buf, fence_pe, fence_st):
        pan = (e_i * 4 + fcb) * KD * 512
        P.on("sync", DMA(
            web[:, buf * PANE:(buf + 1) * PANE],
            dr["scr_ew"][:, pan:pan + KD * 512]),
             wait=[("pe", wE_free[buf]), ("pe", fence_pe),
                   ("st", fence_st), ("wd", ew_wd[(e_i, fcb)])],
             inc=[("wd", 16)])
        return P.v("wd")

    def load_wF(buf, d0, fence):
        db = d0 // 512
        P.on("sync", DMA(
            wpanF[:, buf * PANF:(buf + 1) * PANF],
            dr["scr_pw"][:, db * KD * 512:(db + 1) * KD * 512]),
             wait=[("pe", wF_free[buf]), ("pe", fence["pe"]),
                   ("dv", fence["dv"]), ("sc", fence["sc"]),
                   ("wd", pw_wd)],
             inc=[("wd", 16)])
        return P.v("wd")

    P.fence_all()
    P.on("sync", DMA(S("pbb")[0:1, :], dr["p_b"].rearrange("(o d) -> o d", o=1)),
         inc=[("ld", 16)])
    P.on("sync", DMA(S("nwb")[0:1, :], dr["n_w"].rearrange("(o d) -> o d", o=1)),
         inc=[("ld", 16)])
    ldv = P.v("ld")
    for nm in ("pbb", "nwb"):
        for b in range(4):
            P.on("tensor", MM(p_bc[:, 0:512], S("ones_f")[:],
                              S(nm)[0:1, b * 512:(b + 1) * 512], True, True),
                 wait=[("ld", ldv), ("dv", fr_bc[0])], inc=[("pe", 1)])
            P.on("vector", CPY(S(nm)[:, b * 512:(b + 1) * 512], p_bc[:, 0:512]),
                 wait=[("pe", P.v("pe"))], inc=[("dv", 1)])
            fr_bc[0] = P.v("dv")
    bb_dv = P.v("dv")

    for H in range(2):
        P.fence_all()
        fE = dict(pe=P.v("pe"), dv=P.v("dv"), sc=P.v("sc"), st=P.v("st"))
        P.on("sync", DMA(w1blk[:].rearrange("p (kt t) -> p kt t", kt=KD),
                         dr["scr_w1t"].rearrange("p (kt t) -> p kt t", kt=KD)[
                             :, :, H * TH:(H + 1) * TH]),
             wait=[("ld", w1t_ld), ("pe", w1blk_free[0]), ("pe", fE["pe"]),
                   ("dv", fE["dv"]), ("sc", fE["sc"]), ("st", fE["st"])],
             inc=[("ld", 16)])
        w1_ld2 = P.v("ld")
        P.on("vector", MEMS(comb[:], 0.0),
             wait=[("sc", comb_free[0])], inc=[("dv", 1)])
        cmdv = P.v("dv")
        wE_free = [fE["pe"], fE["pe"]]
        wd_first = load_we(0, 0, 0, fE["pe"], fE["st"])
        load_we(0, 1, 1, fE["pe"], fE["st"])
        wE_wd = {0: wd_first, 1: P.v("wd")}
        for e_i in range(NE):
            for bb in range(2):
                P.on("tensor", MM(p_bc[:, 0:512],
                                  S("sel8")[:, e_i * 128:(e_i + 1) * 128],
                                  S("fwt")[0:8, H * TH + bb * 512:
                                           H * TH + (bb + 1) * 512], True, True),
                     wait=[("sc", D_sc), ("dv", fr_bc[0]), ("dv", fwb_free[0])],
                     inc=[("pe", 1)])
                P.on("vector", CPY(S("fwb")[:, bb * 512:(bb + 1) * 512],
                                   p_bc[:, 0:512]),
                     wait=[("pe", P.v("pe")), ("sc", fE["sc"]),
                           ("st", fE["st"])],
                     inc=[("dv", 1)])
                fr_bc[0] = P.v("dv")
            fwb_dv = P.v("dv")
            for fcb in range(4):
                buf = fcb % 2
                wd_v = wE_wd[buf]
                for fi in range(4):
                    f = fcb * 4 + fi
                    for b in range(2):
                        par = (f * 2 + b) % 2
                        pp = p_mm[par]
                        for kt in range(KD):
                            w = ([("wd", wd_v), ("ld", w1_ld2),
                                  ("sc", mmE[par]["sc"])] if kt == 0 else [])
                            P.on("tensor", MM(
                                pp[:, 0:512],
                                web[:, buf * PANE + kt * 512 + fi * 128:
                                    buf * PANE + kt * 512 + fi * 128 + 128],
                                w1blk[:, kt * TH + b * 512:
                                      kt * TH + (b + 1) * 512],
                                kt == 0, kt == KD - 1),
                                 wait=w, inc=[("pe", 1)] if kt == KD - 1 else ())
                        pev = P.v("pe")
                        sl = S("silu1") if par == 0 else S("silu2")
                        P.on("scalar", SCPY(sl[:], pp[:, 0:512], func=AF.Silu,
                                            bias=S("ebcol")[:, e_i * 16 + f:
                                                            e_i * 16 + f + 1]),
                             wait=[("pe", pev), ("dv", silu_free[par]),
                                   ("ld", ld0), ("st", fE["st"])],
                             inc=[("sc", 1)])
                        mmE[par]["sc"] = P.v("sc")
                        ssc = P.v("sc")
                        P.on("vector", TT(S("prodt")[:], sl[:],
                                          S("fwb")[:, b * 512:(b + 1) * 512],
                                          ALU.mult),
                             wait=[("sc", ssc), ("dv", fwb_dv), ("dv", cmdv)],
                             inc=[("dv", 1)])
                        silu_free[par] = P.v("dv")
                        cslot = comb[:, f * TH + b * 512:f * TH + (b + 1) * 512]
                        P.on("vector", TT(cslot, cslot, S("prodt")[:], ALU.add),
                             wait=[("dv", P.v("dv"))], inc=[("dv", 1)])
                wE_free[buf] = P.v("pe")
                if fcb < 2:
                    wE_wd[buf] = load_we(e_i, fcb + 2, buf, fE["pe"], fE["st"])
                elif e_i < NE - 1:
                    wE_wd[buf] = load_we(e_i + 1, fcb - 2, buf, fE["pe"],
                                         fE["st"])
            fwb_free[0] = P.v("dv")
            if H == 0 and e_i == 0:
                tpx = mktap("sel8chk", [NE, NE * 128])
                if tpx is not None:
                    P.on("sync", DMA(tpx, S("sel8")[:]), wait=[("dv", P.v("dv"))],
                         inc=[("st", 16)])
                tpx = mktap("combE0", [128, KD * TH])
                if tpx is not None:
                    P.on("sync", DMA(tpx, comb[:]), wait=[("dv", P.v("dv"))],
                         inc=[("st", 16)])
                tpx = mktap("fwb0", [128, TH])
                if tpx is not None:
                    P.on("sync", DMA(tpx, S("fwb")[:]), wait=[("dv", P.v("dv"))],
                         inc=[("st", 16)])
        comb_dv = P.v("dv")
        if H == 0:
            tpx = mktap("comb0", [128, KD * TH])
            if tpx is not None:
                P.on("sync", DMA(tpx, comb[:]), wait=[("dv", comb_dv)],
                     inc=[("st", 16)])
        E_end_pe = P.v("pe")
        w1blk_free[0] = E_end_pe

        P.fence_all()
        fF = dict(pe=P.v("pe"), dv=P.v("dv"), sc=P.v("sc"), st=P.v("st"))
        wF_free = [fF["pe"], fF["pe"]]
        for tb4 in range(2):
            for db in range(4):
                buf = db % 2
                wd_v = load_wF(buf, db * 512, fF)
                for q in range(4):
                    ti = tb4 * 4 + q
                    tt = H * 8 + ti
                    slot = tt % 6
                    par = (q + db) % 2
                    pp = p_mm[par]
                    cb0 = q * KD * 128
                    if db == 0:
                        for kt in range(KD):
                            P.on("scalar", SCPY(
                                S("cbstage")[:, cb0 + kt * 128:
                                             cb0 + (kt + 1) * 128],
                                comb[:, kt * TH + ti * 128:
                                     kt * TH + ti * 128 + 128]),
                                 wait=([("dv", comb_dv),
                                        ("pe", cbst_free[0]),
                                        ("st", fF["st"])] if kt == 0 else []),
                                 inc=[("sc", 1)])
                        mmF[par]["cb"] = P.v("sc")
                        cbsc_by_ti = P.v("sc")
                        cb_sc_map[ti] = cbsc_by_ti
                    for kt in range(KD):
                        w = ([("wd", wd_v), ("sc", cb_sc_map[ti]),
                              ("sc", mmF[par]["sc"]),
                              ("st", fstage_free[q])] if kt == 0 else [])
                        P.on("tensor", MM(
                            pp[:, 0:512],
                            S("cbstage")[:, cb0 + kt * 128:cb0 + (kt + 1) * 128],
                            wpanF[:, buf * PANF + kt * 512:
                                  buf * PANF + (kt + 1) * 512],
                            kt == 0, kt == KD - 1),
                             wait=w, inc=[("pe", 1)] if kt == KD - 1 else ())
                    pev = P.v("pe")
                    P.on("scalar", SCPY(
                        S("fstage")[:, q * DM + db * 512:
                                    q * DM + (db + 1) * 512],
                        pp[:, 0:512]),
                         wait=[("pe", pev), ("dv", fF["dv"])],
                         inc=[("sc", 1)])
                    mmF[par]["sc"] = P.v("sc")
                wF_free[buf] = P.v("pe")
            cbst_free[0] = P.v("pe")
            evF_sc = P.v("sc")
            for q in range(4):
                ti = tb4 * 4 + q
                tt = H * 8 + ti
                slot = tt % 6
                par = tt % 2
                stg = S("fstage")[:, q * DM:(q + 1) * DM]
                x1 = S("x1t")[:, par * DM:(par + 1) * DM]
                P.on("sync", DMA(x1, dr["scr_x1"][tt * 128:(tt + 1) * 128, :]),
                     wait=[("ld", w1t_ld), ("dv", x1b_free[par]),
                           ("pe", fF["pe"]), ("dv", fF["dv"]),
                           ("sc", fF["sc"])],
                     inc=[("ld", 16)])
                ldv = P.v("ld")
                P.on("scalar", SCPY(S("fyt")[:], x1,
                                    bias=S("x1stats")[:, 2 * tt + 1:2 * tt + 2],
                                    scale=S("x1stats")[:, 2 * tt:2 * tt + 1]),
                     wait=[("ld", ldv), ("st", fF["st"]),
                           ("dv", fyt_free[0])],
                     inc=[("sc", 1)])
                w1sc = P.v("sc")
                P.on("vector", TT(stg, stg, S("fyt")[:], ALU.add),
                     wait=[("sc", w1sc), ("sc", evF_sc), ("dv", bb_dv)],
                     inc=[("dv", 1)])
                P.on("vector", TT(stg, stg, S("pbb")[:], ALU.add),
                     wait=[("dv", P.v("dv"))], inc=[("dv", 1)])
                ydv = P.v("dv")
                dvv, cc = ln_stats(stg, par, RMS_EPS, mean_sq=True,
                                   pre_wait=[("dv", ydv)])
                P.on("scalar", SCPY(S("fyt")[:], stg,
                                    scale=S("rc")[:, cc:cc + 1]),
                     wait=[("dv", dvv)], inc=[("sc", 1)])
                P.on("vector", TT(S("fyt")[:], S("fyt")[:], S("nwb")[:],
                                  ALU.mult),
                     wait=[("sc", P.v("sc"))], inc=[("dv", 1)])
                frdv = P.v("dv")
                dvv, cc = ln_stats(S("fyt"), par, LN_EPS,
                                   pre_wait=[("dv", frdv)])
                P.on("scalar", SCPY(stg, S("fyt")[:],
                                    bias=S("rc")[:, cc + 1:cc + 2],
                                    scale=S("rc")[:, cc:cc + 1]),
                     wait=[("dv", dvv)], inc=[("sc", 1)])
                P.on("vector", TT(stg, stg, x1, ALU.add),
                     wait=[("sc", P.v("sc"))], inc=[("dv", 1)])
                odv = P.v("dv")
                x1b_free[par] = odv
                fyt_free[0] = odv
                P.on("sync", DMA(dr["out"][tt * 128:(tt + 1) * 128, :], stg),
                     wait=[("dv", odv)], inc=[("st", 16)])
                fstage_free[q] = P.v("st")


    P.on("sync", None, wait=[("st", P.v("st"))])


def kernel(**inputs):
    xs = np.ascontiguousarray(inputs["L0_attn_x"]).astype(np.float32, copy=False)
    iden = np.eye(128, dtype=np.float32)
    consts = {
        "c_iden_bf": iden.astype(ml_dtypes.bfloat16),
        "c_iden_f": iden,
        "c_tri": np.triu(np.ones((128, 128), np.float32)),
        "c_sel32": np.repeat(np.eye(NH, dtype=np.float32), 128, axis=1),
        "c_sel32b": np.repeat(np.eye(NH, dtype=np.float32), 128,
                              axis=1).astype(ml_dtypes.bfloat16),
        "c_sel8": np.repeat(np.eye(NE, dtype=np.float32), 128, axis=1),
        "c_ones": np.ones((1, 128), np.float32),
    }
    wmap = {
        "w_in": inputs["L0_attn_in_proj_w"], "conv_w": inputs["L0_attn_conv_weight"],
        "conv_b": inputs["L0_attn_conv_bias"], "a_log": inputs["L0_attn_A_log"],
        "dt_bias": inputs["L0_attn_dt_bias"], "d_vec": inputs["L0_attn_D"],
        "w_out": inputs["L0_attn_out_proj_w"], "nsc": inputs["L0_attn_norm_scale"],
        "r_w": inputs["L0_ffn_router_w"], "r_b": inputs["L0_ffn_router_b"],
        "e_w": inputs["L0_ffn_expert_w"], "e_b": inputs["L0_ffn_expert_b"],
        "p_w": inputs["L0_ffn_proj_w"], "p_b": inputs["L0_ffn_proj_b"],
        "n_w": inputs["L0_ffn_norm_w"],
    }
    wmap = {k: np.ascontiguousarray(v, dtype=np.float32) for k, v in wmap.items()}
    nc, _ = build()
    in_maps = []
    for b in range(8):
        m = dict(consts)
        m.update(wmap)
        m["x_in"] = np.ascontiguousarray(xs[b])
        in_maps.append(m)
    res = run_bass_kernel_spmd(nc, in_maps, core_ids=list(range(8)))
    return np.stack([res.results[b]["out"] for b in range(8)], axis=0)


# version 18
# speedup vs baseline: 1.0335x; 1.0335x over previous
import numpy as np
import ml_dtypes
from contextlib import ExitStack
import concourse.bass as bass
import concourse.mybir as mybir
from concourse.bass_utils import run_bass_kernel_spmd

F32 = mybir.dt.float32
BF16 = mybir.dt.bfloat16
AF = mybir.ActivationFunctionType
ALU = mybir.AluOpType

T = 2048
DM = 2048
DI = 4096
NH = 32
QC = 128
NCH = T // QC
NTT = T // 128
KD = DM // 128
KI = DI // 128
PROJ = 8480
NE = 8
TH = T // 2
CLAMP = 30.0
LN_EPS = 1e-5
RMS_EPS = 1e-6
NEG = -1e30


class Prog:
    ENGINES = ("tensor", "vector", "scalar", "sync", "gpsimd")

    def __init__(self, nc, ctx):
        self.nc = nc
        self.q = {e: [] for e in self.ENGINES}
        self.cnt = {}
        self.sems = {}
        self._lw = {e: {} for e in self.ENGINES}
        for s in ("wd", "ld", "tr", "pe", "dv", "sc", "st"):
            self.sems[s] = ctx.enter_context(nc.semaphore("sem_" + s))
            self.cnt[s] = 0

    def v(self, s):
        return self.cnt[s]

    def on(self, eng, fn, wait=(), inc=()):
        w = []
        for s, val in wait:
            if val <= 0 or self._lw[eng].get(s, 0) >= val:
                continue
            self._lw[eng][s] = val
            w.append((s, val))
        for s, a in inc:
            self.cnt[s] += a
        self.q[eng].append((fn, tuple(w), tuple(inc)))

    def fence_all(self):
        vals = [(x, self.cnt[x]) for x in
                ("pe", "dv", "sc", "ld", "tr", "wd", "st")]
        for eng in self.ENGINES:
            self.on(eng, None, wait=vals)

    def flush(self, block):
        for eng in self.ENGINES:
            ops = self.q[eng]
            if not ops:
                continue

            def body(e, ops=ops):
                for fn, waits, incs in ops:
                    for s, val in waits:
                        e.wait_ge(self.sems[s], val)
                    inst = fn(e) if fn is not None else None
                    for s, a in incs:
                        inst.then_inc(self.sems[s], a)

            getattr(block, eng)(body)


class SbufPlan:
    def __init__(self, nc):
        self.nc = nc
        self.items = []

    def add(self, name, shape, dtype, p0, p1):
        nb = int(np.prod(shape[1:])) * mybir.dt.size(dtype)
        self.items.append((name, shape, dtype, p0, p1, nb))

    def alloc(self, base, budget):
        placed, offs = [], {}
        for name, shape, dtype, p0, p1, nb in sorted(self.items,
                                                     key=lambda x: -x[5]):
            nba = (nb + 31) // 32 * 32
            off = 0
            while True:
                hit = None
                for o, e, q0, q1 in placed:
                    if o < off + nba and off < e and not (q1 < p0 or p1 < q0):
                        hit = e
                        break
                if hit is None:
                    break
                off = (hit + 31) // 32 * 32
            placed.append((off, off + nba, p0, p1))
            offs[name] = (off, shape, dtype)
        self.top = max(e for _, e, _, _ in placed)
        assert self.top <= budget, f"sbuf over budget: {self.top} > {budget}"
        return {n: self.nc.alloc_sbuf_tensor_at(n, sh, dt, offset=base + o)
                for n, (o, sh, dt) in offs.items()}


def build(taps=()):
    taps = set(taps)
    nc = bass.Bass("TRN2", target_bir_lowering=False)
    dr = {}

    def din(name, shape, dtype=F32):
        dr[name] = nc.dram_tensor(name, shape, dtype, kind="ExternalInput").ap()

    din("x_in", [T, DM]); din("w_in", [DM, PROJ]); din("conv_w", [4, DI])
    din("conv_b", [DI]); din("a_log", [NH]); din("dt_bias", [NH])
    din("d_vec", [DI]); din("w_out", [DI, DM]); din("nsc", [DM])
    din("r_w", [DM, NE]); din("r_b", [NE]); din("e_w", [NE, DM, DM])
    din("e_b", [NE, DM]); din("p_w", [DM, DM]); din("p_b", [DM]); din("n_w", [DM])
    din("c_iden_bf", [128, 128], BF16); din("c_iden_f", [128, 128])
    din("c_tri", [128, 128]); din("c_sel32", [NH, NH * 128])
    din("c_sel8", [NE, NE * 128]); din("c_ones", [1, 128])
    din("c_sel32b", [NH, NH * 128], BF16)
    dr["out"] = nc.dram_tensor("out", [T, DM], F32, kind="ExternalOutput").ap()
    for nm, shape, dt in [("scr_gate", [128, NH * T], BF16),
                          ("scr_ssm", [128, NH * T], BF16),
                          ("scr_y", [128, NH * T], BF16),
                          ("scr_x1", [T, DM], F32),
                          ("scr_w1t", [128, KD * T], BF16),
                          ("scr_wo", [128, KI * DM], BF16),
                          ("scr_pw", [128, 4 * KD * 512], BF16)]:
        dr[nm] = nc.dram_tensor(nm, shape, dt, kind="Internal").ap()

    tap_specs = {}

    def mktap(name, shape, dtype=F32):
        if name not in taps:
            return None
        ap = nc.dram_tensor("tap_" + name, shape, dtype,
                            kind="ExternalOutput").ap()
        tap_specs[name] = ap
        return ap

    with ExitStack() as ctx:
        pl = SbufPlan(nc)
        A = pl.add
        A("iden_bf", [128, 128], BF16, 0, 5)
        A("iden_f", [128, 128], F32, 0, 5)
        A("tri", [128, 128], F32, 0, 5)
        A("ones_f", [1, 128], F32, 0, 5)
        A("sel32b", [NH, NH * 128], BF16, 0, 2)
        A("sel8", [NE, NE * 128], F32, 0, 5)
        A("cwcol", [128, 128], F32, 0, 1)
        A("cbcol", [128, 32], F32, 0, 1)
        A("dcol", [128, 32], F32, 0, 2)
        A("ebcol", [128, 128], F32, 0, 5)
        A("dtbcol", [32, 1], F32, 0, 1)
        A("acol", [32, 1], F32, 0, 1)
        A("wr_bf", [128, 128], BF16, 0, 3)
        A("rb_b", [128, 8], F32, 0, 3)
        A("astats", [128, 2 * NTT], F32, 0, 3)
        A("x1stats", [128, 2 * NTT], F32, 3, 5)
        A("nscb", [128, DM], F32, 0, 0)
        A("fwt", [8, T], F32, 3, 5)
        A("x0t", [128, 2 * DM], F32, 0, 0)
        A("zrst", [128, DM], F32, 0, 0)
        A("h2s", [128, 2 * DM], BF16, 0, 0)
        A("h2T", [128, KD * T], BF16, 0, 1)
        A("wpanB", [128, 2 * KD * 512], BF16, 0, 1)
        A("convrow", [128, T], F32, 0, 1)
        A("convtmp", [128, T], F32, 0, 1)
        A("ssmrow", [128, 2 * T], BF16, 0, 1)
        A("sgrow", [128, 2 * T], BF16, 1, 1)
        A("bc_bf", [128, T], BF16, 1, 2)
        A("cc_bf", [128, T], BF16, 1, 2)
        A("cumv", [32, T], F32, 1, 2)
        A("cumh", [32, T], BF16, 1, 2)
        A("cumm", [32, T], BF16, 1, 2)
        A("cuml", [32, T], BF16, 1, 2)
        A("dtv", [32, T], F32, 1, 2)
        A("excum", [32, T], BF16, 1, 2)
        A("t32a", [32, T], F32, 1, 1)
        A("t32b", [32, T], F32, 1, 1)
        A("wst2", [32, 128], F32, 2, 2)
        A("acol2", [32, 1], F32, 0, 1)
        A("ywin", [128, 2 * NH * QC], BF16, 2, 2)
        A("gwin", [128, 2 * NH * QC], BF16, 2, 2)
        A("xtch", [128, NH * 128], BF16, 2, 2)
        A("s_st", [128, NH * 128], F32, 2, 2)
        A("sbf", [128, NH * 128], BF16, 2, 2)
        A("cjlt", [128, 32], F32, 2, 2)
        A("wcolt", [128, 32], F32, 2, 2)
        A("cdbc", [128, 32], F32, 2, 2)
        A("cdrow", [1, 32], F32, 2, 2)
        A("cdrow2", [1, 32], F32, 2, 2)
        A("cbtri", [128, 128], F32, 2, 2)
        A("e_t", [128, 2 * 128], F32, 2, 2)
        A("et2", [128, 2 * 128], F32, 2, 2)
        A("dt_t", [128, 2 * 128], F32, 2, 2)
        A("mt_bf", [128, 2 * 128], BF16, 2, 2)
        A("ce_bf", [128, 2 * 128], BF16, 2, 2)
        A("bw_bf", [128, 2 * 128], BF16, 2, 2)
        A("bct_bf", [128, 128], BF16, 2, 2)
        A("lnst", [32, 128], F32, 2, 2)
        A("cjlst", [32, 128], F32, 2, 2)
        A("wst", [32, 128], F32, 2, 2)
        A("ylhs", [128, 4 * KI * 128], BF16, 3, 3)
        A("wpanD", [128, 2 * KI * 512], BF16, 3, 3)
        A("dstage", [128, 4 * DM], F32, 3, 3)
        A("x0t2", [128, 2 * DM], F32, 3, 3)
        A("lnt2", [128, DM], F32, 3, 3)
        A("w1bf", [128, 6 * DM], BF16, 3, 3)
        A("w1tst", [128, 2 * DM], BF16, 3, 3)
        A("m8", [128, 8], F32, 3, 3)
        A("mv8", [128, 8], F32, 3, 3)
        A("mr8", [128, 8], F32, 3, 3)
        A("lg8", [128, 8], F32, 3, 3)
        A("fw8", [128, 16], F32, 3, 3)
        A("comb", [128, KD * TH], F32, 4, 5)
        A("web", [128, 2 * KD * 512], BF16, 4, 4)
        A("w1blk", [128, KD * TH], BF16, 4, 4)
        A("fwb", [128, TH], F32, 4, 4)
        A("silu1", [128, 512], F32, 4, 4)
        A("silu2", [128, 512], F32, 4, 4)
        A("prodt", [128, 512], F32, 4, 4)
        A("cbstage", [128, 4 * KD * 128], BF16, 5, 5)
        A("wpanF", [128, 2 * KD * 512], BF16, 5, 5)
        A("x1t", [128, 2 * DM], F32, 5, 5)
        A("fstage", [128, 4 * DM], F32, 5, 5)
        A("fyt", [128, DM], F32, 5, 5)
        A("pbb", [128, DM], F32, 4, 5)
        A("nwb", [128, DM], F32, 4, 5)
        A("cst4", [128, 4], F32, 0, 5)
        A("bn24", [128, 48], F32, 0, 5)
        A("mv2", [128, 4], F32, 0, 5)
        A("rc", [128, 8], F32, 0, 5)
        base = (nc.sbuf_base + 4096 + 31) // 32 * 32
        sb = pl.alloc(base, nc.sbuf_base + 212000 - base)

        p_mm = [ctx.enter_context(nc.psum_tensor("p_mm0", [128, 512], F32)),
                ctx.enter_context(nc.psum_tensor("p_mm1", [128, 512], F32))]
        p_cb = ctx.enter_context(nc.psum_tensor("p_cb", [128, 512], F32))
        p_bc = ctx.enter_context(nc.psum_tensor("p_bc", [128, 512], F32))
        p_y = ctx.enter_context(nc.psum_tensor("p_y", [128, 512], F32))
        p_sl = ctx.enter_context(nc.psum_tensor("p_sl", [128, 512], F32))
        p_tr = ctx.enter_context(nc.psum_tensor("p_tr", [128, 1024], BF16))
        p_x = ctx.enter_context(nc.psum_tensor("p_x", [128, 512], F32))
        psd = dict(mm=p_mm, cb=p_cb, bc=p_bc, y=p_y, sl=p_sl, tr=p_tr, x=p_x)

        P = Prog(nc, ctx)
        blk = ctx.enter_context(nc.Block())
        _emit(nc, P, sb, psd, dr, mktap)
        P.flush(blk)
    return nc, tap_specs


def _emit(nc, P, sb, psd, dr, mktap):
    S = lambda n: sb[n]
    p_mm, p_cb, p_bc = psd["mm"], psd["cb"], psd["bc"]
    p_y, p_sl, p_tr, p_x = psd["y"], psd["sl"], psd["tr"], psd["x"]

    DMA = lambda dst, src: (lambda e: e.dma_start(dst, src))
    DMAS = lambda dst, src: (lambda e: e.dma_start(
        dst, src, allow_slow_non_contiguous=True))
    DMAT = lambda dst, src: (lambda e: e.dma_start_transpose(dst, src))
    MM = lambda o, l, r, s0, s1: (lambda e: e.matmul(o, l, r, start=s0, stop=s1))
    TRN = lambda o, i, idn: (lambda e: e.transpose(o, i, idn))
    CPY = lambda o, i: (lambda e: e.tensor_copy(o, i))
    MEMS = lambda o, v: (lambda e: e.memset(o, v))

    def SCPY(o, i, bias=0.0, scale=1.0, func=AF.Copy):
        if func == AF.Copy and not (isinstance(bias, float)
                                    and isinstance(scale, float)):
            func = AF.Identity
        return lambda e: e.activation(o, i, func, bias=bias, scale=scale)

    TT = lambda o, a, b, op: (lambda e: e.tensor_tensor(o, a, b, op))

    def TS(o, a, s1, s2, op0, op1=ALU.bypass):
        return lambda e: e.tensor_scalar(o, a, s1, s2, op0=op0, op1=op1)

    def STT(o, a, s, b, op0, op1):
        return lambda e: e.scalar_tensor_tensor(o, a, s, b, op0=op0, op1=op1)

    P.on("vector", MEMS(S("cst4")[:, 0:1], LN_EPS), inc=[("dv", 1)])
    P.on("vector", MEMS(S("cst4")[:, 1:2], RMS_EPS), inc=[("dv", 1)])
    for dst, srcn in [(S("iden_bf")[:], "c_iden_bf"), (S("iden_f")[:], "c_iden_f"),
                      (S("tri")[:], "c_tri"), (S("ones_f")[:], "c_ones"),
                      (S("sel32b")[:], "c_sel32b"), (S("sel8")[:], "c_sel8")]:
        P.on("sync", DMA(dst, dr[srcn]), inc=[("ld", 16)])
    P.on("sync", DMA(S("dtbcol")[:, 0:1], dr["dt_bias"].rearrange("(h o) -> h o", o=1)),
         inc=[("ld", 16)])
    P.on("sync", DMA(S("acol")[:, 0:1], dr["a_log"].rearrange("(h o) -> h o", o=1)),
         inc=[("ld", 16)])
    P.on("sync", DMAS(S("dcol")[:], dr["d_vec"].rearrange("(h p) -> p h", p=128)),
         inc=[("ld", 16)])
    P.on("sync", DMAS(S("cbcol")[:], dr["conv_b"].rearrange("(f p) -> p f", p=128)),
         inc=[("ld", 16)])
    P.on("sync", DMAS(S("ebcol")[:].rearrange("p (e f) -> p e f", f=16),
                     dr["e_b"].rearrange("e (f p) -> p e f", p=128)),
         inc=[("ld", 16)])
    P.on("sync", DMAS(S("cwcol")[:].rearrange("p (k f) -> p k f", f=32),
                      dr["conv_w"].rearrange("k (f p) -> p k f", p=128)),
         inc=[("ld", 16)])
    P.on("gpsimd", DMA(S("wr_bf")[:].rearrange("p (kt e) -> p kt e", kt=KD),
                       dr["r_w"].rearrange("(kt p) e -> p kt e", p=128)),
         inc=[("wd", 16)])
    wr_wd = P.v("wd")
    P.on("sync", DMA(S("nscb")[0:1, :], dr["nsc"].rearrange("(o d) -> o d", o=1)),
         inc=[("ld", 16)])
    P.on("sync", DMA(S("rb_b")[0:1, :], dr["r_b"].rearrange("(o e) -> o e", o=1)),
         inc=[("ld", 16)])
    ld0 = P.v("ld")

    P.on("scalar", SCPY(S("acol2")[:, 0:1], S("acol")[:, 0:1], func=AF.Exp),
         wait=[("ld", ld0)], inc=[("sc", 1)])
    P.on("vector", TS(S("acol")[:, 0:1], S("acol2")[:, 0:1], -1.0, None, ALU.mult),
         wait=[("sc", P.v("sc"))], inc=[("dv", 1)])
    fb_bc = [0]
    for b in range(4):
        P.on("tensor", MM(p_bc[:, 0:512], S("ones_f")[:],
                          S("nscb")[0:1, b * 512:(b + 1) * 512], True, True),
             wait=[("ld", ld0), ("dv", fb_bc[0])], inc=[("pe", 1)])
        P.on("vector", CPY(S("nscb")[:, b * 512:(b + 1) * 512], p_bc[:, 0:512]),
             wait=[("pe", P.v("pe"))], inc=[("dv", 1)])
        fb_bc[0] = P.v("dv")
    P.on("tensor", MM(p_bc[:, 0:8], S("ones_f")[:], S("rb_b")[0:1, :], True, True),
         wait=[("dv", fb_bc[0])], inc=[("pe", 1)])
    P.on("vector", CPY(S("rb_b")[:], p_bc[:, 0:8]),
         wait=[("pe", P.v("pe"))], inc=[("dv", 1)])
    fb_bc[0] = P.v("dv")
    init_dv = P.v("dv")

    def ln_stats(src, par, eps, mean_sq=False, pre_wait=()):
        eps_ap = S("cst4")[:, 0:1] if eps == LN_EPS else S("cst4")[:, 1:2]
        b0 = par * 24
        for i in range(4):
            P.on("vector", lambda e, i=i, src=src, b0=b0: e.bn_stats(
                S("bn24")[:, b0 + i * 6:b0 + (i + 1) * 6],
                src[:, i * 512:(i + 1) * 512]),
                 wait=(list(pre_wait) if i == 0 else []), inc=[("dv", 1)])
        P.on("vector", lambda e, b0=b0, par=par: e.bn_aggr(
            S("mv2")[:, par * 2:par * 2 + 2], S("bn24")[:, b0:b0 + 24]),
             wait=[("dv", P.v("dv"))], inc=[("dv", 1)])
        c = par * 4
        if mean_sq:
            P.on("vector", STT(S("rc")[:, c + 2:c + 3],
                               S("mv2")[:, par * 2:par * 2 + 1],
                               S("mv2")[:, par * 2:par * 2 + 1],
                               S("mv2")[:, par * 2 + 1:par * 2 + 2],
                               ALU.mult, ALU.add),
                 wait=[("dv", P.v("dv"))], inc=[("dv", 1)])
            var_ap = S("rc")[:, c + 2:c + 3]
        else:
            var_ap = S("mv2")[:, par * 2 + 1:par * 2 + 2]
        dvv = P.v("dv")
        P.on("scalar", SCPY(S("rc")[:, c + 3:c + 4], var_ap, bias=eps_ap,
                            func=AF.Sqrt),
             wait=[("dv", dvv)], inc=[("sc", 1)])
        P.on("vector", lambda e, c=c: e.reciprocal(S("rc")[:, c:c + 1],
                                                   S("rc")[:, c + 3:c + 4]),
             wait=[("sc", P.v("sc"))], inc=[("dv", 1)])
        P.on("vector", TS(S("rc")[:, c + 1:c + 2],
                          S("mv2")[:, par * 2:par * 2 + 1],
                          S("rc")[:, c:c + 1], -1.0, ALU.mult, ALU.mult),
             wait=[("dv", P.v("dv"))], inc=[("dv", 1)])
        return P.v("dv"), c

    h2T = S("h2T")
    ptr_free = [0]
    a_free = {0: (0, 0), 1: (0, 0)}
    astat_sc = {0: 0, 1: 0}
    a_done_sc = {}
    prev_xld = [ld0]
    for tt in range(NTT):
        par = tt % 2
        x0 = S("x0t")[:, par * DM:(par + 1) * DM]
        fsc, ftr = a_free[par]
        P.on("sync", DMA(x0, dr["x_in"][tt * 128:(tt + 1) * 128, :]),
             wait=[("ld", prev_xld[0]), ("sc", fsc), ("sc", ftr)],
             inc=[("ld", 16)])
        ldv = P.v("ld")
        prev_xld[0] = ldv
        b0 = par * 24
        for i in range(4):
            P.on("vector", lambda e, i=i, x0=x0, b0=b0: e.bn_stats(
                S("bn24")[:, b0 + i * 6:b0 + (i + 1) * 6],
                x0[:, i * 512:(i + 1) * 512]),
                 wait=([("ld", ldv), ("dv", init_dv),
                        ("sc", astat_sc[par])] if i == 0 else []),
                 inc=[("dv", 1)])
        P.on("vector", lambda e, b0=b0, par=par: e.bn_aggr(
            S("mv2")[:, par * 2:par * 2 + 2], S("bn24")[:, b0:b0 + 24]),
             wait=[("dv", P.v("dv"))], inc=[("dv", 1)])
        agg_dv = P.v("dv")
        c = par * 4
        mean_ap = S("mv2")[:, par * 2:par * 2 + 1]
        var_ap = S("mv2")[:, par * 2 + 1:par * 2 + 2]
        P.on("vector", TS(S("rc")[:, c + 3:c + 4], var_ap,
                          1.0 + RMS_EPS, RMS_EPS * LN_EPS, ALU.mult, ALU.add),
             wait=[("dv", agg_dv)], inc=[("dv", 1)])
        P.on("scalar", SCPY(S("rc")[:, c + 2:c + 3], S("rc")[:, c + 3:c + 4],
                            func=AF.Sqrt),
             wait=[("dv", P.v("dv"))], inc=[("sc", 1)])
        P.on("vector", lambda e, c=c: e.reciprocal(S("rc")[:, c:c + 1],
                                                   S("rc")[:, c + 2:c + 3]),
             wait=[("sc", P.v("sc"))], inc=[("dv", 1)])
        P.on("vector", TS(S("rc")[:, c + 1:c + 2], mean_ap,
                          S("rc")[:, c:c + 1], -1.0, ALU.mult, ALU.mult),
             wait=[("dv", P.v("dv"))], inc=[("dv", 1)])
        main_dv = P.v("dv")
        ac = S("astats")[:, 2 * tt:2 * tt + 1]
        ac1 = S("astats")[:, 2 * tt + 1:2 * tt + 2]
        P.on("scalar", SCPY(ac, var_ap, bias=S("cst4")[:, 0:1], func=AF.Sqrt),
             wait=[("dv", agg_dv)], inc=[("sc", 1)])
        P.on("vector", lambda e, ac=ac, ac1=ac1: e.reciprocal(ac1, ac),
             wait=[("sc", P.v("sc"))], inc=[("dv", 1)])
        P.on("vector", TS(ac, mean_ap, ac1, -1.0, ALU.mult, ALU.mult),
             wait=[("dv", P.v("dv"))], inc=[("dv", 1)])
        astat_sc[par] = P.v("sc")
        P.on("scalar", SCPY(S("zrst")[:], x0, bias=S("rc")[:, c + 1:c + 2],
                            scale=S("rc")[:, c:c + 1]),
             wait=[("dv", main_dv), ("dv", P.v("dv"))], inc=[("sc", 1)])
        z_sc = P.v("sc")
        a_sc = P.v("sc")
        h2sl = S("h2s")[:, par * DM:(par + 1) * DM]
        P.on("vector", TT(h2sl, S("zrst")[:], S("nscb")[:], ALU.mult),
             wait=[("sc", a_sc)], inc=[("dv", 1)])
        hdv = P.v("dv")
        h2T_r = h2T[:].rearrange("p (kt t) -> p kt t", kt=KD)
        for bi in range(2):
            for j in range(8):
                kt = bi * 8 + j
                P.on("tensor", TRN(p_tr[:, j * 128:(j + 1) * 128],
                                   h2sl[:, kt * 128:(kt + 1) * 128],
                                   S("iden_bf")[:]),
                     wait=([("dv", hdv), ("sc", ptr_free[0]), ("ld", ld0)]
                           if j == 0 else []),
                     inc=[("pe", 1)])
            pev = P.v("pe")
            P.on("scalar", lambda e, bi=bi, tt=tt, h2T_r=h2T_r: e.copy(
                h2T_r[:, bi * 8:(bi + 1) * 8, tt * 128:(tt + 1) * 128],
                p_tr[:, 0:1024].rearrange("p (j t) -> p j t", j=8)),
                 wait=[("pe", pev)], inc=[("sc", 1)])
            ptr_free[0] = P.v("sc")
        a_free[par] = (z_sc, P.v("sc"))
        a_done_sc[tt] = P.v("sc")
    A_tr = P.v("sc")

    tp = mktap("h2T", [128, KD * T], BF16)
    if tp is not None:
        P.on("sync", DMA(tp, h2T[:]), wait=[("sc", A_tr)], inc=[("st", 16)])

    wpan = S("wpanB")
    PANB = KD * 512
    wpan_free = [0, 0]
    mm_free = [dict(dv=0, sc=0), dict(dv=0, sc=0)]
    unit = [0]

    def load_panel(buf, f0, ncols):
        dst = wpan[:, buf * PANB:buf * PANB + KD * ncols].rearrange(
            "p (kt f) -> p kt f", kt=KD)
        src = dr["w_in"].rearrange("(kt p) f -> p kt f", p=128)[:, :, f0:f0 + ncols]
        for pc in range(4):
            k0, k1 = pc * 4, pc * 4 + 4
            P.on("gpsimd", DMA(dst[:, k0:k1], src[:, k0:k1]),
                 wait=[("pe", wpan_free[buf])], inc=[("wd", 16)])
        return P.v("wd")

    def acc_tile(buf, ncols, fcol, tb, mrows, wd_v, extra=(), a_sc=None):
        par = unit[0] % 2
        unit[0] += 1
        pp = p_mm[par]
        fr = mm_free[par]
        for kt in range(KD):
            w = ([("wd", wd_v), ("sc", A_tr if a_sc is None else a_sc),
                  ("dv", fr["dv"]),
                  ("sc", fr["sc"])] + list(extra)) if kt == 0 else []
            P.on("tensor", MM(pp[0:mrows, 0:512],
                              wpan[:, buf * PANB + kt * ncols + fcol:
                                   buf * PANB + kt * ncols + fcol + mrows],
                              h2T[:, kt * T + tb * 512:kt * T + (tb + 1) * 512],
                              kt == 0, kt == KD - 1),
                 wait=w, inc=[("pe", 1)] if kt == KD - 1 else ())
        return pp, par, P.v("pe")

    spill_free = {0: 0, 1: 0}
    sgrow_free = {0: 0, 1: 0}
    prev_conv = [0, 0]
    prev_silu = [0, 0]

    for fcb in range(8):
        buf = fcb % 2
        wd_v = load_panel(buf, DI + fcb * 512, 512)
        for ft in range(4):
            frow = fcb * 4 + ft
            rbuf = frow % 2
            cw = S("cwcol")
            last_silu = 0
            for tb in range(4):
                pp, par, pe_v = acc_tile(buf, 512, ft * 128, tb, 128, wd_v,
                                         a_sc=a_done_sc[4 * tb + 3])
                P.on("scalar", SCPY(S("convrow")[:, tb * 512:(tb + 1) * 512],
                                    pp[:, 0:512]),
                     wait=[("pe", pe_v),
                           ("dv", prev_conv[0] if tb == 0 else prev_conv[1])],
                     inc=[("sc", 1)])
                mm_free[par]["sc"] = P.v("sc")
                if tb in (1, 3):
                    half = tb // 2
                    a, b2 = half * 1024, (half + 1) * 1024
                    evac_sc = P.v("sc")
                    P.on("vector", TS(S("convtmp")[:, a:b2],
                                      S("convrow")[:, a:b2],
                                      cw[:, 3 * 32 + frow:3 * 32 + frow + 1],
                                      None, ALU.mult),
                         wait=[("sc", evac_sc), ("sc", prev_silu[half]),
                               ("ld", ld0)],
                         inc=[("dv", 1)])
                    for k, sh in ((2, 1), (1, 2), (0, 3)):
                        lo = max(a, sh)
                        P.on("vector", STT(S("convtmp")[:, lo:b2],
                                           S("convrow")[:, lo - sh:b2 - sh],
                                           cw[:, k * 32 + frow:k * 32 + frow + 1],
                                           S("convtmp")[:, lo:b2],
                                           ALU.mult, ALU.add),
                             wait=[("dv", P.v("dv"))], inc=[("dv", 1)])
                    prev_conv[half] = P.v("dv")
                    P.on("scalar", SCPY(
                        S("ssmrow")[:, rbuf * T + a:rbuf * T + b2],
                        S("convtmp")[:, a:b2], func=AF.Silu,
                        bias=S("cbcol")[:, frow:frow + 1]),
                         wait=[("dv", prev_conv[half]),
                               ("ld", spill_free[rbuf])],
                         inc=[("sc", 1)])
                    prev_silu[half] = P.v("sc")
                    last_silu = P.v("sc")
            P.on("sync", DMA(dr["scr_ssm"][:, frow * T:(frow + 1) * T],
                             S("ssmrow")[:, rbuf * T:(rbuf + 1) * T]),
                 wait=[("sc", last_silu)], inc=[("ld", 16)])
            spill_free[rbuf] = P.v("ld")
        wpan_free[buf] = P.v("pe")
    ssm_ld = P.v("ld")

    buf = 0
    wd_v = load_panel(buf, 2 * DI, 288)
    for tb in range(4):
        pp, par, pe_v = acc_tile(buf, 288, 0, tb, 32, wd_v)
        P.on("vector", CPY(S("dtv")[:, tb * 512:(tb + 1) * 512],
                           pp[0:32, 0:512]),
             wait=[("pe", pe_v)], inc=[("dv", 1)])
        mm_free[par]["dv"] = P.v("dv")
    for name, fcol in (("bc_bf", 32), ("cc_bf", 160)):
        for tb in range(4):
            pp, par, pe_v = acc_tile(buf, 288, fcol, tb, 128, wd_v)
            P.on("scalar", SCPY(S(name)[:, tb * 512:(tb + 1) * 512], pp[:, 0:512]),
                 wait=[("pe", pe_v)], inc=[("sc", 1)])
            mm_free[par]["sc"] = P.v("sc")
    wpan_free[buf] = P.v("pe")
    tail_dv, tail_sc = P.v("dv"), P.v("sc")

    dts = S("dtv")[:, :]
    P.on("vector", TS(dts, dts, S("dtbcol")[:, 0:1], None, ALU.add),
         wait=[("dv", tail_dv), ("ld", ld0)], inc=[("dv", 1)])
    P.on("scalar", SCPY(S("t32a")[:], dts, func=AF.Abs),
         wait=[("dv", P.v("dv"))], inc=[("sc", 1)])
    P.on("scalar", SCPY(S("t32b")[:], S("t32a")[:], scale=-1.0, func=AF.Exp),
         wait=[("sc", P.v("sc"))], inc=[("sc", 1)])
    P.on("scalar", SCPY(S("t32a")[:], S("t32b")[:], bias=1.0, func=AF.Ln),
         wait=[("sc", P.v("sc"))], inc=[("sc", 1)])
    sp_sc = P.v("sc")
    P.on("vector", TS(dts, dts, 0.0, None, ALU.max),
         wait=[("sc", sp_sc)], inc=[("dv", 1)])
    P.on("vector", TT(dts, dts, S("t32a")[:], ALU.add),
         wait=[("dv", P.v("dv"))], inc=[("dv", 1)])
    P.on("vector", TS(S("t32a")[:], dts, S("acol")[:, 0:1], None, ALU.mult),
         wait=[("dv", P.v("dv"))], inc=[("dv", 1)])
    for ch in range(NCH):
        sl = slice(ch * QC, (ch + 1) * QC)
        P.on("vector", lambda e, sl=sl: e.tensor_tensor_scan(
            S("cumv")[:, sl], S("t32a")[:, sl], S("t32a")[:, sl], 0.0,
            op0=ALU.add, op1=ALU.bypass),
             wait=[("dv", P.v("dv"))], inc=[("dv", 1)])
    cum_dv = P.v("dv")
    P.on("scalar", SCPY(S("excum")[:], S("cumv")[:, :], func=AF.Exp),
         wait=[("dv", cum_dv)], inc=[("sc", 1)])
    P.on("vector", CPY(S("cumh")[:], S("cumv")[:]),
         wait=[("dv", cum_dv)], inc=[("dv", 1)])
    P.on("vector", TT(S("t32b")[:], S("cumv")[:], S("cumh")[:], ALU.subtract),
         wait=[("dv", P.v("dv"))], inc=[("dv", 1)])
    P.on("vector", CPY(S("cumm")[:], S("t32b")[:]),
         wait=[("dv", P.v("dv"))], inc=[("dv", 1)])
    P.on("vector", TT(S("t32b")[:], S("t32b")[:], S("cumm")[:], ALU.subtract),
         wait=[("dv", P.v("dv"))], inc=[("dv", 1)])
    P.on("vector", CPY(S("cuml")[:], S("t32b")[:]),
         wait=[("dv", P.v("dv"))], inc=[("dv", 1)])
    prep_dv, prep_sc = P.v("dv"), P.v("sc")

    for fcb in range(8):
        buf = fcb % 2
        wd_v = load_panel(buf, fcb * 512, 512)
        for ft in range(4):
            frow = fcb * 4 + ft
            rbuf = frow % 2
            grow = S("sgrow")[:, rbuf * T:(rbuf + 1) * T]
            for tb in range(4):
                pp, par, pe_v = acc_tile(buf, 512, ft * 128, tb, 128, wd_v)
                P.on("scalar", SCPY(grow[:, tb * 512:(tb + 1) * 512],
                                    pp[:, 0:512], func=AF.Silu),
                     wait=[("pe", pe_v), ("ld", sgrow_free[rbuf])],
                     inc=[("sc", 1)])
                mm_free[par]["sc"] = P.v("sc")
            P.on("sync", DMA(dr["scr_gate"][:, frow * T:(frow + 1) * T], grow),
                 wait=[("sc", P.v("sc"))], inc=[("ld", 16)])
            sgrow_free[rbuf] = P.v("ld")
        wpan_free[buf] = P.v("pe")
    gate_ld = P.v("ld")

    tp = mktap("cum", [32, T])
    if tp is not None:
        P.on("sync", DMA(tp, S("cumv")[:]),
             wait=[("sc", prep_sc), ("dv", prep_dv)], inc=[("st", 16)])
    tp = mktap("dt", [32, T])
    if tp is not None:
        P.on("sync", DMA(tp, S("dtv")[:]),
             wait=[("sc", prep_sc), ("dv", prep_dv)], inc=[("st", 16)])
    tp = mktap("bc", [128, T], BF16)
    if tp is not None:
        P.on("sync", DMA(tp, S("bc_bf")[:]), wait=[("sc", tail_sc)],
             inc=[("st", 16)])
    tp = mktap("cc", [128, T], BF16)
    if tp is not None:
        P.on("sync", DMA(tp, S("cc_bf")[:]), wait=[("sc", tail_sc)],
             inc=[("st", 16)])
    tp = mktap("scr_ssm", [128, NH * T], BF16)
    if tp is not None:
        P.on("sync", DMA(tp, dr["scr_ssm"]), wait=[("ld", ssm_ld)],
             inc=[("st", 16)])
    tp = mktap("scr_gate", [128, NH * T], BF16)
    if tp is not None:
        P.on("sync", DMA(tp, dr["scr_gate"]), wait=[("ld", gate_ld)],
             inc=[("st", 16)])

    ywin, xtch = S("ywin"), S("xtch")
    sst, sbf = S("s_st"), S("sbf")
    P.fence_all()
    for db in range(4):
        for kh in range(2):
            k0, k1 = kh * 16, kh * 16 + 16
            P.on("gpsimd", DMA(
                dr["scr_wo"].rearrange("p (db kt f) -> p db kt f",
                                       db=4, kt=KI)[:, db, k0:k1, :],
                dr["w_out"].rearrange("(kt p) f -> p kt f", p=128)[
                    :, k0:k1, db * 512:(db + 1) * 512]),
                 inc=[("wd", 16)])
    wo_wd = P.v("wd")
    for db in range(4):
        P.on("gpsimd", DMA(
            dr["scr_pw"].rearrange("p (db kt f) -> p db kt f",
                                   db=4, kt=KD)[:, db],
            dr["p_w"].rearrange("(kt p) f -> p kt f", p=128)[
                :, :, db * 512:(db + 1) * 512]),
             inc=[("wd", 16)])
    pw_wd = P.v("wd")
    P.on("vector", MEMS(sst[:], 0.0), inc=[("dv", 1)])
    P.on("vector", MEMS(sbf[:], 0.0), inc=[("dv", 1)])
    sinit_dv = P.v("dv")

    pair_ld = {}
    chunk_done_dv = {-1: 0}
    yspill_ld = {}
    gwin_ld = {}
    gmul_dv = [0]

    def load_pair(cp):
        gate_free = chunk_done_dv.get(2 * cp - 1, 0)
        xw = [("ld", ssm_ld), ("ld", yspill_ld.get(cp - 1, 0)),
              ("dv", gate_free)]
        for pc in range(4):
            h0, h1 = pc * 8, pc * 8 + 8
            for ci in range(2):
                src = dr["scr_ssm"].rearrange("p (h t) -> p h t", h=NH)[
                    :, h0:h1, (2 * cp + ci) * QC:(2 * cp + ci + 1) * QC]
                dst = ywin[:].rearrange("p (c h i) -> p c h i", c=2, h=NH)[
                    :, ci, h0:h1]
                P.on("sync", DMA(dst, src), wait=xw, inc=[("ld", 16)])
        pair_ld[cp] = P.v("ld")
        gw = [("ld", gate_ld), ("dv", gmul_dv[0])]
        for pc in range(4):
            h0, h1 = pc * 8, pc * 8 + 8
            for ci in range(2):
                src = dr["scr_gate"].rearrange("p (h t) -> p h t", h=NH)[
                    :, h0:h1, (2 * cp + ci) * QC:(2 * cp + ci + 1) * QC]
                dst = S("gwin")[:].rearrange("p (c h i) -> p c h i",
                                             c=2, h=NH)[:, ci, h0:h1]
                P.on("sync", DMA(dst, src), wait=gw, inc=[("ld", 16)])
        gwin_ld[cp] = P.v("ld")

    def yw(cpar, h):
        return ywin[:, cpar * NH * QC + h * QC:cpar * NH * QC + (h + 1) * QC]

    load_pair(0)
    fr_bc = fb_bc
    fr_cb = [0]
    fr_y = [0]
    fr_sl = [0]
    fr_x = [0]
    sbf_rd = {h: 0 for h in range(NH)}
    xt_last_sc = [A_tr * 0]
    prev = dict(s1=0, t3=0, t4=0, t5=0, d1=0, d4=0, d7=0, s2=0)
    prevu = [dict(s1=0, t3=0, t4=0, t5=0, d2=0, d4=0),
             dict(s1=0, t3=0, t4=0, t5=0, d2=0, d4=0)]
    chunk_rd = dict(cjlt=0, wcolt=0, cdbc=0, cbtri=0, bct=0)

    for c in range(NCH):
        cpar = c % 2
        csl = slice(c * QC, (c + 1) * QC)
        if c % 2 == 0 and c > 0:
            load_pair(c // 2)
        ld_need = pair_ld[c // 2]
        P.on("tensor", MM(p_cb[:, 0:128], S("bc_bf")[:, csl],
                          S("cc_bf")[:, csl], True, True),
             wait=[("sc", tail_sc), ("dv", fr_cb[0])], inc=[("pe", 1)])
        P.on("vector", TT(S("cbtri")[:], p_cb[:, 0:128], S("tri")[:], ALU.mult),
             wait=[("pe", P.v("pe")), ("dv", chunk_rd["cbtri"]), ("ld", ld0)],
             inc=[("dv", 1)])
        fr_cb[0] = P.v("dv")
        cbtri_dv = P.v("dv")
        P.on("tensor", TRN(p_tr[:, 0:128], S("bc_bf")[:, csl], S("iden_bf")[:]),
             wait=[("sc", xt_last_sc[0])], inc=[("pe", 1)])
        P.on("scalar", SCPY(S("bct_bf")[:], p_tr[:, 0:128]),
             wait=[("pe", P.v("pe")), ("dv", chunk_rd["bct"])], inc=[("sc", 1)])
        bct_sc = P.v("sc")
        P.on("scalar", SCPY(S("lnst")[:], S("dtv")[:, csl], func=AF.Ln),
             wait=[("sc", prep_sc), ("dv", prep_dv)], inc=[("sc", 1)])
        P.on("vector", TT(S("cjlst")[:], S("cumv")[:, csl], S("lnst")[:],
                          ALU.subtract),
             wait=[("sc", P.v("sc"))], inc=[("dv", 1)])
        cjl_dv = P.v("dv")
        P.on("vector", TS(S("wst")[:], S("cumv")[:, csl],
                          S("cumv")[:, c * QC + QC - 1:c * QC + QC],
                          None, ALU.subtract),
             inc=[("dv", 1)])
        P.on("scalar", SCPY(S("wst2")[:], S("wst")[:], scale=-1.0, func=AF.Exp),
             wait=[("dv", P.v("dv"))], inc=[("sc", 1)])
        P.on("vector", TT(S("wst")[:], S("wst2")[:], S("dtv")[:, csl],
                          ALU.mult),
             wait=[("sc", P.v("sc"))], inc=[("dv", 1)])
        w_dv = P.v("dv")
        P.on("tensor", TRN(p_x[:, 0:32], S("cjlst")[:], S("iden_f")[0:32, 0:32]),
             wait=[("dv", cjl_dv), ("dv", fr_x[0])], inc=[("pe", 1)])
        P.on("tensor", TRN(p_x[:, 32:64], S("wst")[:], S("iden_f")[0:32, 0:32]),
             wait=[("dv", w_dv)], inc=[("pe", 1)])
        P.on("tensor", TRN(p_x[0:1, 64:96],
                           S("cumv")[:, c * QC + QC - 1:c * QC + QC],
                           S("iden_f")[0:32, 0:32]),
             wait=[("dv", cum_dv)], inc=[("pe", 1)])
        pev = P.v("pe")
        P.on("vector", CPY(S("cjlt")[:], p_x[:, 0:32]),
             wait=[("pe", pev), ("dv", chunk_rd["cjlt"])], inc=[("dv", 1)])
        P.on("vector", CPY(S("wcolt")[:], p_x[:, 32:64]),
             wait=[("dv", chunk_rd["wcolt"])], inc=[("dv", 1)])
        P.on("scalar", SCPY(S("cdrow")[:], p_x[0:1, 64:96]),
             wait=[("pe", pev)], inc=[("sc", 1)])
        P.on("scalar", SCPY(S("cdrow2")[:], S("cdrow")[:], func=AF.Exp),
             wait=[("sc", P.v("sc"))], inc=[("sc", 1)])
        fr_x[0] = P.v("dv")
        cjl_ev = P.v("dv")
        P.on("tensor", MM(p_bc[:, 0:32], S("ones_f")[:], S("cdrow2")[:],
                          True, True),
             wait=[("sc", P.v("sc")), ("dv", fr_bc[0])], inc=[("pe", 1)])
        P.on("vector", CPY(S("cdbc")[:], p_bc[:, 0:32]),
             wait=[("pe", P.v("pe")), ("dv", chunk_rd["cdbc"])], inc=[("dv", 1)])
        fr_bc[0] = P.v("dv")
        shared_dv = P.v("dv")

        xt_sc = {}
        for bi in range(4):
            for j in range(8):
                h = bi * 8 + j
                P.on("tensor", TRN(p_tr[:, j * 128:(j + 1) * 128],
                                   yw(cpar, h), S("iden_bf")[:]),
                     wait=([("ld", ld_need), ("sc", bct_sc),
                            ("pe", prev["t5"])] if (bi == 0 and j == 0)
                           else ([("sc", xt_sc[bi - 1])] if j == 0 else [])),
                     inc=[("pe", 1)])
            pev = P.v("pe")
            P.on("scalar", SCPY(xtch[:, bi * 1024:(bi + 1) * 1024],
                                p_tr[:, 0:1024]),
                 wait=[("pe", pev)], inc=[("sc", 1)])
            xt_sc[bi] = P.v("sc")
        xt_last_sc[0] = xt_sc[3]

        stash = {}

        def stage1(h):
            up = h % 2
            usl = slice(up * 128, (up + 1) * 128)
            e_t = S("e_t")[:, usl]
            et2 = S("et2")[:, usl]
            dt_t = S("dt_t")[:, usl]
            mt_bf = S("mt_bf")[:, usl]
            ce_bf = S("ce_bf")[:, usl]
            bw_bf = S("bw_bf")[:, usl]
            prev2 = prevu[up]
            sel = S("sel32b")[:, h * 128:(h + 1) * 128]
            P.on("tensor", MM(p_bc[:, 0:128], sel, S("cumh")[:, csl],
                              True, False),
                 wait=[("dv", fr_bc[0]), ("dv", prep_dv)], inc=[("pe", 1)])
            P.on("tensor", MM(p_bc[:, 0:128], sel, S("cumm")[:, csl],
                              False, False), inc=[("pe", 1)])
            P.on("tensor", MM(p_bc[:, 0:128], sel, S("cuml")[:, csl],
                              False, True), inc=[("pe", 1)])
            t1 = P.v("pe")
            P.on("tensor", MM(p_cb[:, 0:128], sel, S("excum")[:, csl],
                              True, True),
                 wait=[("dv", fr_cb[0]), ("sc", prep_sc)], inc=[("pe", 1)])
            t2 = P.v("pe")
            P.on("vector", TS(e_t, p_bc[:, 0:128],
                              S("cjlt")[:, h:h + 1], CLAMP,
                              ALU.subtract, ALU.min),
                 wait=[("pe", t1), ("dv", cjl_ev), ("sc", prev2["s1"])],
                 inc=[("dv", 1)])
            fr_bc[0] = P.v("dv")
            d1 = P.v("dv")
            P.on("scalar", SCPY(et2, e_t, func=AF.Exp),
                 wait=[("dv", d1), ("dv", prev2["d2"])], inc=[("sc", 1)])
            s1 = P.v("sc")
            P.on("vector", TT(ce_bf, p_cb[:, 0:128], S("cc_bf")[:, csl],
                              ALU.mult),
                 wait=[("pe", t2), ("pe", prev2["t4"])], inc=[("dv", 1)])
            fr_cb[0] = P.v("dv")
            d3 = P.v("dv")
            P.on("vector", TS(bw_bf, S("bct_bf")[:],
                              S("wcolt")[:, h:h + 1], None, ALU.mult),
                 wait=[("sc", bct_sc), ("dv", shared_dv), ("pe", prev2["t5"])],
                 inc=[("dv", 1)])
            d6 = P.v("dv")
            P.on("vector", TT(mt_bf, et2, S("cbtri")[:], ALU.mult),
                 wait=[("sc", s1), ("dv", cbtri_dv), ("pe", prev2["t3"])],
                 inc=[("dv", 1)])
            d2 = P.v("dv")
            P.on("scalar", SCPY(dt_t, yw(cpar, h),
                                scale=S("dcol")[:, h:h + 1]),
                 wait=[("ld", ld_need), ("ld", ld0), ("dv", prev2["d4"])],
                 inc=[("sc", 1)])
            s2 = P.v("sc")
            stash[h] = dict(mt=mt_bf, ce=ce_bf, bw=bw_bf, dt_t=dt_t,
                            d2=d2, d3=d3, d6=d6, s1=s1, s2=s2, up=up)

        def stage2(h):
            st2 = stash.pop(h)
            up = st2["up"]
            xt = xtch[:, h * 128:(h + 1) * 128]
            P.on("tensor", MM(p_y[:, 0:128], xt, st2["mt"], True, False),
                 wait=[("dv", st2["d2"]), ("sc", xt_sc[h // 8]),
                       ("dv", fr_y[0])],
                 inc=[("pe", 1)])
            t3 = P.v("pe")
            P.on("tensor", MM(p_y[:, 0:128], sbf[:, h * 128:(h + 1) * 128],
                              st2["ce"], False, True),
                 wait=[("dv", st2["d3"]), ("sc", sbf_rd[h]),
                       ("dv", sinit_dv)],
                 inc=[("pe", 1)])
            t4 = P.v("pe")
            P.on("tensor", MM(p_sl[:, 0:128], st2["bw"], xt, True, True),
                 wait=[("dv", st2["d6"]), ("dv", fr_sl[0])], inc=[("pe", 1)])
            t5 = P.v("pe")
            P.on("vector", TT(yw(cpar, h), st2["dt_t"], p_y[:, 0:128],
                              ALU.add),
                 wait=[("pe", t4), ("sc", st2["s2"])], inc=[("dv", 1)])
            fr_y[0] = P.v("dv")
            d4 = P.v("dv")
            P.on("vector", STT(sst[:, h * 128:(h + 1) * 128],
                               sst[:, h * 128:(h + 1) * 128],
                               S("cdbc")[:, h:h + 1], p_sl[:, 0:128],
                               ALU.mult, ALU.add),
                 wait=[("pe", t5), ("dv", shared_dv), ("sc", sbf_rd[h])],
                 inc=[("dv", 1)])
            fr_sl[0] = P.v("dv")
            d7 = P.v("dv")
            P.on("scalar", SCPY(sbf[:, h * 128:(h + 1) * 128],
                                sst[:, h * 128:(h + 1) * 128]),
                 wait=[("dv", d7)], inc=[("sc", 1)])
            sbf_rd[h] = P.v("sc")
            prev["t5"] = t5
            prevu[up] = dict(s1=st2["s1"], t3=t3, t4=t4, t5=t5,
                             d2=st2["d2"], d4=d4)

        for h in range(NH):
            stage1(h)
            if h > 0:
                stage2(h - 1)
        stage2(NH - 1)
        chunk_done_dv[c] = P.v("dv")
        chunk_rd = dict(cjlt=P.v("dv"), wcolt=P.v("dv"), cdbc=P.v("dv"),
                        cbtri=P.v("dv"), bct=P.v("dv"))
        if c % 2 == 1:
            P.on("vector", TT(ywin[:], ywin[:], S("gwin")[:], ALU.mult),
                 wait=[("dv", P.v("dv")), ("ld", gwin_ld[c // 2])],
                 inc=[("dv", 1)])
            gmul_dv[0] = P.v("dv")
            for ci in range(2):
                P.on("sync", DMA(
                    dr["scr_y"][:, (c - 1 + ci) * NH * QC:(c + ci) * NH * QC],
                    ywin[:, ci * NH * QC:(ci + 1) * NH * QC]),
                     wait=[("dv", P.v("dv"))], inc=[("ld", 16)])
            yspill_ld[c // 2] = P.v("ld")
    y_ld = P.v("ld")
    C_pe = P.v("pe")

    tp = mktap("scr_y", [128, NH * T], BF16)
    if tp is not None:
        P.on("sync", DMA(tp, dr["scr_y"]), wait=[("ld", y_ld)], inc=[("st", 16)])

    P.fence_all()
    wpanD = S("wpanD")
    PAND = KI * 512
    wD_free = [C_pe, C_pe]
    ylhs = S("ylhs")
    ylhs_free = [C_pe]
    mmD = [dict(sc=0), dict(sc=0)]
    x0b_free = {0: 0, 1: 0}
    ptrD_free = [P.v("sc")]
    w1_free = {0: (0, 0, 0), 1: (0, 0, 0)}
    dstage_free = {q: 0 for q in range(4)}
    fwt_sc = 0

    def load_ylhs(tblk):
        for sidx in range(4):
            tt = tblk * 4 + sidx
            P.on("sync", DMA(
                ylhs[:, sidx * KI * 128:(sidx + 1) * KI * 128],
                dr["scr_y"][:, tt * NH * QC:(tt + 1) * NH * QC]),
                 wait=[("ld", y_ld), ("pe", ylhs_free[0])],
                 inc=[("wd", 16)])
        return P.v("wd")

    def yl(slot, ktf):
        return ylhs[:, slot * KI * 128 + ktf * 128:slot * KI * 128 + (ktf + 1) * 128]

    def load_wD(buf, d0):
        db = d0 // 512
        P.on("sync", DMA(
            wpanD[:, buf * PAND:(buf + 1) * PAND],
            dr["scr_wo"][:, db * KI * 512:(db + 1) * KI * 512]),
             wait=[("pe", wD_free[buf]), ("wd", wo_wd)], inc=[("wd", 16)])
        return P.v("wd")

    lnt2_free = [0, 0]
    w1b_free = {s: 0 for s in range(6)}
    pend_p2 = []
    pend_fw = []
    fwt_scL = [0]

    def flush_fw():
        for tt, fwdv, fpar in pend_fw:
            P.on("tensor", TRN(p_x[0:8, 128:256],
                               S("fw8")[:, fpar * 8:fpar * 8 + 8],
                               S("iden_f")[:]),
                 wait=[("dv", fwdv), ("sc", fwt_scL[0])], inc=[("pe", 1)])
            P.on("scalar", SCPY(S("fwt")[0:8, tt * 128:(tt + 1) * 128],
                                p_x[0:8, 128:256]),
                 wait=[("pe", P.v("pe"))], inc=[("sc", 1)])
            fwt_scL[0] = P.v("sc")
        del pend_fw[:]

    def emit_pass2(info):
        tt, slot, wbsc, w1b = info["tt"], info["slot"], info["wbsc"], info["w1b"]
        par = tt % 2
        flush_fw()
        w1t = S("w1tst")[:, par * DM:(par + 1) * DM]
        w1t_r = w1t.rearrange("p (kt i) -> p kt i", kt=KD)
        ftr, fld, fpe = w1_free[par]
        for bi in range(2):
            for j in range(8):
                kt = bi * 8 + j
                P.on("tensor", TRN(p_tr[:, j * 128:(j + 1) * 128],
                                   w1b[:, kt * 128:(kt + 1) * 128],
                                   S("iden_bf")[:]),
                     wait=([("sc", wbsc), ("sc", ptrD_free[0]),
                            ("ld", fld), ("pe", fpe)] if j == 0 else []),
                     inc=[("pe", 1)])
            pev = P.v("pe")
            P.on("scalar", lambda e, bi=bi, w1t_r=w1t_r: e.copy(
                w1t_r[:, bi * 8:(bi + 1) * 8, :],
                p_tr[:, 0:1024].rearrange("p (j t) -> p j t", j=8)),
                 wait=[("pe", pev)], inc=[("sc", 1)])
            ptrD_free[0] = P.v("sc")
        w1b_free[slot] = P.v("pe")
        trv = P.v("sc")
        P.on("sync", DMA(
            dr["scr_w1t"].rearrange("p (kt t) -> p kt t", kt=KD)[
                :, :, tt * 128:(tt + 1) * 128],
            w1t.rearrange("p (kt i) -> p kt i", kt=KD)),
             wait=[("sc", trv)], inc=[("ld", 16)])
        for kt in range(KD):
            P.on("tensor", MM(p_x[0:128, 0:8],
                              w1t[:, kt * 128:(kt + 1) * 128],
                              S("wr_bf")[:, kt * 8:(kt + 1) * 8],
                              kt == 0, kt == KD - 1),
                 wait=([("sc", trv), ("wd", wr_wd), ("dv", fr_x[0])]
                       if kt == 0 else []),
                 inc=[("pe", 1)] if kt == KD - 1 else ())
        pev = P.v("pe")
        w1_free[par] = (trv, P.v("ld"), pev)
        P.on("vector", TT(S("lg8")[:], p_x[:, 0:8], S("rb_b")[:], ALU.add),
             wait=[("pe", pev)], inc=[("dv", 1)])
        fr_x[0] = P.v("dv")
        P.on("vector", lambda e: e.max(out=S("m8")[:], in_=S("lg8")[:]),
             wait=[("dv", P.v("dv"))], inc=[("dv", 1)])
        P.on("vector", MEMS(S("mv8")[:], NEG), inc=[("dv", 1)])
        P.on("vector", CPY(S("mv8")[:, 0:2], S("m8")[:, 0:2]),
             wait=[("dv", P.v("dv"))], inc=[("dv", 1)])
        P.on("vector", lambda e: e.match_replace(
            out=S("mr8")[:], in_to_replace=S("mv8")[:],
            in_values=S("lg8")[:], imm_value=NEG),
             wait=[("dv", P.v("dv"))], inc=[("dv", 1)])
        P.on("vector", TT(S("rc")[:, 6:7], S("m8")[:, 0:1], S("m8")[:, 1:2],
                          ALU.subtract),
             wait=[("dv", P.v("dv"))], inc=[("dv", 1)])
        dvv = P.v("dv")
        P.on("scalar", SCPY(S("rc")[:, 7:8], S("rc")[:, 6:7],
                            func=AF.Sigmoid),
             wait=[("dv", dvv)], inc=[("sc", 1)])
        P.on("scalar", SCPY(S("rc")[:, 6:7], S("rc")[:, 6:7], scale=-1.0,
                            func=AF.Sigmoid),
             wait=[("sc", P.v("sc"))], inc=[("sc", 1)])
        rsc = P.v("sc")
        P.on("vector", TT(S("rc")[:, 7:8], S("rc")[:, 7:8], S("rc")[:, 6:7],
                          ALU.subtract),
             wait=[("sc", rsc)], inc=[("dv", 1)])
        fpar = tt % 2
        fw8 = S("fw8")[:, fpar * 8:fpar * 8 + 8]
        P.on("vector", TS(fw8, S("lg8")[:], S("m8")[:, 0:1], None,
                          ALU.is_equal),
             inc=[("dv", 1)])
        P.on("vector", TS(fw8, fw8, S("rc")[:, 7:8],
                          S("rc")[:, 6:7], ALU.mult, ALU.add),
             wait=[("dv", P.v("dv"))], inc=[("dv", 1)])
        P.on("vector", TT(S("mv8")[:], S("mr8")[:], S("lg8")[:],
                          ALU.not_equal),
             inc=[("dv", 1)])
        P.on("vector", TT(fw8, fw8, S("mv8")[:], ALU.mult),
             wait=[("dv", P.v("dv"))], inc=[("dv", 1)])
        pend_fw.append((tt, P.v("dv"), fpar))

    pending = load_ylhs(0)
    pend_wd = {}
    for tblk in range(4):
        blk_wd = pending
        for db in range(4):
            buf = db % 2
            wd_v = pend_wd.pop((tblk, db), None)
            if wd_v is None:
                wd_v = load_wD(buf, db * 512)
            for q in range(4):
                tt = tblk * 4 + q
                par = (q + db) % 2
                pp = p_mm[par]
                for ktf in range(KI):
                    w = ([("wd", wd_v), ("wd", blk_wd),
                          ("sc", mmD[par]["sc"]),
                          ("sc", dstage_free[q])] if ktf == 0 else [])
                    P.on("tensor", MM(pp[:, 0:512], yl(q, ktf),
                                      wpanD[:, buf * PAND + ktf * 512:
                                            buf * PAND + (ktf + 1) * 512],
                                      ktf == 0, ktf == KI - 1),
                         wait=w, inc=[("pe", 1)] if ktf == KI - 1 else ())
                pe_v = P.v("pe")
                P.on("scalar", SCPY(
                    S("dstage")[:, q * DM + db * 512:
                                q * DM + (db + 1) * 512],
                    pp[:, 0:512]),
                     wait=[("pe", pe_v)], inc=[("sc", 1)])
                mmD[par]["sc"] = P.v("sc")
            wD_free[buf] = P.v("pe")
        ylhs_free[0] = P.v("pe")
        evD_sc = P.v("sc")
        if tblk < 3:
            pend_wd[(tblk + 1, 0)] = load_wD(0, 0)
            pending = load_ylhs(tblk + 1)
            pend_wd[(tblk + 1, 1)] = load_wD(1, 512)
        for info in pend_p2:
            emit_pass2(info)
        pend_p2 = []
        for q in range(4):
            tt = tblk * 4 + q
            slot = tt % 6
            par = tt % 2
            stg = S("dstage")[:, q * DM:(q + 1) * DM]
            x0 = S("x0t2")[:, par * DM:(par + 1) * DM]
            P.on("sync", DMA(x0, dr["x_in"][tt * 128:(tt + 1) * 128, :]),
                 wait=[("dv", x0b_free[par])], inc=[("ld", 16)])
            ldv = P.v("ld")
            P.on("scalar", SCPY(S("lnt2")[:], x0,
                                bias=S("astats")[:, 2 * tt:2 * tt + 1],
                                scale=S("astats")[:, 2 * tt + 1:2 * tt + 2]),
                 wait=[("ld", ldv), ("dv", lnt2_free[0]),
                       ("ld", lnt2_free[1])],
                 inc=[("sc", 1)])
            P.on("vector", TT(stg, stg, S("lnt2")[:], ALU.add),
                 wait=[("sc", P.v("sc")), ("sc", evD_sc)], inc=[("dv", 1)])
            attn_dv = P.v("dv")
            dvv, cc = ln_stats(stg, par, LN_EPS, pre_wait=[("dv", attn_dv)])
            P.on("scalar", SCPY(S("lnt2")[:], stg, bias=S("rc")[:, cc + 1:cc + 2],
                                scale=S("rc")[:, cc:cc + 1]),
                 wait=[("dv", dvv)], inc=[("sc", 1)])
            stg_sc = P.v("sc")
            dstage_free[q] = stg_sc
            P.on("vector", TT(S("lnt2")[:], S("lnt2")[:], x0, ALU.add),
                 wait=[("sc", stg_sc)], inc=[("dv", 1)])
            x1dv = P.v("dv")
            x0b_free[par] = x1dv
            P.on("sync", DMA(dr["scr_x1"][tt * 128:(tt + 1) * 128, :],
                             S("lnt2")[:]),
                 wait=[("dv", x1dv)], inc=[("ld", 16)])
            x1_ld = P.v("ld")
            dvv, cc = ln_stats(S("lnt2"), par, LN_EPS,
                               pre_wait=[("dv", x1dv)])
            P.on("vector", CPY(S("x1stats")[:, 2 * tt:2 * tt + 2],
                               S("rc")[:, cc:cc + 2]),
                 wait=[("dv", dvv)], inc=[("dv", 1)])
            w1b = S("w1bf")[:, slot * DM:(slot + 1) * DM]
            P.on("scalar", SCPY(w1b, S("lnt2")[:],
                                bias=S("rc")[:, cc + 1:cc + 2],
                                scale=S("rc")[:, cc:cc + 1]),
                 wait=[("dv", dvv), ("ld", x1_ld),
                       ("pe", w1b_free[slot])],
                 inc=[("sc", 1)])
            wbsc = P.v("sc")
            lnt2_free = [P.v("dv"), x1_ld]
            pend_p2.append(dict(tt=tt, slot=slot, wbsc=wbsc, w1b=w1b))
    for info in pend_p2:
        emit_pass2(info)
    flush_fw()
    w1t_ld = P.v("ld")
    D_sc = P.v("sc")
    D_pe = P.v("pe")

    tp = mktap("x1", [T, DM])
    if tp is not None:
        P.on("sync", DMA(tp, dr["scr_x1"]), wait=[("ld", w1t_ld)],
             inc=[("st", 16)])
    tp = mktap("fwt", [8, T])
    if tp is not None:
        P.on("sync", DMA(tp, S("fwt")[:]), wait=[("sc", D_sc)], inc=[("st", 16)])
    tp = mktap("scr_w1t", [128, KD * T], BF16)
    if tp is not None:
        P.on("sync", DMA(tp, dr["scr_w1t"]), wait=[("ld", w1t_ld)],
             inc=[("st", 16)])

    web, w1blk, comb = S("web"), S("w1blk"), S("comb")
    PANE = KD * 512
    wE_free = [D_pe, D_pe]
    mmE = [dict(sc=0), dict(sc=0)]
    silu_free = [0, 0]
    comb_free = [0]
    w1blk_free = [D_pe]
    fwb_free = [0]

    wpanF = S("wpanF")
    PANF = KD * 512
    wF_free = [0, 0]
    mmF = [dict(sc=0, cb=0), dict(sc=0, cb=0)]
    cb_sc_map = {}
    fstage_free = {q: 0 for q in range(4)}
    fyt_free = [0]
    x1b_free = {0: 0, 1: 0}
    cbst_free = [0]

    def load_we(e_i, fcb, buf, fence_pe, fence_st):
        dst = web[:, buf * PANE:(buf + 1) * PANE].rearrange(
            "p (kt f) -> p kt f", kt=KD)
        src = dr["e_w"].rearrange("e (kt p) f -> e p kt f", p=128)[
            e_i][:, :, fcb * 512:(fcb + 1) * 512]
        for pc in range(4):
            k0, k1 = pc * 4, pc * 4 + 4
            P.on("gpsimd", DMA(dst[:, k0:k1], src[:, k0:k1]),
                 wait=[("pe", wE_free[buf]), ("pe", fence_pe),
                       ("st", fence_st)],
                 inc=[("wd", 16)])
        return P.v("wd")

    def load_wF(buf, d0, fence):
        db = d0 // 512
        P.on("sync", DMA(
            wpanF[:, buf * PANF:(buf + 1) * PANF],
            dr["scr_pw"][:, db * KD * 512:(db + 1) * KD * 512]),
             wait=[("pe", wF_free[buf]), ("pe", fence["pe"]),
                   ("dv", fence["dv"]), ("sc", fence["sc"]),
                   ("wd", pw_wd)],
             inc=[("wd", 16)])
        return P.v("wd")

    P.fence_all()
    P.on("sync", DMA(S("pbb")[0:1, :], dr["p_b"].rearrange("(o d) -> o d", o=1)),
         inc=[("ld", 16)])
    P.on("sync", DMA(S("nwb")[0:1, :], dr["n_w"].rearrange("(o d) -> o d", o=1)),
         inc=[("ld", 16)])
    ldv = P.v("ld")
    for nm in ("pbb", "nwb"):
        for b in range(4):
            P.on("tensor", MM(p_bc[:, 0:512], S("ones_f")[:],
                              S(nm)[0:1, b * 512:(b + 1) * 512], True, True),
                 wait=[("ld", ldv), ("dv", fr_bc[0])], inc=[("pe", 1)])
            P.on("vector", CPY(S(nm)[:, b * 512:(b + 1) * 512], p_bc[:, 0:512]),
                 wait=[("pe", P.v("pe"))], inc=[("dv", 1)])
            fr_bc[0] = P.v("dv")
    bb_dv = P.v("dv")

    for H in range(2):
        P.fence_all()
        fE = dict(pe=P.v("pe"), dv=P.v("dv"), sc=P.v("sc"), st=P.v("st"))
        P.on("sync", DMA(w1blk[:].rearrange("p (kt t) -> p kt t", kt=KD),
                         dr["scr_w1t"].rearrange("p (kt t) -> p kt t", kt=KD)[
                             :, :, H * TH:(H + 1) * TH]),
             wait=[("ld", w1t_ld), ("pe", w1blk_free[0]), ("pe", fE["pe"]),
                   ("dv", fE["dv"]), ("sc", fE["sc"]), ("st", fE["st"])],
             inc=[("ld", 16)])
        w1_ld2 = P.v("ld")
        P.on("vector", MEMS(comb[:], 0.0),
             wait=[("sc", comb_free[0])], inc=[("dv", 1)])
        cmdv = P.v("dv")
        wE_free = [fE["pe"], fE["pe"]]
        wd_first = load_we(0, 0, 0, fE["pe"], fE["st"])
        load_we(0, 1, 1, fE["pe"], fE["st"])
        wE_wd = {0: wd_first, 1: P.v("wd")}
        for e_i in range(NE):
            for bb in range(2):
                P.on("tensor", MM(p_bc[:, 0:512],
                                  S("sel8")[:, e_i * 128:(e_i + 1) * 128],
                                  S("fwt")[0:8, H * TH + bb * 512:
                                           H * TH + (bb + 1) * 512], True, True),
                     wait=[("sc", D_sc), ("dv", fr_bc[0]), ("dv", fwb_free[0])],
                     inc=[("pe", 1)])
                P.on("vector", CPY(S("fwb")[:, bb * 512:(bb + 1) * 512],
                                   p_bc[:, 0:512]),
                     wait=[("pe", P.v("pe")), ("sc", fE["sc"]),
                           ("st", fE["st"])],
                     inc=[("dv", 1)])
                fr_bc[0] = P.v("dv")
            fwb_dv = P.v("dv")
            for fcb in range(4):
                buf = fcb % 2
                wd_v = wE_wd[buf]
                for fi in range(4):
                    f = fcb * 4 + fi
                    for b in range(2):
                        par = (f * 2 + b) % 2
                        pp = p_mm[par]
                        for kt in range(KD):
                            w = ([("wd", wd_v), ("ld", w1_ld2),
                                  ("sc", mmE[par]["sc"])] if kt == 0 else [])
                            P.on("tensor", MM(
                                pp[:, 0:512],
                                web[:, buf * PANE + kt * 512 + fi * 128:
                                    buf * PANE + kt * 512 + fi * 128 + 128],
                                w1blk[:, kt * TH + b * 512:
                                      kt * TH + (b + 1) * 512],
                                kt == 0, kt == KD - 1),
                                 wait=w, inc=[("pe", 1)] if kt == KD - 1 else ())
                        pev = P.v("pe")
                        sl = S("silu1") if par == 0 else S("silu2")
                        P.on("scalar", SCPY(sl[:], pp[:, 0:512], func=AF.Silu,
                                            bias=S("ebcol")[:, e_i * 16 + f:
                                                            e_i * 16 + f + 1]),
                             wait=[("pe", pev), ("dv", silu_free[par]),
                                   ("ld", ld0), ("st", fE["st"])],
                             inc=[("sc", 1)])
                        mmE[par]["sc"] = P.v("sc")
                        ssc = P.v("sc")
                        P.on("vector", TT(S("prodt")[:], sl[:],
                                          S("fwb")[:, b * 512:(b + 1) * 512],
                                          ALU.mult),
                             wait=[("sc", ssc), ("dv", fwb_dv), ("dv", cmdv)],
                             inc=[("dv", 1)])
                        silu_free[par] = P.v("dv")
                        cslot = comb[:, f * TH + b * 512:f * TH + (b + 1) * 512]
                        P.on("vector", TT(cslot, cslot, S("prodt")[:], ALU.add),
                             wait=[("dv", P.v("dv"))], inc=[("dv", 1)])
                wE_free[buf] = P.v("pe")
                if fcb < 2:
                    wE_wd[buf] = load_we(e_i, fcb + 2, buf, fE["pe"], fE["st"])
                elif e_i < NE - 1:
                    wE_wd[buf] = load_we(e_i + 1, fcb - 2, buf, fE["pe"],
                                         fE["st"])
            fwb_free[0] = P.v("dv")
            if H == 0 and e_i == 0:
                tpx = mktap("sel8chk", [NE, NE * 128])
                if tpx is not None:
                    P.on("sync", DMA(tpx, S("sel8")[:]), wait=[("dv", P.v("dv"))],
                         inc=[("st", 16)])
                tpx = mktap("combE0", [128, KD * TH])
                if tpx is not None:
                    P.on("sync", DMA(tpx, comb[:]), wait=[("dv", P.v("dv"))],
                         inc=[("st", 16)])
                tpx = mktap("fwb0", [128, TH])
                if tpx is not None:
                    P.on("sync", DMA(tpx, S("fwb")[:]), wait=[("dv", P.v("dv"))],
                         inc=[("st", 16)])
        comb_dv = P.v("dv")
        if H == 0:
            tpx = mktap("comb0", [128, KD * TH])
            if tpx is not None:
                P.on("sync", DMA(tpx, comb[:]), wait=[("dv", comb_dv)],
                     inc=[("st", 16)])
        E_end_pe = P.v("pe")
        w1blk_free[0] = E_end_pe

        P.fence_all()
        fF = dict(pe=P.v("pe"), dv=P.v("dv"), sc=P.v("sc"), st=P.v("st"))
        wF_free = [fF["pe"], fF["pe"]]
        for tb4 in range(2):
            for db in range(4):
                buf = db % 2
                wd_v = load_wF(buf, db * 512, fF)
                for q in range(4):
                    ti = tb4 * 4 + q
                    tt = H * 8 + ti
                    slot = tt % 6
                    par = (q + db) % 2
                    pp = p_mm[par]
                    cb0 = q * KD * 128
                    if db == 0:
                        for kt in range(KD):
                            P.on("scalar", SCPY(
                                S("cbstage")[:, cb0 + kt * 128:
                                             cb0 + (kt + 1) * 128],
                                comb[:, kt * TH + ti * 128:
                                     kt * TH + ti * 128 + 128]),
                                 wait=([("dv", comb_dv),
                                        ("pe", cbst_free[0]),
                                        ("st", fF["st"])] if kt == 0 else []),
                                 inc=[("sc", 1)])
                        mmF[par]["cb"] = P.v("sc")
                        cbsc_by_ti = P.v("sc")
                        cb_sc_map[ti] = cbsc_by_ti
                    for kt in range(KD):
                        w = ([("wd", wd_v), ("sc", cb_sc_map[ti]),
                              ("sc", mmF[par]["sc"]),
                              ("st", fstage_free[q])] if kt == 0 else [])
                        P.on("tensor", MM(
                            pp[:, 0:512],
                            S("cbstage")[:, cb0 + kt * 128:cb0 + (kt + 1) * 128],
                            wpanF[:, buf * PANF + kt * 512:
                                  buf * PANF + (kt + 1) * 512],
                            kt == 0, kt == KD - 1),
                             wait=w, inc=[("pe", 1)] if kt == KD - 1 else ())
                    pev = P.v("pe")
                    P.on("scalar", SCPY(
                        S("fstage")[:, q * DM + db * 512:
                                    q * DM + (db + 1) * 512],
                        pp[:, 0:512]),
                         wait=[("pe", pev), ("dv", fF["dv"])],
                         inc=[("sc", 1)])
                    mmF[par]["sc"] = P.v("sc")
                wF_free[buf] = P.v("pe")
            cbst_free[0] = P.v("pe")
            evF_sc = P.v("sc")
            for q in range(4):
                ti = tb4 * 4 + q
                tt = H * 8 + ti
                slot = tt % 6
                par = tt % 2
                stg = S("fstage")[:, q * DM:(q + 1) * DM]
                x1 = S("x1t")[:, par * DM:(par + 1) * DM]
                P.on("sync", DMA(x1, dr["scr_x1"][tt * 128:(tt + 1) * 128, :]),
                     wait=[("ld", w1t_ld), ("dv", x1b_free[par]),
                           ("pe", fF["pe"]), ("dv", fF["dv"]),
                           ("sc", fF["sc"])],
                     inc=[("ld", 16)])
                ldv = P.v("ld")
                P.on("scalar", SCPY(S("fyt")[:], x1,
                                    bias=S("x1stats")[:, 2 * tt + 1:2 * tt + 2],
                                    scale=S("x1stats")[:, 2 * tt:2 * tt + 1]),
                     wait=[("ld", ldv), ("st", fF["st"]),
                           ("dv", fyt_free[0])],
                     inc=[("sc", 1)])
                w1sc = P.v("sc")
                P.on("vector", TT(stg, stg, S("fyt")[:], ALU.add),
                     wait=[("sc", w1sc), ("sc", evF_sc), ("dv", bb_dv)],
                     inc=[("dv", 1)])
                P.on("vector", TT(stg, stg, S("pbb")[:], ALU.add),
                     wait=[("dv", P.v("dv"))], inc=[("dv", 1)])
                ydv = P.v("dv")
                dvv, cc = ln_stats(stg, par, RMS_EPS, mean_sq=True,
                                   pre_wait=[("dv", ydv)])
                P.on("scalar", SCPY(S("fyt")[:], stg,
                                    scale=S("rc")[:, cc:cc + 1]),
                     wait=[("dv", dvv)], inc=[("sc", 1)])
                P.on("vector", TT(S("fyt")[:], S("fyt")[:], S("nwb")[:],
                                  ALU.mult),
                     wait=[("sc", P.v("sc"))], inc=[("dv", 1)])
                frdv = P.v("dv")
                dvv, cc = ln_stats(S("fyt"), par, LN_EPS,
                                   pre_wait=[("dv", frdv)])
                P.on("scalar", SCPY(stg, S("fyt")[:],
                                    bias=S("rc")[:, cc + 1:cc + 2],
                                    scale=S("rc")[:, cc:cc + 1]),
                     wait=[("dv", dvv)], inc=[("sc", 1)])
                P.on("vector", TT(stg, stg, x1, ALU.add),
                     wait=[("sc", P.v("sc"))], inc=[("dv", 1)])
                odv = P.v("dv")
                x1b_free[par] = odv
                fyt_free[0] = odv
                P.on("sync", DMA(dr["out"][tt * 128:(tt + 1) * 128, :], stg),
                     wait=[("dv", odv)], inc=[("st", 16)])
                fstage_free[q] = P.v("st")


    P.on("sync", None, wait=[("st", P.v("st"))])


def kernel(**inputs):
    xs = np.ascontiguousarray(inputs["L0_attn_x"]).astype(np.float32, copy=False)
    iden = np.eye(128, dtype=np.float32)
    consts = {
        "c_iden_bf": iden.astype(ml_dtypes.bfloat16),
        "c_iden_f": iden,
        "c_tri": np.triu(np.ones((128, 128), np.float32)),
        "c_sel32": np.repeat(np.eye(NH, dtype=np.float32), 128, axis=1),
        "c_sel32b": np.repeat(np.eye(NH, dtype=np.float32), 128,
                              axis=1).astype(ml_dtypes.bfloat16),
        "c_sel8": np.repeat(np.eye(NE, dtype=np.float32), 128, axis=1),
        "c_ones": np.ones((1, 128), np.float32),
    }
    wmap = {
        "w_in": inputs["L0_attn_in_proj_w"], "conv_w": inputs["L0_attn_conv_weight"],
        "conv_b": inputs["L0_attn_conv_bias"], "a_log": inputs["L0_attn_A_log"],
        "dt_bias": inputs["L0_attn_dt_bias"], "d_vec": inputs["L0_attn_D"],
        "w_out": inputs["L0_attn_out_proj_w"], "nsc": inputs["L0_attn_norm_scale"],
        "r_w": inputs["L0_ffn_router_w"], "r_b": inputs["L0_ffn_router_b"],
        "e_w": inputs["L0_ffn_expert_w"], "e_b": inputs["L0_ffn_expert_b"],
        "p_w": inputs["L0_ffn_proj_w"], "p_b": inputs["L0_ffn_proj_b"],
        "n_w": inputs["L0_ffn_norm_w"],
    }
    wmap = {k: np.ascontiguousarray(v, dtype=np.float32) for k, v in wmap.items()}
    nc, _ = build()
    in_maps = []
    for b in range(8):
        m = dict(consts)
        m.update(wmap)
        m["x_in"] = np.ascontiguousarray(xs[b])
        in_maps.append(m)
    res = run_bass_kernel_spmd(nc, in_maps, core_ids=list(range(8)))
    return np.stack([res.results[b]["out"] for b in range(8)], axis=0)


# version 19
# speedup vs baseline: 1.0401x; 1.0064x over previous
import numpy as np
import ml_dtypes
from contextlib import ExitStack
import concourse.bass as bass
import concourse.mybir as mybir
from concourse.bass_utils import run_bass_kernel_spmd

F32 = mybir.dt.float32
BF16 = mybir.dt.bfloat16
AF = mybir.ActivationFunctionType
ALU = mybir.AluOpType

T = 2048
DM = 2048
DI = 4096
NH = 32
QC = 128
NCH = T // QC
NTT = T // 128
KD = DM // 128
KI = DI // 128
PROJ = 8480
NE = 8
TH = T // 2
CLAMP = 30.0
LN_EPS = 1e-5
RMS_EPS = 1e-6
NEG = -1e30


class Prog:
    ENGINES = ("tensor", "vector", "scalar", "sync", "gpsimd")

    def __init__(self, nc, ctx):
        self.nc = nc
        self.q = {e: [] for e in self.ENGINES}
        self.cnt = {}
        self.sems = {}
        self._lw = {e: {} for e in self.ENGINES}
        for s in ("wd", "ld", "tr", "pe", "dv", "sc", "st"):
            self.sems[s] = ctx.enter_context(nc.semaphore("sem_" + s))
            self.cnt[s] = 0

    def v(self, s):
        return self.cnt[s]

    def on(self, eng, fn, wait=(), inc=()):
        w = []
        for s, val in wait:
            if val <= 0 or self._lw[eng].get(s, 0) >= val:
                continue
            self._lw[eng][s] = val
            w.append((s, val))
        for s, a in inc:
            self.cnt[s] += a
        self.q[eng].append((fn, tuple(w), tuple(inc)))

    def fence_all(self):
        vals = [(x, self.cnt[x]) for x in
                ("pe", "dv", "sc", "ld", "tr", "wd", "st")]
        for eng in self.ENGINES:
            self.on(eng, None, wait=vals)

    def flush(self, block):
        for eng in self.ENGINES:
            ops = self.q[eng]
            if not ops:
                continue

            def body(e, ops=ops):
                for fn, waits, incs in ops:
                    for s, val in waits:
                        e.wait_ge(self.sems[s], val)
                    inst = fn(e) if fn is not None else None
                    for s, a in incs:
                        inst.then_inc(self.sems[s], a)

            getattr(block, eng)(body)


class SbufPlan:
    def __init__(self, nc):
        self.nc = nc
        self.items = []

    def add(self, name, shape, dtype, p0, p1):
        nb = int(np.prod(shape[1:])) * mybir.dt.size(dtype)
        self.items.append((name, shape, dtype, p0, p1, nb))

    def alloc(self, base, budget):
        placed, offs = [], {}
        for name, shape, dtype, p0, p1, nb in sorted(self.items,
                                                     key=lambda x: -x[5]):
            nba = (nb + 31) // 32 * 32
            off = 0
            while True:
                hit = None
                for o, e, q0, q1 in placed:
                    if o < off + nba and off < e and not (q1 < p0 or p1 < q0):
                        hit = e
                        break
                if hit is None:
                    break
                off = (hit + 31) // 32 * 32
            placed.append((off, off + nba, p0, p1))
            offs[name] = (off, shape, dtype)
        self.top = max(e for _, e, _, _ in placed)
        assert self.top <= budget, f"sbuf over budget: {self.top} > {budget}"
        return {n: self.nc.alloc_sbuf_tensor_at(n, sh, dt, offset=base + o)
                for n, (o, sh, dt) in offs.items()}


def build(taps=()):
    taps = set(taps)
    nc = bass.Bass("TRN2", target_bir_lowering=False)
    dr = {}

    def din(name, shape, dtype=F32):
        dr[name] = nc.dram_tensor(name, shape, dtype, kind="ExternalInput").ap()

    din("x_in", [T, DM]); din("w_in", [DM, PROJ]); din("conv_w", [4, DI])
    din("conv_b", [DI]); din("a_log", [NH]); din("dt_bias", [NH])
    din("d_vec", [DI]); din("w_out", [DI, DM]); din("nsc", [DM])
    din("r_w", [DM, NE]); din("r_b", [NE]); din("e_w", [NE, DM, DM])
    din("e_b", [NE, DM]); din("p_w", [DM, DM]); din("p_b", [DM]); din("n_w", [DM])
    din("c_iden_bf", [128, 128], BF16); din("c_iden_f", [128, 128])
    din("c_tri", [128, 128]); din("c_sel32", [NH, NH * 128])
    din("c_sel8", [NE, NE * 128]); din("c_ones", [1, 128])
    din("c_sel32b", [NH, NH * 128], BF16)
    dr["out"] = nc.dram_tensor("out", [T, DM], F32, kind="ExternalOutput").ap()
    for nm, shape, dt in [("scr_gate", [128, NH * T], BF16),
                          ("scr_ssm", [128, NH * T], BF16),
                          ("scr_y", [128, NH * T], BF16),
                          ("scr_x1", [T, DM], F32),
                          ("scr_w1t", [128, KD * T], BF16),
                          ("scr_wo", [128, KI * DM], BF16),
                          ("scr_pw", [128, 4 * KD * 512], BF16)]:
        dr[nm] = nc.dram_tensor(nm, shape, dt, kind="Internal").ap()

    tap_specs = {}

    def mktap(name, shape, dtype=F32):
        if name not in taps:
            return None
        ap = nc.dram_tensor("tap_" + name, shape, dtype,
                            kind="ExternalOutput").ap()
        tap_specs[name] = ap
        return ap

    with ExitStack() as ctx:
        pl = SbufPlan(nc)
        A = pl.add
        A("iden_bf", [128, 128], BF16, 0, 5)
        A("iden_f", [128, 128], F32, 0, 5)
        A("tri", [128, 128], F32, 0, 5)
        A("ones_f", [1, 128], F32, 0, 5)
        A("sel32b", [NH, NH * 128], BF16, 0, 2)
        A("sel8", [NE, NE * 128], F32, 0, 5)
        A("cwcol", [128, 128], F32, 0, 1)
        A("cbcol", [128, 32], F32, 0, 1)
        A("dcol", [128, 32], F32, 0, 2)
        A("ebcol", [128, 128], F32, 0, 5)
        A("dtbcol", [32, 1], F32, 0, 1)
        A("acol", [32, 1], F32, 0, 1)
        A("wr_bf", [128, 128], BF16, 0, 3)
        A("rb_b", [128, 8], F32, 0, 3)
        A("astats", [128, 2 * NTT], F32, 0, 3)
        A("x1stats", [128, 2 * NTT], F32, 3, 5)
        A("nscb", [128, DM], F32, 0, 0)
        A("fwt", [8, T], F32, 3, 5)
        A("x0t", [128, 2 * DM], F32, 0, 0)
        A("zrst", [128, DM], F32, 0, 0)
        A("h2s", [128, 2 * DM], BF16, 0, 0)
        A("h2T", [128, KD * T], BF16, 0, 1)
        A("wpanB", [128, 2 * KD * 512], BF16, 0, 1)
        A("convrow", [128, T], F32, 0, 1)
        A("convtmp", [128, T], F32, 0, 1)
        A("ssmrow", [128, 2 * T], BF16, 0, 1)
        A("sgrow", [128, 2 * T], BF16, 1, 1)
        A("bc_bf", [128, T], BF16, 1, 2)
        A("cc_bf", [128, T], BF16, 1, 2)
        A("cumv", [32, T], F32, 1, 2)
        A("cumh", [32, T], BF16, 1, 2)
        A("cumm", [32, T], BF16, 1, 2)
        A("cuml", [32, T], BF16, 1, 2)
        A("dtv", [32, T], F32, 1, 2)
        A("excum", [32, T], BF16, 1, 2)
        A("t32a", [32, T], F32, 1, 1)
        A("t32b", [32, T], F32, 1, 1)
        A("wst2", [32, 128], F32, 2, 2)
        A("acol2", [32, 1], F32, 0, 1)
        A("ywin", [128, 2 * NH * QC], BF16, 2, 2)
        A("gwin", [128, 2 * NH * QC], BF16, 2, 2)
        A("xtch", [128, NH * 128], BF16, 2, 2)
        A("s_st", [128, NH * 128], F32, 2, 2)
        A("sbf", [128, NH * 128], BF16, 2, 2)
        A("cjlt", [128, 32], F32, 2, 2)
        A("wcolt", [128, 32], F32, 2, 2)
        A("cdbc", [128, 32], F32, 2, 2)
        A("cdrow", [1, 32], F32, 2, 2)
        A("cdrow2", [1, 32], F32, 2, 2)
        A("cbtri", [128, 128], F32, 2, 2)
        A("e_t", [128, 2 * 128], F32, 2, 2)
        A("et2", [128, 2 * 128], F32, 2, 2)
        A("dt_t", [128, 2 * 128], F32, 2, 2)
        A("mt_bf", [128, 2 * 128], BF16, 2, 2)
        A("ce_bf", [128, 2 * 128], BF16, 2, 2)
        A("bw_bf", [128, 2 * 128], BF16, 2, 2)
        A("bct_bf", [128, 128], BF16, 2, 2)
        A("lnst", [32, 128], F32, 2, 2)
        A("cjlst", [32, 128], F32, 2, 2)
        A("wst", [32, 128], F32, 2, 2)
        A("ylhs", [128, 4 * KI * 128], BF16, 3, 3)
        A("wpanD", [128, 2 * KI * 512], BF16, 3, 3)
        A("dstage", [128, 4 * DM], F32, 3, 3)
        A("x0t2", [128, 2 * DM], F32, 3, 3)
        A("lnt2", [128, DM], F32, 3, 3)
        A("w1bf", [128, 6 * DM], BF16, 3, 3)
        A("w1tst", [128, 2 * DM], BF16, 3, 3)
        A("m8", [128, 8], F32, 3, 3)
        A("mv8", [128, 8], F32, 3, 3)
        A("mr8", [128, 8], F32, 3, 3)
        A("lg8", [128, 8], F32, 3, 3)
        A("fw8", [128, 16], F32, 3, 3)
        A("comb", [128, KD * TH], F32, 4, 5)
        A("web", [128, 2 * KD * 512], BF16, 4, 4)
        A("w1blk", [128, KD * TH], BF16, 4, 4)
        A("fwb", [128, TH], F32, 4, 4)
        A("silu1", [128, 512], F32, 4, 4)
        A("silu2", [128, 512], F32, 4, 4)
        A("prodt", [128, 512], F32, 4, 4)
        A("cbstage", [128, 4 * KD * 128], BF16, 5, 5)
        A("wpanF", [128, 2 * KD * 512], BF16, 5, 5)
        A("x1t", [128, 2 * DM], F32, 5, 5)
        A("fstage", [128, 4 * DM], F32, 5, 5)
        A("fyt", [128, DM], F32, 5, 5)
        A("pbb", [128, DM], F32, 4, 5)
        A("nwb", [128, DM], F32, 4, 5)
        A("cst4", [128, 4], F32, 0, 5)
        A("bn24", [128, 48], F32, 0, 5)
        A("mv2", [128, 4], F32, 0, 5)
        A("rc", [128, 8], F32, 0, 5)
        base = (nc.sbuf_base + 4096 + 31) // 32 * 32
        sb = pl.alloc(base, nc.sbuf_base + 212000 - base)

        p_mm = [ctx.enter_context(nc.psum_tensor("p_mm0", [128, 512], F32)),
                ctx.enter_context(nc.psum_tensor("p_mm1", [128, 512], F32))]
        p_cb = ctx.enter_context(nc.psum_tensor("p_cb", [128, 512], F32))
        p_bc = ctx.enter_context(nc.psum_tensor("p_bc", [128, 512], F32))
        p_y = ctx.enter_context(nc.psum_tensor("p_y", [128, 512], F32))
        p_sl = ctx.enter_context(nc.psum_tensor("p_sl", [128, 512], F32))
        p_tr = ctx.enter_context(nc.psum_tensor("p_tr", [128, 1024], BF16))
        p_x = ctx.enter_context(nc.psum_tensor("p_x", [128, 512], F32))
        psd = dict(mm=p_mm, cb=p_cb, bc=p_bc, y=p_y, sl=p_sl, tr=p_tr, x=p_x)

        P = Prog(nc, ctx)
        blk = ctx.enter_context(nc.Block())
        _emit(nc, P, sb, psd, dr, mktap)
        P.flush(blk)
    return nc, tap_specs


def _emit(nc, P, sb, psd, dr, mktap):
    S = lambda n: sb[n]
    p_mm, p_cb, p_bc = psd["mm"], psd["cb"], psd["bc"]
    p_y, p_sl, p_tr, p_x = psd["y"], psd["sl"], psd["tr"], psd["x"]

    DMA = lambda dst, src: (lambda e: e.dma_start(dst, src))
    DMAS = lambda dst, src: (lambda e: e.dma_start(
        dst, src, allow_slow_non_contiguous=True))
    DMAT = lambda dst, src: (lambda e: e.dma_start_transpose(dst, src))
    MM = lambda o, l, r, s0, s1: (lambda e: e.matmul(o, l, r, start=s0, stop=s1))
    TRN = lambda o, i, idn: (lambda e: e.transpose(o, i, idn))
    CPY = lambda o, i: (lambda e: e.tensor_copy(o, i))
    MEMS = lambda o, v: (lambda e: e.memset(o, v))

    def SCPY(o, i, bias=0.0, scale=1.0, func=AF.Copy):
        if func == AF.Copy and not (isinstance(bias, float)
                                    and isinstance(scale, float)):
            func = AF.Identity
        return lambda e: e.activation(o, i, func, bias=bias, scale=scale)

    TT = lambda o, a, b, op: (lambda e: e.tensor_tensor(o, a, b, op))

    def TS(o, a, s1, s2, op0, op1=ALU.bypass):
        return lambda e: e.tensor_scalar(o, a, s1, s2, op0=op0, op1=op1)

    def STT(o, a, s, b, op0, op1):
        return lambda e: e.scalar_tensor_tensor(o, a, s, b, op0=op0, op1=op1)

    P.on("vector", MEMS(S("cst4")[:, 0:1], LN_EPS), inc=[("dv", 1)])
    P.on("vector", MEMS(S("cst4")[:, 1:2], RMS_EPS), inc=[("dv", 1)])
    for dst, srcn in [(S("iden_bf")[:], "c_iden_bf"), (S("iden_f")[:], "c_iden_f"),
                      (S("tri")[:], "c_tri"), (S("ones_f")[:], "c_ones"),
                      (S("sel32b")[:], "c_sel32b"), (S("sel8")[:], "c_sel8")]:
        P.on("sync", DMA(dst, dr[srcn]), inc=[("ld", 16)])
    P.on("sync", DMA(S("dtbcol")[:, 0:1], dr["dt_bias"].rearrange("(h o) -> h o", o=1)),
         inc=[("ld", 16)])
    P.on("sync", DMA(S("acol")[:, 0:1], dr["a_log"].rearrange("(h o) -> h o", o=1)),
         inc=[("ld", 16)])
    P.on("sync", DMAS(S("dcol")[:], dr["d_vec"].rearrange("(h p) -> p h", p=128)),
         inc=[("ld", 16)])
    P.on("sync", DMAS(S("cbcol")[:], dr["conv_b"].rearrange("(f p) -> p f", p=128)),
         inc=[("ld", 16)])
    P.on("sync", DMAS(S("ebcol")[:].rearrange("p (e f) -> p e f", f=16),
                     dr["e_b"].rearrange("e (f p) -> p e f", p=128)),
         inc=[("ld", 16)])
    P.on("sync", DMAS(S("cwcol")[:].rearrange("p (k f) -> p k f", f=32),
                      dr["conv_w"].rearrange("k (f p) -> p k f", p=128)),
         inc=[("ld", 16)])
    P.on("gpsimd", DMA(S("wr_bf")[:].rearrange("p (kt e) -> p kt e", kt=KD),
                       dr["r_w"].rearrange("(kt p) e -> p kt e", p=128)),
         inc=[("wd", 16)])
    wr_wd = P.v("wd")
    P.on("sync", DMA(S("nscb")[0:1, :], dr["nsc"].rearrange("(o d) -> o d", o=1)),
         inc=[("ld", 16)])
    P.on("sync", DMA(S("rb_b")[0:1, :], dr["r_b"].rearrange("(o e) -> o e", o=1)),
         inc=[("ld", 16)])
    ld0 = P.v("ld")

    P.on("scalar", SCPY(S("acol2")[:, 0:1], S("acol")[:, 0:1], func=AF.Exp),
         wait=[("ld", ld0)], inc=[("sc", 1)])
    P.on("vector", TS(S("acol")[:, 0:1], S("acol2")[:, 0:1], -1.0, None, ALU.mult),
         wait=[("sc", P.v("sc"))], inc=[("dv", 1)])
    fb_bc = [0]
    for b in range(4):
        P.on("tensor", MM(p_bc[:, 0:512], S("ones_f")[:],
                          S("nscb")[0:1, b * 512:(b + 1) * 512], True, True),
             wait=[("ld", ld0), ("dv", fb_bc[0])], inc=[("pe", 1)])
        P.on("vector", CPY(S("nscb")[:, b * 512:(b + 1) * 512], p_bc[:, 0:512]),
             wait=[("pe", P.v("pe"))], inc=[("dv", 1)])
        fb_bc[0] = P.v("dv")
    P.on("tensor", MM(p_bc[:, 0:8], S("ones_f")[:], S("rb_b")[0:1, :], True, True),
         wait=[("dv", fb_bc[0])], inc=[("pe", 1)])
    P.on("vector", CPY(S("rb_b")[:], p_bc[:, 0:8]),
         wait=[("pe", P.v("pe"))], inc=[("dv", 1)])
    fb_bc[0] = P.v("dv")
    init_dv = P.v("dv")

    def ln_stats(src, par, eps, mean_sq=False, pre_wait=()):
        eps_ap = S("cst4")[:, 0:1] if eps == LN_EPS else S("cst4")[:, 1:2]
        b0 = par * 24
        for i in range(4):
            P.on("vector", lambda e, i=i, src=src, b0=b0: e.bn_stats(
                S("bn24")[:, b0 + i * 6:b0 + (i + 1) * 6],
                src[:, i * 512:(i + 1) * 512]),
                 wait=(list(pre_wait) if i == 0 else []), inc=[("dv", 1)])
        P.on("vector", lambda e, b0=b0, par=par: e.bn_aggr(
            S("mv2")[:, par * 2:par * 2 + 2], S("bn24")[:, b0:b0 + 24]),
             wait=[("dv", P.v("dv"))], inc=[("dv", 1)])
        c = par * 4
        if mean_sq:
            P.on("vector", STT(S("rc")[:, c + 2:c + 3],
                               S("mv2")[:, par * 2:par * 2 + 1],
                               S("mv2")[:, par * 2:par * 2 + 1],
                               S("mv2")[:, par * 2 + 1:par * 2 + 2],
                               ALU.mult, ALU.add),
                 wait=[("dv", P.v("dv"))], inc=[("dv", 1)])
            var_ap = S("rc")[:, c + 2:c + 3]
        else:
            var_ap = S("mv2")[:, par * 2 + 1:par * 2 + 2]
        dvv = P.v("dv")
        P.on("scalar", SCPY(S("rc")[:, c + 3:c + 4], var_ap, bias=eps_ap,
                            func=AF.Sqrt),
             wait=[("dv", dvv)], inc=[("sc", 1)])
        P.on("vector", lambda e, c=c: e.reciprocal(S("rc")[:, c:c + 1],
                                                   S("rc")[:, c + 3:c + 4]),
             wait=[("sc", P.v("sc"))], inc=[("dv", 1)])
        P.on("vector", TS(S("rc")[:, c + 1:c + 2],
                          S("mv2")[:, par * 2:par * 2 + 1],
                          S("rc")[:, c:c + 1], -1.0, ALU.mult, ALU.mult),
             wait=[("dv", P.v("dv"))], inc=[("dv", 1)])
        return P.v("dv"), c

    h2T = S("h2T")
    ptr_free = [0]
    a_free = {0: (0, 0), 1: (0, 0)}
    astat_sc = {0: 0, 1: 0}
    a_done_sc = {}
    prev_xld = [ld0]
    for tt in range(NTT):
        par = tt % 2
        x0 = S("x0t")[:, par * DM:(par + 1) * DM]
        fsc, ftr = a_free[par]
        P.on("sync", DMA(x0, dr["x_in"][tt * 128:(tt + 1) * 128, :]),
             wait=[("ld", prev_xld[0]), ("sc", fsc), ("sc", ftr)],
             inc=[("ld", 16)])
        ldv = P.v("ld")
        prev_xld[0] = ldv
        b0 = par * 24
        for i in range(4):
            P.on("vector", lambda e, i=i, x0=x0, b0=b0: e.bn_stats(
                S("bn24")[:, b0 + i * 6:b0 + (i + 1) * 6],
                x0[:, i * 512:(i + 1) * 512]),
                 wait=([("ld", ldv), ("dv", init_dv),
                        ("sc", astat_sc[par])] if i == 0 else []),
                 inc=[("dv", 1)])
        P.on("vector", lambda e, b0=b0, par=par: e.bn_aggr(
            S("mv2")[:, par * 2:par * 2 + 2], S("bn24")[:, b0:b0 + 24]),
             wait=[("dv", P.v("dv"))], inc=[("dv", 1)])
        agg_dv = P.v("dv")
        c = par * 4
        mean_ap = S("mv2")[:, par * 2:par * 2 + 1]
        var_ap = S("mv2")[:, par * 2 + 1:par * 2 + 2]
        P.on("vector", TS(S("rc")[:, c + 3:c + 4], var_ap,
                          1.0 + RMS_EPS, RMS_EPS * LN_EPS, ALU.mult, ALU.add),
             wait=[("dv", agg_dv)], inc=[("dv", 1)])
        P.on("scalar", SCPY(S("rc")[:, c + 2:c + 3], S("rc")[:, c + 3:c + 4],
                            func=AF.Sqrt),
             wait=[("dv", P.v("dv"))], inc=[("sc", 1)])
        P.on("vector", lambda e, c=c: e.reciprocal(S("rc")[:, c:c + 1],
                                                   S("rc")[:, c + 2:c + 3]),
             wait=[("sc", P.v("sc"))], inc=[("dv", 1)])
        P.on("vector", TS(S("rc")[:, c + 1:c + 2], mean_ap,
                          S("rc")[:, c:c + 1], -1.0, ALU.mult, ALU.mult),
             wait=[("dv", P.v("dv"))], inc=[("dv", 1)])
        main_dv = P.v("dv")
        ac = S("astats")[:, 2 * tt:2 * tt + 1]
        ac1 = S("astats")[:, 2 * tt + 1:2 * tt + 2]
        P.on("scalar", SCPY(ac, var_ap, bias=S("cst4")[:, 0:1], func=AF.Sqrt),
             wait=[("dv", agg_dv)], inc=[("sc", 1)])
        P.on("vector", lambda e, ac=ac, ac1=ac1: e.reciprocal(ac1, ac),
             wait=[("sc", P.v("sc"))], inc=[("dv", 1)])
        P.on("vector", TS(ac, mean_ap, ac1, -1.0, ALU.mult, ALU.mult),
             wait=[("dv", P.v("dv"))], inc=[("dv", 1)])
        astat_sc[par] = P.v("sc")
        P.on("scalar", SCPY(S("zrst")[:], x0, bias=S("rc")[:, c + 1:c + 2],
                            scale=S("rc")[:, c:c + 1]),
             wait=[("dv", main_dv), ("dv", P.v("dv"))], inc=[("sc", 1)])
        z_sc = P.v("sc")
        a_sc = P.v("sc")
        h2sl = S("h2s")[:, par * DM:(par + 1) * DM]
        P.on("vector", TT(h2sl, S("zrst")[:], S("nscb")[:], ALU.mult),
             wait=[("sc", a_sc)], inc=[("dv", 1)])
        hdv = P.v("dv")
        h2T_r = h2T[:].rearrange("p (kt t) -> p kt t", kt=KD)
        for bi in range(2):
            for j in range(8):
                kt = bi * 8 + j
                P.on("tensor", TRN(p_tr[:, j * 128:(j + 1) * 128],
                                   h2sl[:, kt * 128:(kt + 1) * 128],
                                   S("iden_bf")[:]),
                     wait=([("dv", hdv), ("sc", ptr_free[0]), ("ld", ld0)]
                           if j == 0 else []),
                     inc=[("pe", 1)])
            pev = P.v("pe")
            P.on("scalar", lambda e, bi=bi, tt=tt, h2T_r=h2T_r: e.copy(
                h2T_r[:, bi * 8:(bi + 1) * 8, tt * 128:(tt + 1) * 128],
                p_tr[:, 0:1024].rearrange("p (j t) -> p j t", j=8)),
                 wait=[("pe", pev)], inc=[("sc", 1)])
            ptr_free[0] = P.v("sc")
        a_free[par] = (z_sc, P.v("sc"))
        a_done_sc[tt] = P.v("sc")
    A_tr = P.v("sc")

    tp = mktap("h2T", [128, KD * T], BF16)
    if tp is not None:
        P.on("sync", DMA(tp, h2T[:]), wait=[("sc", A_tr)], inc=[("st", 16)])

    wpan = S("wpanB")
    PANB = KD * 512
    wpan_free = [0, 0]
    mm_free = [dict(dv=0, sc=0), dict(dv=0, sc=0)]
    unit = [0]

    def load_panel(buf, f0, ncols):
        dst = wpan[:, buf * PANB:buf * PANB + KD * ncols].rearrange(
            "p (kt f) -> p kt f", kt=KD)
        src = dr["w_in"].rearrange("(kt p) f -> p kt f", p=128)[:, :, f0:f0 + ncols]
        for pc in range(4):
            k0, k1 = pc * 4, pc * 4 + 4
            P.on("gpsimd", DMA(dst[:, k0:k1], src[:, k0:k1]),
                 wait=[("pe", wpan_free[buf])], inc=[("wd", 16)])
        return P.v("wd")

    def acc_tile(buf, ncols, fcol, tb, mrows, wd_v, extra=(), a_sc=None):
        par = unit[0] % 2
        unit[0] += 1
        pp = p_mm[par]
        fr = mm_free[par]
        for kt in range(KD):
            w = ([("wd", wd_v), ("sc", A_tr if a_sc is None else a_sc),
                  ("dv", fr["dv"]),
                  ("sc", fr["sc"])] + list(extra)) if kt == 0 else []
            P.on("tensor", MM(pp[0:mrows, 0:512],
                              wpan[:, buf * PANB + kt * ncols + fcol:
                                   buf * PANB + kt * ncols + fcol + mrows],
                              h2T[:, kt * T + tb * 512:kt * T + (tb + 1) * 512],
                              kt == 0, kt == KD - 1),
                 wait=w, inc=[("pe", 1)] if kt == KD - 1 else ())
        return pp, par, P.v("pe")

    spill_free = {0: 0, 1: 0}
    sgrow_free = {0: 0, 1: 0}
    prev_conv = [0, 0]
    prev_silu = [0, 0]

    for fcb in range(8):
        buf = fcb % 2
        wd_v = load_panel(buf, DI + fcb * 512, 512)
        for ft in range(4):
            frow = fcb * 4 + ft
            rbuf = frow % 2
            cw = S("cwcol")
            last_silu = 0
            for tb in range(4):
                pp, par, pe_v = acc_tile(buf, 512, ft * 128, tb, 128, wd_v,
                                         a_sc=a_done_sc[4 * tb + 3])
                P.on("scalar", SCPY(S("convrow")[:, tb * 512:(tb + 1) * 512],
                                    pp[:, 0:512]),
                     wait=[("pe", pe_v),
                           ("dv", prev_conv[0] if tb == 0 else prev_conv[1])],
                     inc=[("sc", 1)])
                mm_free[par]["sc"] = P.v("sc")
                if tb in (1, 3):
                    half = tb // 2
                    a, b2 = half * 1024, (half + 1) * 1024
                    evac_sc = P.v("sc")
                    P.on("vector", TS(S("convtmp")[:, a:b2],
                                      S("convrow")[:, a:b2],
                                      cw[:, 3 * 32 + frow:3 * 32 + frow + 1],
                                      None, ALU.mult),
                         wait=[("sc", evac_sc), ("sc", prev_silu[half]),
                               ("ld", ld0)],
                         inc=[("dv", 1)])
                    for k, sh in ((2, 1), (1, 2), (0, 3)):
                        lo = max(a, sh)
                        P.on("vector", STT(S("convtmp")[:, lo:b2],
                                           S("convrow")[:, lo - sh:b2 - sh],
                                           cw[:, k * 32 + frow:k * 32 + frow + 1],
                                           S("convtmp")[:, lo:b2],
                                           ALU.mult, ALU.add),
                             wait=[("dv", P.v("dv"))], inc=[("dv", 1)])
                    prev_conv[half] = P.v("dv")
                    P.on("scalar", SCPY(
                        S("ssmrow")[:, rbuf * T + a:rbuf * T + b2],
                        S("convtmp")[:, a:b2], func=AF.Silu,
                        bias=S("cbcol")[:, frow:frow + 1]),
                         wait=[("dv", prev_conv[half]),
                               ("ld", spill_free[rbuf])],
                         inc=[("sc", 1)])
                    prev_silu[half] = P.v("sc")
                    last_silu = P.v("sc")
            P.on("sync", DMA(dr["scr_ssm"][:, frow * T:(frow + 1) * T],
                             S("ssmrow")[:, rbuf * T:(rbuf + 1) * T]),
                 wait=[("sc", last_silu)], inc=[("ld", 16)])
            spill_free[rbuf] = P.v("ld")
        wpan_free[buf] = P.v("pe")
    ssm_ld = P.v("ld")

    buf = 0
    wd_v = load_panel(buf, 2 * DI, 288)
    for tb in range(4):
        pp, par, pe_v = acc_tile(buf, 288, 0, tb, 32, wd_v)
        P.on("vector", CPY(S("dtv")[:, tb * 512:(tb + 1) * 512],
                           pp[0:32, 0:512]),
             wait=[("pe", pe_v)], inc=[("dv", 1)])
        mm_free[par]["dv"] = P.v("dv")
    for name, fcol in (("bc_bf", 32), ("cc_bf", 160)):
        for tb in range(4):
            pp, par, pe_v = acc_tile(buf, 288, fcol, tb, 128, wd_v)
            P.on("scalar", SCPY(S(name)[:, tb * 512:(tb + 1) * 512], pp[:, 0:512]),
                 wait=[("pe", pe_v)], inc=[("sc", 1)])
            mm_free[par]["sc"] = P.v("sc")
    wpan_free[buf] = P.v("pe")
    tail_dv, tail_sc = P.v("dv"), P.v("sc")

    dts = S("dtv")[:, :]
    P.on("vector", TS(dts, dts, S("dtbcol")[:, 0:1], None, ALU.add),
         wait=[("dv", tail_dv), ("ld", ld0)], inc=[("dv", 1)])
    P.on("scalar", SCPY(S("t32a")[:], dts, func=AF.Abs),
         wait=[("dv", P.v("dv"))], inc=[("sc", 1)])
    P.on("scalar", SCPY(S("t32b")[:], S("t32a")[:], scale=-1.0, func=AF.Exp),
         wait=[("sc", P.v("sc"))], inc=[("sc", 1)])
    P.on("scalar", SCPY(S("t32a")[:], S("t32b")[:], bias=1.0, func=AF.Ln),
         wait=[("sc", P.v("sc"))], inc=[("sc", 1)])
    sp_sc = P.v("sc")
    P.on("vector", TS(dts, dts, 0.0, None, ALU.max),
         wait=[("sc", sp_sc)], inc=[("dv", 1)])
    P.on("vector", TT(dts, dts, S("t32a")[:], ALU.add),
         wait=[("dv", P.v("dv"))], inc=[("dv", 1)])
    P.on("vector", TS(S("t32a")[:], dts, S("acol")[:, 0:1], None, ALU.mult),
         wait=[("dv", P.v("dv"))], inc=[("dv", 1)])
    for ch in range(NCH):
        sl = slice(ch * QC, (ch + 1) * QC)
        P.on("vector", lambda e, sl=sl: e.tensor_tensor_scan(
            S("cumv")[:, sl], S("t32a")[:, sl], S("t32a")[:, sl], 0.0,
            op0=ALU.add, op1=ALU.bypass),
             wait=[("dv", P.v("dv"))], inc=[("dv", 1)])
    cum_dv = P.v("dv")
    P.on("scalar", SCPY(S("excum")[:], S("cumv")[:, :], func=AF.Exp),
         wait=[("dv", cum_dv)], inc=[("sc", 1)])
    P.on("vector", CPY(S("cumh")[:], S("cumv")[:]),
         wait=[("dv", cum_dv)], inc=[("dv", 1)])
    P.on("vector", TT(S("t32b")[:], S("cumv")[:], S("cumh")[:], ALU.subtract),
         wait=[("dv", P.v("dv"))], inc=[("dv", 1)])
    P.on("vector", CPY(S("cumm")[:], S("t32b")[:]),
         wait=[("dv", P.v("dv"))], inc=[("dv", 1)])
    P.on("vector", TT(S("t32b")[:], S("t32b")[:], S("cumm")[:], ALU.subtract),
         wait=[("dv", P.v("dv"))], inc=[("dv", 1)])
    P.on("vector", CPY(S("cuml")[:], S("t32b")[:]),
         wait=[("dv", P.v("dv"))], inc=[("dv", 1)])
    prep_dv, prep_sc = P.v("dv"), P.v("sc")

    for fcb in range(8):
        buf = fcb % 2
        wd_v = load_panel(buf, fcb * 512, 512)
        for ft in range(4):
            frow = fcb * 4 + ft
            rbuf = frow % 2
            grow = S("sgrow")[:, rbuf * T:(rbuf + 1) * T]
            for tb in range(4):
                pp, par, pe_v = acc_tile(buf, 512, ft * 128, tb, 128, wd_v)
                P.on("scalar", SCPY(grow[:, tb * 512:(tb + 1) * 512],
                                    pp[:, 0:512], func=AF.Silu),
                     wait=[("pe", pe_v), ("ld", sgrow_free[rbuf])],
                     inc=[("sc", 1)])
                mm_free[par]["sc"] = P.v("sc")
            P.on("sync", DMA(dr["scr_gate"][:, frow * T:(frow + 1) * T], grow),
                 wait=[("sc", P.v("sc"))], inc=[("ld", 16)])
            sgrow_free[rbuf] = P.v("ld")
        wpan_free[buf] = P.v("pe")
    gate_ld = P.v("ld")

    tp = mktap("cum", [32, T])
    if tp is not None:
        P.on("sync", DMA(tp, S("cumv")[:]),
             wait=[("sc", prep_sc), ("dv", prep_dv)], inc=[("st", 16)])
    tp = mktap("dt", [32, T])
    if tp is not None:
        P.on("sync", DMA(tp, S("dtv")[:]),
             wait=[("sc", prep_sc), ("dv", prep_dv)], inc=[("st", 16)])
    tp = mktap("bc", [128, T], BF16)
    if tp is not None:
        P.on("sync", DMA(tp, S("bc_bf")[:]), wait=[("sc", tail_sc)],
             inc=[("st", 16)])
    tp = mktap("cc", [128, T], BF16)
    if tp is not None:
        P.on("sync", DMA(tp, S("cc_bf")[:]), wait=[("sc", tail_sc)],
             inc=[("st", 16)])
    tp = mktap("scr_ssm", [128, NH * T], BF16)
    if tp is not None:
        P.on("sync", DMA(tp, dr["scr_ssm"]), wait=[("ld", ssm_ld)],
             inc=[("st", 16)])
    tp = mktap("scr_gate", [128, NH * T], BF16)
    if tp is not None:
        P.on("sync", DMA(tp, dr["scr_gate"]), wait=[("ld", gate_ld)],
             inc=[("st", 16)])

    ywin, xtch = S("ywin"), S("xtch")
    sst, sbf = S("s_st"), S("sbf")
    P.fence_all()
    for db in range(4):
        for kh in range(2):
            k0, k1 = kh * 16, kh * 16 + 16
            P.on("gpsimd", DMA(
                dr["scr_wo"].rearrange("p (db kt f) -> p db kt f",
                                       db=4, kt=KI)[:, db, k0:k1, :],
                dr["w_out"].rearrange("(kt p) f -> p kt f", p=128)[
                    :, k0:k1, db * 512:(db + 1) * 512]),
                 inc=[("wd", 16)])
    wo_wd = P.v("wd")
    for db in range(4):
        P.on("gpsimd", DMA(
            dr["scr_pw"].rearrange("p (db kt f) -> p db kt f",
                                   db=4, kt=KD)[:, db],
            dr["p_w"].rearrange("(kt p) f -> p kt f", p=128)[
                :, :, db * 512:(db + 1) * 512]),
             inc=[("wd", 16)])
    pw_wd = P.v("wd")
    P.on("vector", MEMS(sst[:], 0.0), inc=[("dv", 1)])
    P.on("vector", MEMS(sbf[:], 0.0), inc=[("dv", 1)])
    sinit_dv = P.v("dv")

    pair_ld = {}
    chunk_done_dv = {-1: 0}
    yspill_ld = {}
    gwin_ld = {}
    gmul_dv = [0]

    def load_pair(cp):
        gate_free = chunk_done_dv.get(2 * cp - 1, 0)
        xw = [("ld", ssm_ld), ("ld", yspill_ld.get(cp - 1, 0)),
              ("dv", gate_free)]
        for pc in range(4):
            h0, h1 = pc * 8, pc * 8 + 8
            for ci in range(2):
                src = dr["scr_ssm"].rearrange("p (h t) -> p h t", h=NH)[
                    :, h0:h1, (2 * cp + ci) * QC:(2 * cp + ci + 1) * QC]
                dst = ywin[:].rearrange("p (c h i) -> p c h i", c=2, h=NH)[
                    :, ci, h0:h1]
                P.on("sync", DMA(dst, src), wait=xw, inc=[("ld", 16)])
        pair_ld[cp] = P.v("ld")
        gw = [("ld", gate_ld), ("dv", gmul_dv[0])]
        for pc in range(4):
            h0, h1 = pc * 8, pc * 8 + 8
            for ci in range(2):
                src = dr["scr_gate"].rearrange("p (h t) -> p h t", h=NH)[
                    :, h0:h1, (2 * cp + ci) * QC:(2 * cp + ci + 1) * QC]
                dst = S("gwin")[:].rearrange("p (c h i) -> p c h i",
                                             c=2, h=NH)[:, ci, h0:h1]
                P.on("sync", DMA(dst, src), wait=gw, inc=[("ld", 16)])
        gwin_ld[cp] = P.v("ld")

    def yw(cpar, h):
        return ywin[:, cpar * NH * QC + h * QC:cpar * NH * QC + (h + 1) * QC]

    load_pair(0)
    fr_bc = fb_bc
    fr_cb = [0]
    fr_y = [0]
    fr_sl = [0]
    fr_x = [0]
    sbf_rd = {h: 0 for h in range(NH)}
    xt_last_sc = [A_tr * 0]
    prev = dict(s1=0, t3=0, t4=0, t5=0, d1=0, d4=0, d7=0, s2=0)
    prevu = [dict(s1=0, t3=0, t4=0, t5=0, d2=0, d4=0),
             dict(s1=0, t3=0, t4=0, t5=0, d2=0, d4=0)]
    chunk_rd = dict(cjlt=0, wcolt=0, cdbc=0, cbtri=0, bct=0)

    for c in range(NCH):
        cpar = c % 2
        csl = slice(c * QC, (c + 1) * QC)
        if c % 2 == 0 and c > 0:
            load_pair(c // 2)
        ld_need = pair_ld[c // 2]
        P.on("tensor", MM(p_cb[:, 0:128], S("bc_bf")[:, csl],
                          S("cc_bf")[:, csl], True, True),
             wait=[("sc", tail_sc), ("dv", fr_cb[0])], inc=[("pe", 1)])
        P.on("vector", TT(S("cbtri")[:], p_cb[:, 0:128], S("tri")[:], ALU.mult),
             wait=[("pe", P.v("pe")), ("dv", chunk_rd["cbtri"]), ("ld", ld0)],
             inc=[("dv", 1)])
        fr_cb[0] = P.v("dv")
        cbtri_dv = P.v("dv")
        P.on("tensor", TRN(p_tr[:, 0:128], S("bc_bf")[:, csl], S("iden_bf")[:]),
             wait=[("sc", xt_last_sc[0])], inc=[("pe", 1)])
        P.on("scalar", SCPY(S("bct_bf")[:], p_tr[:, 0:128]),
             wait=[("pe", P.v("pe")), ("dv", chunk_rd["bct"])], inc=[("sc", 1)])
        bct_sc = P.v("sc")
        P.on("scalar", SCPY(S("lnst")[:], S("dtv")[:, csl], func=AF.Ln),
             wait=[("sc", prep_sc), ("dv", prep_dv)], inc=[("sc", 1)])
        P.on("vector", TT(S("cjlst")[:], S("cumv")[:, csl], S("lnst")[:],
                          ALU.subtract),
             wait=[("sc", P.v("sc"))], inc=[("dv", 1)])
        cjl_dv = P.v("dv")
        P.on("vector", TS(S("wst")[:], S("cumv")[:, csl],
                          S("cumv")[:, c * QC + QC - 1:c * QC + QC],
                          None, ALU.subtract),
             inc=[("dv", 1)])
        P.on("scalar", SCPY(S("wst2")[:], S("wst")[:], scale=-1.0, func=AF.Exp),
             wait=[("dv", P.v("dv"))], inc=[("sc", 1)])
        P.on("vector", TT(S("wst")[:], S("wst2")[:], S("dtv")[:, csl],
                          ALU.mult),
             wait=[("sc", P.v("sc"))], inc=[("dv", 1)])
        w_dv = P.v("dv")
        P.on("tensor", TRN(p_x[:, 0:32], S("cjlst")[:], S("iden_f")[0:32, 0:32]),
             wait=[("dv", cjl_dv), ("dv", fr_x[0])], inc=[("pe", 1)])
        P.on("tensor", TRN(p_x[:, 32:64], S("wst")[:], S("iden_f")[0:32, 0:32]),
             wait=[("dv", w_dv)], inc=[("pe", 1)])
        P.on("tensor", TRN(p_x[0:1, 64:96],
                           S("cumv")[:, c * QC + QC - 1:c * QC + QC],
                           S("iden_f")[0:32, 0:32]),
             wait=[("dv", cum_dv)], inc=[("pe", 1)])
        pev = P.v("pe")
        P.on("vector", CPY(S("cjlt")[:], p_x[:, 0:32]),
             wait=[("pe", pev), ("dv", chunk_rd["cjlt"])], inc=[("dv", 1)])
        P.on("vector", CPY(S("wcolt")[:], p_x[:, 32:64]),
             wait=[("dv", chunk_rd["wcolt"])], inc=[("dv", 1)])
        P.on("scalar", SCPY(S("cdrow")[:], p_x[0:1, 64:96]),
             wait=[("pe", pev)], inc=[("sc", 1)])
        P.on("scalar", SCPY(S("cdrow2")[:], S("cdrow")[:], func=AF.Exp),
             wait=[("sc", P.v("sc"))], inc=[("sc", 1)])
        fr_x[0] = P.v("dv")
        cjl_ev = P.v("dv")
        P.on("tensor", MM(p_bc[:, 0:32], S("ones_f")[:], S("cdrow2")[:],
                          True, True),
             wait=[("sc", P.v("sc")), ("dv", fr_bc[0])], inc=[("pe", 1)])
        P.on("vector", CPY(S("cdbc")[:], p_bc[:, 0:32]),
             wait=[("pe", P.v("pe")), ("dv", chunk_rd["cdbc"])], inc=[("dv", 1)])
        fr_bc[0] = P.v("dv")
        shared_dv = P.v("dv")

        xt_sc = {}
        for bi in range(4):
            for j in range(8):
                h = bi * 8 + j
                P.on("tensor", TRN(p_tr[:, j * 128:(j + 1) * 128],
                                   yw(cpar, h), S("iden_bf")[:]),
                     wait=([("ld", ld_need), ("sc", bct_sc),
                            ("pe", prev["t5"])] if (bi == 0 and j == 0)
                           else ([("sc", xt_sc[bi - 1])] if j == 0 else [])),
                     inc=[("pe", 1)])
            pev = P.v("pe")
            P.on("scalar", SCPY(xtch[:, bi * 1024:(bi + 1) * 1024],
                                p_tr[:, 0:1024]),
                 wait=[("pe", pev)], inc=[("sc", 1)])
            xt_sc[bi] = P.v("sc")
        xt_last_sc[0] = xt_sc[3]

        stash = {}

        def stage1(h):
            up = h % 2
            usl = slice(up * 128, (up + 1) * 128)
            e_t = S("e_t")[:, usl]
            et2 = S("et2")[:, usl]
            dt_t = S("dt_t")[:, usl]
            mt_bf = S("mt_bf")[:, usl]
            ce_bf = S("ce_bf")[:, usl]
            bw_bf = S("bw_bf")[:, usl]
            prev2 = prevu[up]
            sel = S("sel32b")[:, h * 128:(h + 1) * 128]
            P.on("tensor", MM(p_bc[:, 0:128], sel, S("cumh")[:, csl],
                              True, False),
                 wait=[("dv", fr_bc[0]), ("dv", prep_dv)], inc=[("pe", 1)])
            P.on("tensor", MM(p_bc[:, 0:128], sel, S("cumm")[:, csl],
                              False, False), inc=[("pe", 1)])
            P.on("tensor", MM(p_bc[:, 0:128], sel, S("cuml")[:, csl],
                              False, True), inc=[("pe", 1)])
            t1 = P.v("pe")
            P.on("tensor", MM(p_cb[:, 0:128], sel, S("excum")[:, csl],
                              True, True),
                 wait=[("dv", fr_cb[0]), ("sc", prep_sc)], inc=[("pe", 1)])
            t2 = P.v("pe")
            P.on("vector", TS(e_t, p_bc[:, 0:128],
                              S("cjlt")[:, h:h + 1], CLAMP,
                              ALU.subtract, ALU.min),
                 wait=[("pe", t1), ("dv", cjl_ev), ("sc", prev2["s1"])],
                 inc=[("dv", 1)])
            fr_bc[0] = P.v("dv")
            d1 = P.v("dv")
            P.on("scalar", SCPY(et2, e_t, func=AF.Exp),
                 wait=[("dv", d1), ("dv", prev2["d2"])], inc=[("sc", 1)])
            s1 = P.v("sc")
            P.on("vector", TT(ce_bf, p_cb[:, 0:128], S("cc_bf")[:, csl],
                              ALU.mult),
                 wait=[("pe", t2), ("pe", prev2["t4"])], inc=[("dv", 1)])
            fr_cb[0] = P.v("dv")
            d3 = P.v("dv")
            P.on("vector", TS(bw_bf, S("bct_bf")[:],
                              S("wcolt")[:, h:h + 1], None, ALU.mult),
                 wait=[("sc", bct_sc), ("dv", shared_dv), ("pe", prev2["t5"])],
                 inc=[("dv", 1)])
            d6 = P.v("dv")
            P.on("vector", TT(mt_bf, et2, S("cbtri")[:], ALU.mult),
                 wait=[("sc", s1), ("dv", cbtri_dv), ("pe", prev2["t3"])],
                 inc=[("dv", 1)])
            d2 = P.v("dv")
            P.on("scalar", SCPY(dt_t, yw(cpar, h),
                                scale=S("dcol")[:, h:h + 1]),
                 wait=[("ld", ld_need), ("ld", ld0), ("dv", prev2["d4"])],
                 inc=[("sc", 1)])
            s2 = P.v("sc")
            stash[h] = dict(mt=mt_bf, ce=ce_bf, bw=bw_bf, dt_t=dt_t,
                            d2=d2, d3=d3, d6=d6, s1=s1, s2=s2, up=up)

        def stage2(h):
            st2 = stash.pop(h)
            up = st2["up"]
            xt = xtch[:, h * 128:(h + 1) * 128]
            P.on("tensor", MM(p_y[:, 0:128], xt, st2["mt"], True, False),
                 wait=[("dv", st2["d2"]), ("sc", xt_sc[h // 8]),
                       ("dv", fr_y[0])],
                 inc=[("pe", 1)])
            t3 = P.v("pe")
            P.on("tensor", MM(p_y[:, 0:128], sbf[:, h * 128:(h + 1) * 128],
                              st2["ce"], False, True),
                 wait=[("dv", st2["d3"]), ("sc", sbf_rd[h]),
                       ("dv", sinit_dv)],
                 inc=[("pe", 1)])
            t4 = P.v("pe")
            P.on("tensor", MM(p_sl[:, 0:128], st2["bw"], xt, True, True),
                 wait=[("dv", st2["d6"]), ("dv", fr_sl[0])], inc=[("pe", 1)])
            t5 = P.v("pe")
            P.on("vector", TT(yw(cpar, h), st2["dt_t"], p_y[:, 0:128],
                              ALU.add),
                 wait=[("pe", t4), ("sc", st2["s2"])], inc=[("dv", 1)])
            fr_y[0] = P.v("dv")
            d4 = P.v("dv")
            P.on("vector", STT(sst[:, h * 128:(h + 1) * 128],
                               sst[:, h * 128:(h + 1) * 128],
                               S("cdbc")[:, h:h + 1], p_sl[:, 0:128],
                               ALU.mult, ALU.add),
                 wait=[("pe", t5), ("dv", shared_dv), ("sc", sbf_rd[h])],
                 inc=[("dv", 1)])
            fr_sl[0] = P.v("dv")
            d7 = P.v("dv")
            P.on("scalar", SCPY(sbf[:, h * 128:(h + 1) * 128],
                                sst[:, h * 128:(h + 1) * 128]),
                 wait=[("dv", d7)], inc=[("sc", 1)])
            sbf_rd[h] = P.v("sc")
            prev["t5"] = t5
            prevu[up] = dict(s1=st2["s1"], t3=t3, t4=t4, t5=t5,
                             d2=st2["d2"], d4=d4)

        for h in range(NH):
            stage1(h)
            if h > 0:
                stage2(h - 1)
        stage2(NH - 1)
        chunk_done_dv[c] = P.v("dv")
        chunk_rd = dict(cjlt=P.v("dv"), wcolt=P.v("dv"), cdbc=P.v("dv"),
                        cbtri=P.v("dv"), bct=P.v("dv"))
        if c % 2 == 1:
            P.on("vector", TT(ywin[:], ywin[:], S("gwin")[:], ALU.mult),
                 wait=[("dv", P.v("dv")), ("ld", gwin_ld[c // 2])],
                 inc=[("dv", 1)])
            gmul_dv[0] = P.v("dv")
            for ci in range(2):
                P.on("sync", DMA(
                    dr["scr_y"][:, (c - 1 + ci) * NH * QC:(c + ci) * NH * QC],
                    ywin[:, ci * NH * QC:(ci + 1) * NH * QC]),
                     wait=[("dv", P.v("dv"))], inc=[("ld", 16)])
            yspill_ld[c // 2] = P.v("ld")
    y_ld = P.v("ld")
    C_pe = P.v("pe")

    tp = mktap("scr_y", [128, NH * T], BF16)
    if tp is not None:
        P.on("sync", DMA(tp, dr["scr_y"]), wait=[("ld", y_ld)], inc=[("st", 16)])

    P.fence_all()
    wpanD = S("wpanD")
    PAND = KI * 512
    wD_free = [C_pe, C_pe]
    ylhs = S("ylhs")
    ylhs_free = [C_pe]
    mmD = [dict(sc=0), dict(sc=0)]
    x0b_free = {0: 0, 1: 0}
    ptrD_free = [P.v("sc")]
    w1_free = {0: (0, 0, 0), 1: (0, 0, 0)}
    dstage_free = {q: 0 for q in range(4)}
    fwt_sc = 0

    def load_ylhs(tblk, slot_pe=None):
        for sidx in range(4):
            tt = tblk * 4 + sidx
            fpe = slot_pe[sidx] if slot_pe is not None else ylhs_free[0]
            P.on("sync", DMA(
                ylhs[:, sidx * KI * 128:(sidx + 1) * KI * 128],
                dr["scr_y"][:, tt * NH * QC:(tt + 1) * NH * QC]),
                 wait=[("ld", y_ld), ("pe", fpe)],
                 inc=[("wd", 16)])
        return P.v("wd")

    def yl(slot, ktf):
        return ylhs[:, slot * KI * 128 + ktf * 128:slot * KI * 128 + (ktf + 1) * 128]

    def load_wD(buf, d0):
        db = d0 // 512
        P.on("sync", DMA(
            wpanD[:, buf * PAND:(buf + 1) * PAND],
            dr["scr_wo"][:, db * KI * 512:(db + 1) * KI * 512]),
             wait=[("pe", wD_free[buf]), ("wd", wo_wd)], inc=[("wd", 16)])
        return P.v("wd")

    lnt2_free = [0, 0]
    w1b_free = {s: 0 for s in range(6)}
    pend_p2 = []
    pend_fw = []
    fwt_scL = [0]

    def flush_fw():
        for tt, fwdv, fpar in pend_fw:
            P.on("tensor", TRN(p_x[0:8, 128:256],
                               S("fw8")[:, fpar * 8:fpar * 8 + 8],
                               S("iden_f")[:]),
                 wait=[("dv", fwdv), ("sc", fwt_scL[0])], inc=[("pe", 1)])
            P.on("scalar", SCPY(S("fwt")[0:8, tt * 128:(tt + 1) * 128],
                                p_x[0:8, 128:256]),
                 wait=[("pe", P.v("pe"))], inc=[("sc", 1)])
            fwt_scL[0] = P.v("sc")
        del pend_fw[:]

    def emit_pass2(info):
        tt, slot, wbsc, w1b = info["tt"], info["slot"], info["wbsc"], info["w1b"]
        par = tt % 2
        flush_fw()
        w1t = S("w1tst")[:, par * DM:(par + 1) * DM]
        w1t_r = w1t.rearrange("p (kt i) -> p kt i", kt=KD)
        ftr, fld, fpe = w1_free[par]
        for bi in range(2):
            for j in range(8):
                kt = bi * 8 + j
                P.on("tensor", TRN(p_tr[:, j * 128:(j + 1) * 128],
                                   w1b[:, kt * 128:(kt + 1) * 128],
                                   S("iden_bf")[:]),
                     wait=([("sc", wbsc), ("sc", ptrD_free[0]),
                            ("ld", fld), ("pe", fpe)] if j == 0 else []),
                     inc=[("pe", 1)])
            pev = P.v("pe")
            P.on("scalar", lambda e, bi=bi, w1t_r=w1t_r: e.copy(
                w1t_r[:, bi * 8:(bi + 1) * 8, :],
                p_tr[:, 0:1024].rearrange("p (j t) -> p j t", j=8)),
                 wait=[("pe", pev)], inc=[("sc", 1)])
            ptrD_free[0] = P.v("sc")
        w1b_free[slot] = P.v("pe")
        trv = P.v("sc")
        P.on("sync", DMA(
            dr["scr_w1t"].rearrange("p (kt t) -> p kt t", kt=KD)[
                :, :, tt * 128:(tt + 1) * 128],
            w1t.rearrange("p (kt i) -> p kt i", kt=KD)),
             wait=[("sc", trv)], inc=[("ld", 16)])
        for kt in range(KD):
            P.on("tensor", MM(p_x[0:128, 0:8],
                              w1t[:, kt * 128:(kt + 1) * 128],
                              S("wr_bf")[:, kt * 8:(kt + 1) * 8],
                              kt == 0, kt == KD - 1),
                 wait=([("sc", trv), ("wd", wr_wd), ("dv", fr_x[0])]
                       if kt == 0 else []),
                 inc=[("pe", 1)] if kt == KD - 1 else ())
        pev = P.v("pe")
        w1_free[par] = (trv, P.v("ld"), pev)
        P.on("vector", TT(S("lg8")[:], p_x[:, 0:8], S("rb_b")[:], ALU.add),
             wait=[("pe", pev)], inc=[("dv", 1)])
        fr_x[0] = P.v("dv")
        P.on("vector", lambda e: e.max(out=S("m8")[:], in_=S("lg8")[:]),
             wait=[("dv", P.v("dv"))], inc=[("dv", 1)])
        P.on("vector", MEMS(S("mv8")[:], NEG), inc=[("dv", 1)])
        P.on("vector", CPY(S("mv8")[:, 0:2], S("m8")[:, 0:2]),
             wait=[("dv", P.v("dv"))], inc=[("dv", 1)])
        P.on("vector", lambda e: e.match_replace(
            out=S("mr8")[:], in_to_replace=S("mv8")[:],
            in_values=S("lg8")[:], imm_value=NEG),
             wait=[("dv", P.v("dv"))], inc=[("dv", 1)])
        P.on("vector", TT(S("rc")[:, 6:7], S("m8")[:, 0:1], S("m8")[:, 1:2],
                          ALU.subtract),
             wait=[("dv", P.v("dv"))], inc=[("dv", 1)])
        dvv = P.v("dv")
        P.on("scalar", SCPY(S("rc")[:, 7:8], S("rc")[:, 6:7],
                            func=AF.Sigmoid),
             wait=[("dv", dvv)], inc=[("sc", 1)])
        P.on("scalar", SCPY(S("rc")[:, 6:7], S("rc")[:, 6:7], scale=-1.0,
                            func=AF.Sigmoid),
             wait=[("sc", P.v("sc"))], inc=[("sc", 1)])
        rsc = P.v("sc")
        P.on("vector", TT(S("rc")[:, 7:8], S("rc")[:, 7:8], S("rc")[:, 6:7],
                          ALU.subtract),
             wait=[("sc", rsc)], inc=[("dv", 1)])
        fpar = tt % 2
        fw8 = S("fw8")[:, fpar * 8:fpar * 8 + 8]
        P.on("vector", TS(fw8, S("lg8")[:], S("m8")[:, 0:1], None,
                          ALU.is_equal),
             inc=[("dv", 1)])
        P.on("vector", TS(fw8, fw8, S("rc")[:, 7:8],
                          S("rc")[:, 6:7], ALU.mult, ALU.add),
             wait=[("dv", P.v("dv"))], inc=[("dv", 1)])
        P.on("vector", TT(S("mv8")[:], S("mr8")[:], S("lg8")[:],
                          ALU.not_equal),
             inc=[("dv", 1)])
        P.on("vector", TT(fw8, fw8, S("mv8")[:], ALU.mult),
             wait=[("dv", P.v("dv"))], inc=[("dv", 1)])
        pend_fw.append((tt, P.v("dv"), fpar))

    slotq_pe = {}
    pending = load_ylhs(0)
    pend_wd = {}
    for tblk in range(4):
        blk_wd = pending
        for db in range(4):
            buf = db % 2
            wd_v = pend_wd.pop((tblk, db), None)
            if wd_v is None:
                wd_v = load_wD(buf, db * 512)
            for q in range(4):
                tt = tblk * 4 + q
                par = (q + db) % 2
                pp = p_mm[par]
                for ktf in range(KI):
                    w = ([("wd", wd_v), ("wd", blk_wd),
                          ("sc", mmD[par]["sc"]),
                          ("sc", dstage_free[q])] if ktf == 0 else [])
                    P.on("tensor", MM(pp[:, 0:512], yl(q, ktf),
                                      wpanD[:, buf * PAND + ktf * 512:
                                            buf * PAND + (ktf + 1) * 512],
                                      ktf == 0, ktf == KI - 1),
                         wait=w, inc=[("pe", 1)] if ktf == KI - 1 else ())
                pe_v = P.v("pe")
                if db == 3:
                    slotq_pe[q] = pe_v
                P.on("scalar", SCPY(
                    S("dstage")[:, q * DM + db * 512:
                                q * DM + (db + 1) * 512],
                    pp[:, 0:512]),
                     wait=[("pe", pe_v)], inc=[("sc", 1)])
                mmD[par]["sc"] = P.v("sc")
            wD_free[buf] = P.v("pe")
        ylhs_free[0] = P.v("pe")
        evD_sc = P.v("sc")
        if tblk < 3:
            pend_wd[(tblk + 1, 0)] = load_wD(0, 0)
            pending = load_ylhs(tblk + 1, slot_pe=dict(slotq_pe))
            pend_wd[(tblk + 1, 1)] = load_wD(1, 512)
        for info in pend_p2:
            emit_pass2(info)
        pend_p2 = []
        for q in range(4):
            tt = tblk * 4 + q
            slot = tt % 6
            par = tt % 2
            stg = S("dstage")[:, q * DM:(q + 1) * DM]
            x0 = S("x0t2")[:, par * DM:(par + 1) * DM]
            P.on("sync", DMA(x0, dr["x_in"][tt * 128:(tt + 1) * 128, :]),
                 wait=[("dv", x0b_free[par])], inc=[("ld", 16)])
            ldv = P.v("ld")
            P.on("scalar", SCPY(S("lnt2")[:], x0,
                                bias=S("astats")[:, 2 * tt:2 * tt + 1],
                                scale=S("astats")[:, 2 * tt + 1:2 * tt + 2]),
                 wait=[("ld", ldv), ("dv", lnt2_free[0]),
                       ("ld", lnt2_free[1])],
                 inc=[("sc", 1)])
            P.on("vector", TT(stg, stg, S("lnt2")[:], ALU.add),
                 wait=[("sc", P.v("sc")), ("sc", evD_sc)], inc=[("dv", 1)])
            attn_dv = P.v("dv")
            dvv, cc = ln_stats(stg, par, LN_EPS, pre_wait=[("dv", attn_dv)])
            P.on("scalar", SCPY(S("lnt2")[:], stg, bias=S("rc")[:, cc + 1:cc + 2],
                                scale=S("rc")[:, cc:cc + 1]),
                 wait=[("dv", dvv)], inc=[("sc", 1)])
            stg_sc = P.v("sc")
            dstage_free[q] = stg_sc
            P.on("vector", TT(S("lnt2")[:], S("lnt2")[:], x0, ALU.add),
                 wait=[("sc", stg_sc)], inc=[("dv", 1)])
            x1dv = P.v("dv")
            x0b_free[par] = x1dv
            P.on("sync", DMA(dr["scr_x1"][tt * 128:(tt + 1) * 128, :],
                             S("lnt2")[:]),
                 wait=[("dv", x1dv)], inc=[("ld", 16)])
            x1_ld = P.v("ld")
            dvv, cc = ln_stats(S("lnt2"), par, LN_EPS,
                               pre_wait=[("dv", x1dv)])
            P.on("vector", CPY(S("x1stats")[:, 2 * tt:2 * tt + 2],
                               S("rc")[:, cc:cc + 2]),
                 wait=[("dv", dvv)], inc=[("dv", 1)])
            w1b = S("w1bf")[:, slot * DM:(slot + 1) * DM]
            P.on("scalar", SCPY(w1b, S("lnt2")[:],
                                bias=S("rc")[:, cc + 1:cc + 2],
                                scale=S("rc")[:, cc:cc + 1]),
                 wait=[("dv", dvv), ("ld", x1_ld),
                       ("pe", w1b_free[slot])],
                 inc=[("sc", 1)])
            wbsc = P.v("sc")
            lnt2_free = [P.v("dv"), x1_ld]
            pend_p2.append(dict(tt=tt, slot=slot, wbsc=wbsc, w1b=w1b))
    for info in pend_p2:
        emit_pass2(info)
    flush_fw()
    w1t_ld = P.v("ld")
    D_sc = P.v("sc")
    D_pe = P.v("pe")

    tp = mktap("x1", [T, DM])
    if tp is not None:
        P.on("sync", DMA(tp, dr["scr_x1"]), wait=[("ld", w1t_ld)],
             inc=[("st", 16)])
    tp = mktap("fwt", [8, T])
    if tp is not None:
        P.on("sync", DMA(tp, S("fwt")[:]), wait=[("sc", D_sc)], inc=[("st", 16)])
    tp = mktap("scr_w1t", [128, KD * T], BF16)
    if tp is not None:
        P.on("sync", DMA(tp, dr["scr_w1t"]), wait=[("ld", w1t_ld)],
             inc=[("st", 16)])

    web, w1blk, comb = S("web"), S("w1blk"), S("comb")
    PANE = KD * 512
    wE_free = [D_pe, D_pe]
    mmE = [dict(sc=0), dict(sc=0)]
    silu_free = [0, 0]
    comb_free = [0]
    w1blk_free = [D_pe]
    fwb_free = [0]

    wpanF = S("wpanF")
    PANF = KD * 512
    wF_free = [0, 0]
    mmF = [dict(sc=0, cb=0), dict(sc=0, cb=0)]
    cb_sc_map = {}
    fstage_free = {q: 0 for q in range(4)}
    fyt_free = [0]
    x1b_free = {0: 0, 1: 0}
    cbst_free = [0]

    def load_we(e_i, fcb, buf, fence_pe, fence_st):
        dst = web[:, buf * PANE:(buf + 1) * PANE].rearrange(
            "p (kt f) -> p kt f", kt=KD)
        src = dr["e_w"].rearrange("e (kt p) f -> e p kt f", p=128)[
            e_i][:, :, fcb * 512:(fcb + 1) * 512]
        for pc in range(4):
            k0, k1 = pc * 4, pc * 4 + 4
            P.on("gpsimd", DMA(dst[:, k0:k1], src[:, k0:k1]),
                 wait=[("pe", wE_free[buf]), ("pe", fence_pe),
                       ("st", fence_st)],
                 inc=[("wd", 16)])
        return P.v("wd")

    def load_wF(buf, d0, fence):
        db = d0 // 512
        P.on("sync", DMA(
            wpanF[:, buf * PANF:(buf + 1) * PANF],
            dr["scr_pw"][:, db * KD * 512:(db + 1) * KD * 512]),
             wait=[("pe", wF_free[buf]), ("pe", fence["pe"]),
                   ("dv", fence["dv"]), ("sc", fence["sc"]),
                   ("wd", pw_wd)],
             inc=[("wd", 16)])
        return P.v("wd")

    P.fence_all()
    P.on("sync", DMA(S("pbb")[0:1, :], dr["p_b"].rearrange("(o d) -> o d", o=1)),
         inc=[("ld", 16)])
    P.on("sync", DMA(S("nwb")[0:1, :], dr["n_w"].rearrange("(o d) -> o d", o=1)),
         inc=[("ld", 16)])
    ldv = P.v("ld")
    for nm in ("pbb", "nwb"):
        for b in range(4):
            P.on("tensor", MM(p_bc[:, 0:512], S("ones_f")[:],
                              S(nm)[0:1, b * 512:(b + 1) * 512], True, True),
                 wait=[("ld", ldv), ("dv", fr_bc[0])], inc=[("pe", 1)])
            P.on("vector", CPY(S(nm)[:, b * 512:(b + 1) * 512], p_bc[:, 0:512]),
                 wait=[("pe", P.v("pe"))], inc=[("dv", 1)])
            fr_bc[0] = P.v("dv")
    bb_dv = P.v("dv")

    for H in range(2):
        P.fence_all()
        fE = dict(pe=P.v("pe"), dv=P.v("dv"), sc=P.v("sc"), st=P.v("st"))
        P.on("sync", DMA(w1blk[:].rearrange("p (kt t) -> p kt t", kt=KD),
                         dr["scr_w1t"].rearrange("p (kt t) -> p kt t", kt=KD)[
                             :, :, H * TH:(H + 1) * TH]),
             wait=[("ld", w1t_ld), ("pe", w1blk_free[0]), ("pe", fE["pe"]),
                   ("dv", fE["dv"]), ("sc", fE["sc"]), ("st", fE["st"])],
             inc=[("ld", 16)])
        w1_ld2 = P.v("ld")
        P.on("vector", MEMS(comb[:], 0.0),
             wait=[("sc", comb_free[0])], inc=[("dv", 1)])
        cmdv = P.v("dv")
        wE_free = [fE["pe"], fE["pe"]]
        wd_first = load_we(0, 0, 0, fE["pe"], fE["st"])
        load_we(0, 1, 1, fE["pe"], fE["st"])
        wE_wd = {0: wd_first, 1: P.v("wd")}
        for e_i in range(NE):
            for bb in range(2):
                P.on("tensor", MM(p_bc[:, 0:512],
                                  S("sel8")[:, e_i * 128:(e_i + 1) * 128],
                                  S("fwt")[0:8, H * TH + bb * 512:
                                           H * TH + (bb + 1) * 512], True, True),
                     wait=[("sc", D_sc), ("dv", fr_bc[0]), ("dv", fwb_free[0])],
                     inc=[("pe", 1)])
                P.on("vector", CPY(S("fwb")[:, bb * 512:(bb + 1) * 512],
                                   p_bc[:, 0:512]),
                     wait=[("pe", P.v("pe")), ("sc", fE["sc"]),
                           ("st", fE["st"])],
                     inc=[("dv", 1)])
                fr_bc[0] = P.v("dv")
            fwb_dv = P.v("dv")
            for fcb in range(4):
                buf = fcb % 2
                wd_v = wE_wd[buf]
                for fi in range(4):
                    f = fcb * 4 + fi
                    for b in range(2):
                        par = (f * 2 + b) % 2
                        pp = p_mm[par]
                        for kt in range(KD):
                            w = ([("wd", wd_v), ("ld", w1_ld2),
                                  ("sc", mmE[par]["sc"])] if kt == 0 else [])
                            P.on("tensor", MM(
                                pp[:, 0:512],
                                web[:, buf * PANE + kt * 512 + fi * 128:
                                    buf * PANE + kt * 512 + fi * 128 + 128],
                                w1blk[:, kt * TH + b * 512:
                                      kt * TH + (b + 1) * 512],
                                kt == 0, kt == KD - 1),
                                 wait=w, inc=[("pe", 1)] if kt == KD - 1 else ())
                        pev = P.v("pe")
                        sl = S("silu1") if par == 0 else S("silu2")
                        P.on("scalar", SCPY(sl[:], pp[:, 0:512], func=AF.Silu,
                                            bias=S("ebcol")[:, e_i * 16 + f:
                                                            e_i * 16 + f + 1]),
                             wait=[("pe", pev), ("dv", silu_free[par]),
                                   ("ld", ld0), ("st", fE["st"])],
                             inc=[("sc", 1)])
                        mmE[par]["sc"] = P.v("sc")
                        ssc = P.v("sc")
                        P.on("vector", TT(S("prodt")[:], sl[:],
                                          S("fwb")[:, b * 512:(b + 1) * 512],
                                          ALU.mult),
                             wait=[("sc", ssc), ("dv", fwb_dv), ("dv", cmdv)],
                             inc=[("dv", 1)])
                        silu_free[par] = P.v("dv")
                        cslot = comb[:, f * TH + b * 512:f * TH + (b + 1) * 512]
                        P.on("vector", TT(cslot, cslot, S("prodt")[:], ALU.add),
                             wait=[("dv", P.v("dv"))], inc=[("dv", 1)])
                wE_free[buf] = P.v("pe")
                if fcb < 2:
                    wE_wd[buf] = load_we(e_i, fcb + 2, buf, fE["pe"], fE["st"])
                elif e_i < NE - 1:
                    wE_wd[buf] = load_we(e_i + 1, fcb - 2, buf, fE["pe"],
                                         fE["st"])
            fwb_free[0] = P.v("dv")
            if H == 0 and e_i == 0:
                tpx = mktap("sel8chk", [NE, NE * 128])
                if tpx is not None:
                    P.on("sync", DMA(tpx, S("sel8")[:]), wait=[("dv", P.v("dv"))],
                         inc=[("st", 16)])
                tpx = mktap("combE0", [128, KD * TH])
                if tpx is not None:
                    P.on("sync", DMA(tpx, comb[:]), wait=[("dv", P.v("dv"))],
                         inc=[("st", 16)])
                tpx = mktap("fwb0", [128, TH])
                if tpx is not None:
                    P.on("sync", DMA(tpx, S("fwb")[:]), wait=[("dv", P.v("dv"))],
                         inc=[("st", 16)])
        comb_dv = P.v("dv")
        if H == 0:
            tpx = mktap("comb0", [128, KD * TH])
            if tpx is not None:
                P.on("sync", DMA(tpx, comb[:]), wait=[("dv", comb_dv)],
                     inc=[("st", 16)])
        E_end_pe = P.v("pe")
        w1blk_free[0] = E_end_pe

        P.fence_all()
        fF = dict(pe=P.v("pe"), dv=P.v("dv"), sc=P.v("sc"), st=P.v("st"))
        wF_free = [fF["pe"], fF["pe"]]
        for tb4 in range(2):
            for db in range(4):
                buf = db % 2
                wd_v = load_wF(buf, db * 512, fF)
                for q in range(4):
                    ti = tb4 * 4 + q
                    tt = H * 8 + ti
                    slot = tt % 6
                    par = (q + db) % 2
                    pp = p_mm[par]
                    cb0 = q * KD * 128
                    if db == 0:
                        for kt in range(KD):
                            P.on("scalar", SCPY(
                                S("cbstage")[:, cb0 + kt * 128:
                                             cb0 + (kt + 1) * 128],
                                comb[:, kt * TH + ti * 128:
                                     kt * TH + ti * 128 + 128]),
                                 wait=([("dv", comb_dv),
                                        ("pe", cbst_free[0]),
                                        ("st", fF["st"])] if kt == 0 else []),
                                 inc=[("sc", 1)])
                        mmF[par]["cb"] = P.v("sc")
                        cbsc_by_ti = P.v("sc")
                        cb_sc_map[ti] = cbsc_by_ti
                    for kt in range(KD):
                        w = ([("wd", wd_v), ("sc", cb_sc_map[ti]),
                              ("sc", mmF[par]["sc"]),
                              ("st", fstage_free[q])] if kt == 0 else [])
                        P.on("tensor", MM(
                            pp[:, 0:512],
                            S("cbstage")[:, cb0 + kt * 128:cb0 + (kt + 1) * 128],
                            wpanF[:, buf * PANF + kt * 512:
                                  buf * PANF + (kt + 1) * 512],
                            kt == 0, kt == KD - 1),
                             wait=w, inc=[("pe", 1)] if kt == KD - 1 else ())
                    pev = P.v("pe")
                    P.on("scalar", SCPY(
                        S("fstage")[:, q * DM + db * 512:
                                    q * DM + (db + 1) * 512],
                        pp[:, 0:512]),
                         wait=[("pe", pev), ("dv", fF["dv"])],
                         inc=[("sc", 1)])
                    mmF[par]["sc"] = P.v("sc")
                wF_free[buf] = P.v("pe")
            cbst_free[0] = P.v("pe")
            evF_sc = P.v("sc")
            for q in range(4):
                ti = tb4 * 4 + q
                tt = H * 8 + ti
                slot = tt % 6
                par = tt % 2
                stg = S("fstage")[:, q * DM:(q + 1) * DM]
                x1 = S("x1t")[:, par * DM:(par + 1) * DM]
                P.on("sync", DMA(x1, dr["scr_x1"][tt * 128:(tt + 1) * 128, :]),
                     wait=[("ld", w1t_ld), ("dv", x1b_free[par]),
                           ("pe", fF["pe"]), ("dv", fF["dv"]),
                           ("sc", fF["sc"])],
                     inc=[("ld", 16)])
                ldv = P.v("ld")
                P.on("scalar", SCPY(S("fyt")[:], x1,
                                    bias=S("x1stats")[:, 2 * tt + 1:2 * tt + 2],
                                    scale=S("x1stats")[:, 2 * tt:2 * tt + 1]),
                     wait=[("ld", ldv), ("st", fF["st"]),
                           ("dv", fyt_free[0])],
                     inc=[("sc", 1)])
                w1sc = P.v("sc")
                P.on("vector", TT(stg, stg, S("fyt")[:], ALU.add),
                     wait=[("sc", w1sc), ("sc", evF_sc), ("dv", bb_dv)],
                     inc=[("dv", 1)])
                P.on("vector", TT(stg, stg, S("pbb")[:], ALU.add),
                     wait=[("dv", P.v("dv"))], inc=[("dv", 1)])
                ydv = P.v("dv")
                dvv, cc = ln_stats(stg, par, RMS_EPS, mean_sq=True,
                                   pre_wait=[("dv", ydv)])
                P.on("scalar", SCPY(S("fyt")[:], stg,
                                    scale=S("rc")[:, cc:cc + 1]),
                     wait=[("dv", dvv)], inc=[("sc", 1)])
                P.on("vector", TT(S("fyt")[:], S("fyt")[:], S("nwb")[:],
                                  ALU.mult),
                     wait=[("sc", P.v("sc"))], inc=[("dv", 1)])
                frdv = P.v("dv")
                dvv, cc = ln_stats(S("fyt"), par, LN_EPS,
                                   pre_wait=[("dv", frdv)])
                P.on("scalar", SCPY(stg, S("fyt")[:],
                                    bias=S("rc")[:, cc + 1:cc + 2],
                                    scale=S("rc")[:, cc:cc + 1]),
                     wait=[("dv", dvv)], inc=[("sc", 1)])
                P.on("vector", TT(stg, stg, x1, ALU.add),
                     wait=[("sc", P.v("sc"))], inc=[("dv", 1)])
                odv = P.v("dv")
                x1b_free[par] = odv
                fyt_free[0] = odv
                P.on("sync", DMA(dr["out"][tt * 128:(tt + 1) * 128, :], stg),
                     wait=[("dv", odv)], inc=[("st", 16)])
                fstage_free[q] = P.v("st")


    P.on("sync", None, wait=[("st", P.v("st"))])


def kernel(**inputs):
    xs = np.ascontiguousarray(inputs["L0_attn_x"]).astype(np.float32, copy=False)
    iden = np.eye(128, dtype=np.float32)
    consts = {
        "c_iden_bf": iden.astype(ml_dtypes.bfloat16),
        "c_iden_f": iden,
        "c_tri": np.triu(np.ones((128, 128), np.float32)),
        "c_sel32": np.repeat(np.eye(NH, dtype=np.float32), 128, axis=1),
        "c_sel32b": np.repeat(np.eye(NH, dtype=np.float32), 128,
                              axis=1).astype(ml_dtypes.bfloat16),
        "c_sel8": np.repeat(np.eye(NE, dtype=np.float32), 128, axis=1),
        "c_ones": np.ones((1, 128), np.float32),
    }
    wmap = {
        "w_in": inputs["L0_attn_in_proj_w"], "conv_w": inputs["L0_attn_conv_weight"],
        "conv_b": inputs["L0_attn_conv_bias"], "a_log": inputs["L0_attn_A_log"],
        "dt_bias": inputs["L0_attn_dt_bias"], "d_vec": inputs["L0_attn_D"],
        "w_out": inputs["L0_attn_out_proj_w"], "nsc": inputs["L0_attn_norm_scale"],
        "r_w": inputs["L0_ffn_router_w"], "r_b": inputs["L0_ffn_router_b"],
        "e_w": inputs["L0_ffn_expert_w"], "e_b": inputs["L0_ffn_expert_b"],
        "p_w": inputs["L0_ffn_proj_w"], "p_b": inputs["L0_ffn_proj_b"],
        "n_w": inputs["L0_ffn_norm_w"],
    }
    wmap = {k: np.ascontiguousarray(v, dtype=np.float32) for k, v in wmap.items()}
    nc, _ = build()
    in_maps = []
    for b in range(8):
        m = dict(consts)
        m.update(wmap)
        m["x_in"] = np.ascontiguousarray(xs[b])
        in_maps.append(m)
    res = run_bass_kernel_spmd(nc, in_maps, core_ids=list(range(8)))
    return np.stack([res.results[b]["out"] for b in range(8)], axis=0)


# version 20 (final)
# speedup vs baseline: 1.0594x; 1.0186x over previous
import numpy as np
import ml_dtypes
from contextlib import ExitStack
import concourse.bass as bass
import concourse.mybir as mybir
from concourse.bass_utils import run_bass_kernel_spmd

F32 = mybir.dt.float32
BF16 = mybir.dt.bfloat16
AF = mybir.ActivationFunctionType
ALU = mybir.AluOpType

T = 2048
DM = 2048
DI = 4096
NH = 32
QC = 128
NCH = T // QC
NTT = T // 128
KD = DM // 128
KI = DI // 128
PROJ = 8480
NE = 8
TH = T // 2
CLAMP = 30.0
LN_EPS = 1e-5
RMS_EPS = 1e-6
NEG = -1e30


class Prog:
    ENGINES = ("tensor", "vector", "scalar", "sync", "gpsimd")

    def __init__(self, nc, ctx):
        self.nc = nc
        self.q = {e: [] for e in self.ENGINES}
        self.cnt = {}
        self.sems = {}
        self._lw = {e: {} for e in self.ENGINES}
        for s in ("wd", "ld", "tr", "pe", "dv", "sc", "st"):
            self.sems[s] = ctx.enter_context(nc.semaphore("sem_" + s))
            self.cnt[s] = 0

    def v(self, s):
        return self.cnt[s]

    def on(self, eng, fn, wait=(), inc=()):
        w = []
        for s, val in wait:
            if val <= 0 or self._lw[eng].get(s, 0) >= val:
                continue
            self._lw[eng][s] = val
            w.append((s, val))
        for s, a in inc:
            self.cnt[s] += a
        self.q[eng].append((fn, tuple(w), tuple(inc)))

    def fence_all(self):
        vals = [(x, self.cnt[x]) for x in
                ("pe", "dv", "sc", "ld", "tr", "wd", "st")]
        for eng in self.ENGINES:
            self.on(eng, None, wait=vals)

    def flush(self, block):
        for eng in self.ENGINES:
            ops = self.q[eng]
            if not ops:
                continue

            def body(e, ops=ops):
                for fn, waits, incs in ops:
                    for s, val in waits:
                        e.wait_ge(self.sems[s], val)
                    inst = fn(e) if fn is not None else None
                    for s, a in incs:
                        inst.then_inc(self.sems[s], a)

            getattr(block, eng)(body)


class SbufPlan:
    def __init__(self, nc):
        self.nc = nc
        self.items = []

    def add(self, name, shape, dtype, p0, p1):
        nb = int(np.prod(shape[1:])) * mybir.dt.size(dtype)
        self.items.append((name, shape, dtype, p0, p1, nb))

    def alloc(self, base, budget):
        placed, offs = [], {}
        for name, shape, dtype, p0, p1, nb in sorted(self.items,
                                                     key=lambda x: -x[5]):
            nba = (nb + 31) // 32 * 32
            off = 0
            while True:
                hit = None
                for o, e, q0, q1 in placed:
                    if o < off + nba and off < e and not (q1 < p0 or p1 < q0):
                        hit = e
                        break
                if hit is None:
                    break
                off = (hit + 31) // 32 * 32
            placed.append((off, off + nba, p0, p1))
            offs[name] = (off, shape, dtype)
        self.top = max(e for _, e, _, _ in placed)
        assert self.top <= budget, f"sbuf over budget: {self.top} > {budget}"
        return {n: self.nc.alloc_sbuf_tensor_at(n, sh, dt, offset=base + o)
                for n, (o, sh, dt) in offs.items()}


def build(taps=()):
    taps = set(taps)
    nc = bass.Bass("TRN2", target_bir_lowering=False)
    dr = {}

    def din(name, shape, dtype=F32):
        dr[name] = nc.dram_tensor(name, shape, dtype, kind="ExternalInput").ap()

    din("x_in", [T, DM]); din("w_in", [DM, PROJ]); din("conv_w", [4, DI])
    din("conv_b", [DI]); din("a_log", [NH]); din("dt_bias", [NH])
    din("d_vec", [DI]); din("w_out", [DI, DM]); din("nsc", [DM])
    din("r_w", [DM, NE]); din("r_b", [NE]); din("e_w", [NE, DM, DM])
    din("e_b", [NE, DM]); din("p_w", [DM, DM]); din("p_b", [DM]); din("n_w", [DM])
    din("c_iden_bf", [128, 128], BF16); din("c_iden_f", [128, 128])
    din("c_tri", [128, 128]); din("c_sel32", [NH, NH * 128])
    din("c_sel8", [NE, NE * 128]); din("c_ones", [1, 128])
    din("c_sel32b", [NH, NH * 128], BF16)
    dr["out"] = nc.dram_tensor("out", [T, DM], F32, kind="ExternalOutput").ap()
    for nm, shape, dt in [("scr_gate", [128, NH * T], BF16),
                          ("scr_ssm", [128, NH * T], BF16),
                          ("scr_y", [128, NH * T], BF16),
                          ("scr_x1", [T, DM], F32),
                          ("scr_w1t", [128, KD * T], BF16),
                          ("scr_wo", [128, KI * DM], BF16),
                          ("scr_pw", [128, 4 * KD * 512], BF16)]:
        dr[nm] = nc.dram_tensor(nm, shape, dt, kind="Internal").ap()

    tap_specs = {}

    def mktap(name, shape, dtype=F32):
        if name not in taps:
            return None
        ap = nc.dram_tensor("tap_" + name, shape, dtype,
                            kind="ExternalOutput").ap()
        tap_specs[name] = ap
        return ap

    with ExitStack() as ctx:
        pl = SbufPlan(nc)
        A = pl.add
        A("iden_bf", [128, 128], BF16, 0, 5)
        A("iden_f", [128, 128], F32, 0, 5)
        A("tri", [128, 128], F32, 0, 5)
        A("ones_f", [1, 128], F32, 0, 5)
        A("sel32b", [NH, NH * 128], BF16, 0, 2)
        A("sel8", [NE, NE * 128], F32, 0, 5)
        A("cwcol", [128, 128], F32, 0, 1)
        A("cbcol", [128, 32], F32, 0, 1)
        A("dcol", [128, 32], F32, 0, 2)
        A("ebcol", [128, 128], F32, 0, 5)
        A("dtbcol", [32, 1], F32, 0, 1)
        A("acol", [32, 1], F32, 0, 1)
        A("wr_bf", [128, 128], BF16, 0, 3)
        A("rb_b", [128, 8], F32, 0, 3)
        A("astats", [128, 2 * NTT], F32, 0, 3)
        A("x1stats", [128, 2 * NTT], F32, 3, 5)
        A("nscb", [128, DM], F32, 0, 0)
        A("fwt", [8, T], F32, 3, 5)
        A("x0t", [128, 2 * DM], F32, 0, 0)
        A("zrst", [128, DM], F32, 0, 0)
        A("h2s", [128, 2 * DM], BF16, 0, 0)
        A("h2T", [128, KD * T], BF16, 0, 1)
        A("wpanB", [128, 2 * KD * 512], BF16, 0, 1)
        A("convrow", [128, T], F32, 0, 1)
        A("convtmp", [128, T], F32, 0, 1)
        A("ssmrow", [128, 2 * T], BF16, 0, 1)
        A("sgrow", [128, 2 * T], BF16, 1, 1)
        A("bc_bf", [128, T], BF16, 1, 2)
        A("cc_bf", [128, T], BF16, 1, 2)
        A("cumv", [32, T], F32, 1, 2)
        A("cumh", [32, T], BF16, 1, 2)
        A("cumm", [32, T], BF16, 1, 2)
        A("cuml", [32, T], BF16, 1, 2)
        A("dtv", [32, T], F32, 1, 2)
        A("excum", [32, T], BF16, 1, 2)
        A("t32a", [32, T], F32, 1, 1)
        A("t32b", [32, T], F32, 1, 1)
        A("wst2", [32, 128], F32, 2, 2)
        A("acol2", [32, 1], F32, 0, 1)
        A("ywin", [128, 2 * NH * QC], BF16, 2, 2)
        A("gwin", [128, 2 * NH * QC], BF16, 2, 2)
        A("xtch", [128, NH * 128], BF16, 2, 2)
        A("s_st", [128, NH * 128], F32, 2, 2)
        A("sbf", [128, NH * 128], BF16, 2, 2)
        A("cjlt", [128, 32], F32, 2, 2)
        A("wcolt", [128, 32], F32, 2, 2)
        A("cdbc", [128, 32], F32, 2, 2)
        A("cdrow", [1, 32], F32, 2, 2)
        A("cdrow2", [1, 32], F32, 2, 2)
        A("cbtri", [128, 128], F32, 2, 2)
        A("e_t", [128, 2 * 128], F32, 2, 2)
        A("et2", [128, 2 * 128], F32, 2, 2)
        A("dt_t", [128, 2 * 128], F32, 2, 2)
        A("mt_bf", [128, 2 * 128], BF16, 2, 2)
        A("ce_bf", [128, 2 * 128], BF16, 2, 2)
        A("bw_bf", [128, 2 * 128], BF16, 2, 2)
        A("bct_bf", [128, 128], BF16, 2, 2)
        A("lnst", [32, 128], F32, 2, 2)
        A("cjlst", [32, 128], F32, 2, 2)
        A("wst", [32, 128], F32, 2, 2)
        A("ylhs", [128, 4 * KI * 128], BF16, 3, 3)
        A("wpanD", [128, 2 * KI * 512], BF16, 3, 3)
        A("dstage", [128, 4 * DM], F32, 3, 3)
        A("x0t2", [128, 2 * DM], F32, 3, 3)
        A("lnt2", [128, DM], F32, 3, 3)
        A("w1bf", [128, 6 * DM], BF16, 3, 3)
        A("w1tst", [128, 2 * DM], BF16, 3, 3)
        A("m8", [128, 8], F32, 3, 3)
        A("mv8", [128, 8], F32, 3, 3)
        A("mr8", [128, 8], F32, 3, 3)
        A("lg8", [128, 8], F32, 3, 3)
        A("fw8", [128, 16], F32, 3, 3)
        A("comb", [128, KD * TH], F32, 4, 5)
        A("web", [128, 2 * KD * 512], BF16, 4, 4)
        A("w1blk", [128, KD * TH], BF16, 4, 4)
        A("fwb", [128, TH], F32, 4, 4)
        A("silu1", [128, 512], F32, 4, 4)
        A("silu2", [128, 512], F32, 4, 4)
        A("prodt", [128, 512], F32, 4, 4)
        A("cbstage", [128, 4 * KD * 128], BF16, 5, 5)
        A("wpanF", [128, 2 * KD * 512], BF16, 5, 5)
        A("x1t", [128, 2 * DM], F32, 5, 5)
        A("fstage", [128, 4 * DM], F32, 5, 5)
        A("fyt", [128, DM], F32, 5, 5)
        A("pbb", [128, DM], F32, 4, 5)
        A("nwb", [128, DM], F32, 4, 5)
        A("cst4", [128, 4], F32, 0, 5)
        A("bn24", [128, 48], F32, 0, 5)
        A("mv2", [128, 4], F32, 0, 5)
        A("rc", [128, 8], F32, 0, 5)
        base = (nc.sbuf_base + 4096 + 31) // 32 * 32
        sb = pl.alloc(base, nc.sbuf_base + 212000 - base)

        p_mm = [ctx.enter_context(nc.psum_tensor("p_mm0", [128, 512], F32)),
                ctx.enter_context(nc.psum_tensor("p_mm1", [128, 512], F32))]
        p_cb = ctx.enter_context(nc.psum_tensor("p_cb", [128, 512], F32))
        p_bc = ctx.enter_context(nc.psum_tensor("p_bc", [128, 512], F32))
        p_y = ctx.enter_context(nc.psum_tensor("p_y", [128, 512], F32))
        p_sl = ctx.enter_context(nc.psum_tensor("p_sl", [128, 512], F32))
        p_tr = ctx.enter_context(nc.psum_tensor("p_tr", [128, 1024], BF16))
        p_x = ctx.enter_context(nc.psum_tensor("p_x", [128, 512], F32))
        psd = dict(mm=p_mm, cb=p_cb, bc=p_bc, y=p_y, sl=p_sl, tr=p_tr, x=p_x)

        P = Prog(nc, ctx)
        blk = ctx.enter_context(nc.Block())
        _emit(nc, P, sb, psd, dr, mktap)
        P.flush(blk)
    return nc, tap_specs


def _emit(nc, P, sb, psd, dr, mktap):
    S = lambda n: sb[n]
    p_mm, p_cb, p_bc = psd["mm"], psd["cb"], psd["bc"]
    p_y, p_sl, p_tr, p_x = psd["y"], psd["sl"], psd["tr"], psd["x"]

    DMA = lambda dst, src: (lambda e: e.dma_start(dst, src))
    DMAS = lambda dst, src: (lambda e: e.dma_start(
        dst, src, allow_slow_non_contiguous=True))
    DMAT = lambda dst, src: (lambda e: e.dma_start_transpose(dst, src))
    MM = lambda o, l, r, s0, s1: (lambda e: e.matmul(o, l, r, start=s0, stop=s1))
    TRN = lambda o, i, idn: (lambda e: e.transpose(o, i, idn))
    CPY = lambda o, i: (lambda e: e.tensor_copy(o, i))
    MEMS = lambda o, v: (lambda e: e.memset(o, v))

    def SCPY(o, i, bias=0.0, scale=1.0, func=AF.Copy):
        if func == AF.Copy and not (isinstance(bias, float)
                                    and isinstance(scale, float)):
            func = AF.Identity
        return lambda e: e.activation(o, i, func, bias=bias, scale=scale)

    TT = lambda o, a, b, op: (lambda e: e.tensor_tensor(o, a, b, op))

    def TS(o, a, s1, s2, op0, op1=ALU.bypass):
        return lambda e: e.tensor_scalar(o, a, s1, s2, op0=op0, op1=op1)

    def STT(o, a, s, b, op0, op1):
        return lambda e: e.scalar_tensor_tensor(o, a, s, b, op0=op0, op1=op1)

    P.on("vector", MEMS(S("cst4")[:, 0:1], LN_EPS), inc=[("dv", 1)])
    P.on("vector", MEMS(S("cst4")[:, 1:2], RMS_EPS), inc=[("dv", 1)])
    for dst, srcn in [(S("iden_bf")[:], "c_iden_bf"), (S("iden_f")[:], "c_iden_f"),
                      (S("tri")[:], "c_tri"), (S("ones_f")[:], "c_ones"),
                      (S("sel32b")[:], "c_sel32b"), (S("sel8")[:], "c_sel8")]:
        P.on("sync", DMA(dst, dr[srcn]), inc=[("ld", 16)])
    P.on("sync", DMA(S("dtbcol")[:, 0:1], dr["dt_bias"].rearrange("(h o) -> h o", o=1)),
         inc=[("ld", 16)])
    P.on("sync", DMA(S("acol")[:, 0:1], dr["a_log"].rearrange("(h o) -> h o", o=1)),
         inc=[("ld", 16)])
    P.on("sync", DMAS(S("dcol")[:], dr["d_vec"].rearrange("(h p) -> p h", p=128)),
         inc=[("ld", 16)])
    P.on("sync", DMAS(S("cbcol")[:], dr["conv_b"].rearrange("(f p) -> p f", p=128)),
         inc=[("ld", 16)])
    P.on("sync", DMAS(S("ebcol")[:].rearrange("p (e f) -> p e f", f=16),
                     dr["e_b"].rearrange("e (f p) -> p e f", p=128)),
         inc=[("ld", 16)])
    P.on("sync", DMAS(S("cwcol")[:].rearrange("p (k f) -> p k f", f=32),
                      dr["conv_w"].rearrange("k (f p) -> p k f", p=128)),
         inc=[("ld", 16)])
    P.on("gpsimd", DMA(S("wr_bf")[:].rearrange("p (kt e) -> p kt e", kt=KD),
                       dr["r_w"].rearrange("(kt p) e -> p kt e", p=128)),
         inc=[("wd", 16)])
    wr_wd = P.v("wd")
    P.on("sync", DMA(S("nscb")[0:1, :], dr["nsc"].rearrange("(o d) -> o d", o=1)),
         inc=[("ld", 16)])
    P.on("sync", DMA(S("rb_b")[0:1, :], dr["r_b"].rearrange("(o e) -> o e", o=1)),
         inc=[("ld", 16)])
    ld0 = P.v("ld")

    P.on("scalar", SCPY(S("acol2")[:, 0:1], S("acol")[:, 0:1], func=AF.Exp),
         wait=[("ld", ld0)], inc=[("sc", 1)])
    P.on("vector", TS(S("acol")[:, 0:1], S("acol2")[:, 0:1], -1.0, None, ALU.mult),
         wait=[("sc", P.v("sc"))], inc=[("dv", 1)])
    fb_bc = [0]
    for b in range(4):
        P.on("tensor", MM(p_bc[:, 0:512], S("ones_f")[:],
                          S("nscb")[0:1, b * 512:(b + 1) * 512], True, True),
             wait=[("ld", ld0), ("dv", fb_bc[0])], inc=[("pe", 1)])
        P.on("vector", CPY(S("nscb")[:, b * 512:(b + 1) * 512], p_bc[:, 0:512]),
             wait=[("pe", P.v("pe"))], inc=[("dv", 1)])
        fb_bc[0] = P.v("dv")
    P.on("tensor", MM(p_bc[:, 0:8], S("ones_f")[:], S("rb_b")[0:1, :], True, True),
         wait=[("dv", fb_bc[0])], inc=[("pe", 1)])
    P.on("vector", CPY(S("rb_b")[:], p_bc[:, 0:8]),
         wait=[("pe", P.v("pe"))], inc=[("dv", 1)])
    fb_bc[0] = P.v("dv")
    init_dv = P.v("dv")

    def ln_stats(src, par, eps, mean_sq=False, pre_wait=()):
        eps_ap = S("cst4")[:, 0:1] if eps == LN_EPS else S("cst4")[:, 1:2]
        b0 = par * 24
        for i in range(4):
            P.on("vector", lambda e, i=i, src=src, b0=b0: e.bn_stats(
                S("bn24")[:, b0 + i * 6:b0 + (i + 1) * 6],
                src[:, i * 512:(i + 1) * 512]),
                 wait=(list(pre_wait) if i == 0 else []), inc=[("dv", 1)])
        P.on("vector", lambda e, b0=b0, par=par: e.bn_aggr(
            S("mv2")[:, par * 2:par * 2 + 2], S("bn24")[:, b0:b0 + 24]),
             wait=[("dv", P.v("dv"))], inc=[("dv", 1)])
        c = par * 4
        if mean_sq:
            P.on("vector", STT(S("rc")[:, c + 2:c + 3],
                               S("mv2")[:, par * 2:par * 2 + 1],
                               S("mv2")[:, par * 2:par * 2 + 1],
                               S("mv2")[:, par * 2 + 1:par * 2 + 2],
                               ALU.mult, ALU.add),
                 wait=[("dv", P.v("dv"))], inc=[("dv", 1)])
            var_ap = S("rc")[:, c + 2:c + 3]
        else:
            var_ap = S("mv2")[:, par * 2 + 1:par * 2 + 2]
        dvv = P.v("dv")
        P.on("scalar", SCPY(S("rc")[:, c + 3:c + 4], var_ap, bias=eps_ap,
                            func=AF.Sqrt),
             wait=[("dv", dvv)], inc=[("sc", 1)])
        P.on("vector", lambda e, c=c: e.reciprocal(S("rc")[:, c:c + 1],
                                                   S("rc")[:, c + 3:c + 4]),
             wait=[("sc", P.v("sc"))], inc=[("dv", 1)])
        P.on("vector", TS(S("rc")[:, c + 1:c + 2],
                          S("mv2")[:, par * 2:par * 2 + 1],
                          S("rc")[:, c:c + 1], -1.0, ALU.mult, ALU.mult),
             wait=[("dv", P.v("dv"))], inc=[("dv", 1)])
        return P.v("dv"), c

    h2T = S("h2T")
    ptr_free = [0]
    a_free = {0: (0, 0), 1: (0, 0)}
    astat_sc = {0: 0, 1: 0}
    a_done_sc = {}
    prev_xld = [ld0]
    for tt in range(NTT):
        par = tt % 2
        x0 = S("x0t")[:, par * DM:(par + 1) * DM]
        fsc, ftr = a_free[par]
        P.on("sync", DMA(x0, dr["x_in"][tt * 128:(tt + 1) * 128, :]),
             wait=[("ld", prev_xld[0]), ("sc", fsc), ("sc", ftr)],
             inc=[("ld", 16)])
        ldv = P.v("ld")
        prev_xld[0] = ldv
        b0 = par * 24
        for i in range(4):
            P.on("vector", lambda e, i=i, x0=x0, b0=b0: e.bn_stats(
                S("bn24")[:, b0 + i * 6:b0 + (i + 1) * 6],
                x0[:, i * 512:(i + 1) * 512]),
                 wait=([("ld", ldv), ("dv", init_dv),
                        ("sc", astat_sc[par])] if i == 0 else []),
                 inc=[("dv", 1)])
        P.on("vector", lambda e, b0=b0, par=par: e.bn_aggr(
            S("mv2")[:, par * 2:par * 2 + 2], S("bn24")[:, b0:b0 + 24]),
             wait=[("dv", P.v("dv"))], inc=[("dv", 1)])
        agg_dv = P.v("dv")
        c = par * 4
        mean_ap = S("mv2")[:, par * 2:par * 2 + 1]
        var_ap = S("mv2")[:, par * 2 + 1:par * 2 + 2]
        P.on("vector", TS(S("rc")[:, c + 3:c + 4], var_ap,
                          1.0 + RMS_EPS, RMS_EPS * LN_EPS, ALU.mult, ALU.add),
             wait=[("dv", agg_dv)], inc=[("dv", 1)])
        P.on("scalar", SCPY(S("rc")[:, c + 2:c + 3], S("rc")[:, c + 3:c + 4],
                            func=AF.Sqrt),
             wait=[("dv", P.v("dv"))], inc=[("sc", 1)])
        P.on("vector", lambda e, c=c: e.reciprocal(S("rc")[:, c:c + 1],
                                                   S("rc")[:, c + 2:c + 3]),
             wait=[("sc", P.v("sc"))], inc=[("dv", 1)])
        P.on("vector", TS(S("rc")[:, c + 1:c + 2], mean_ap,
                          S("rc")[:, c:c + 1], -1.0, ALU.mult, ALU.mult),
             wait=[("dv", P.v("dv"))], inc=[("dv", 1)])
        main_dv = P.v("dv")
        ac = S("astats")[:, 2 * tt:2 * tt + 1]
        ac1 = S("astats")[:, 2 * tt + 1:2 * tt + 2]
        P.on("scalar", SCPY(ac, var_ap, bias=S("cst4")[:, 0:1], func=AF.Sqrt),
             wait=[("dv", agg_dv)], inc=[("sc", 1)])
        P.on("vector", lambda e, ac=ac, ac1=ac1: e.reciprocal(ac1, ac),
             wait=[("sc", P.v("sc"))], inc=[("dv", 1)])
        P.on("vector", TS(ac, mean_ap, ac1, -1.0, ALU.mult, ALU.mult),
             wait=[("dv", P.v("dv"))], inc=[("dv", 1)])
        astat_sc[par] = P.v("sc")
        P.on("scalar", SCPY(S("zrst")[:], x0, bias=S("rc")[:, c + 1:c + 2],
                            scale=S("rc")[:, c:c + 1]),
             wait=[("dv", main_dv), ("dv", P.v("dv"))], inc=[("sc", 1)])
        z_sc = P.v("sc")
        a_sc = P.v("sc")
        h2sl = S("h2s")[:, par * DM:(par + 1) * DM]
        P.on("vector", TT(h2sl, S("zrst")[:], S("nscb")[:], ALU.mult),
             wait=[("sc", a_sc)], inc=[("dv", 1)])
        hdv = P.v("dv")
        h2T_r = h2T[:].rearrange("p (kt t) -> p kt t", kt=KD)
        for bi in range(2):
            for j in range(8):
                kt = bi * 8 + j
                P.on("tensor", TRN(p_tr[:, j * 128:(j + 1) * 128],
                                   h2sl[:, kt * 128:(kt + 1) * 128],
                                   S("iden_bf")[:]),
                     wait=([("dv", hdv), ("sc", ptr_free[0]), ("ld", ld0)]
                           if j == 0 else []),
                     inc=[("pe", 1)])
            pev = P.v("pe")
            P.on("scalar", lambda e, bi=bi, tt=tt, h2T_r=h2T_r: e.copy(
                h2T_r[:, bi * 8:(bi + 1) * 8, tt * 128:(tt + 1) * 128],
                p_tr[:, 0:1024].rearrange("p (j t) -> p j t", j=8)),
                 wait=[("pe", pev)], inc=[("sc", 1)])
            ptr_free[0] = P.v("sc")
        a_free[par] = (z_sc, P.v("sc"))
        a_done_sc[tt] = P.v("sc")
    A_tr = P.v("sc")

    tp = mktap("h2T", [128, KD * T], BF16)
    if tp is not None:
        P.on("sync", DMA(tp, h2T[:]), wait=[("sc", A_tr)], inc=[("st", 16)])

    wpan = S("wpanB")
    PANB = KD * 512
    wpan_free = [0, 0]
    mm_free = [dict(dv=0, sc=0), dict(dv=0, sc=0)]
    unit = [0]

    def load_panel(buf, f0, ncols):
        dst = wpan[:, buf * PANB:buf * PANB + KD * ncols].rearrange(
            "p (kt f) -> p kt f", kt=KD)
        src = dr["w_in"].rearrange("(kt p) f -> p kt f", p=128)[:, :, f0:f0 + ncols]
        for pc in range(4):
            k0, k1 = pc * 4, pc * 4 + 4
            P.on("gpsimd", DMA(dst[:, k0:k1], src[:, k0:k1]),
                 wait=[("pe", wpan_free[buf])], inc=[("wd", 16)])
        return P.v("wd")

    def acc_tile(buf, ncols, fcol, tb, mrows, wd_v, extra=(), a_sc=None):
        par = unit[0] % 2
        unit[0] += 1
        pp = p_mm[par]
        fr = mm_free[par]
        for kt in range(KD):
            w = ([("wd", wd_v), ("sc", A_tr if a_sc is None else a_sc),
                  ("dv", fr["dv"]),
                  ("sc", fr["sc"])] + list(extra)) if kt == 0 else []
            P.on("tensor", MM(pp[0:mrows, 0:512],
                              wpan[:, buf * PANB + kt * ncols + fcol:
                                   buf * PANB + kt * ncols + fcol + mrows],
                              h2T[:, kt * T + tb * 512:kt * T + (tb + 1) * 512],
                              kt == 0, kt == KD - 1),
                 wait=w, inc=[("pe", 1)] if kt == KD - 1 else ())
        return pp, par, P.v("pe")

    spill_free = {0: 0, 1: 0}
    sgrow_free = {0: 0, 1: 0}
    prev_conv = [0, 0]
    prev_silu = [0, 0]

    for fcb in range(8):
        buf = fcb % 2
        wd_v = load_panel(buf, DI + fcb * 512, 512)
        for ft in range(4):
            frow = fcb * 4 + ft
            rbuf = frow % 2
            cw = S("cwcol")
            last_silu = 0
            for tb in range(4):
                pp, par, pe_v = acc_tile(buf, 512, ft * 128, tb, 128, wd_v,
                                         a_sc=a_done_sc[4 * tb + 3])
                P.on("scalar", SCPY(S("convrow")[:, tb * 512:(tb + 1) * 512],
                                    pp[:, 0:512]),
                     wait=[("pe", pe_v),
                           ("dv", prev_conv[0] if tb == 0 else prev_conv[1])],
                     inc=[("sc", 1)])
                mm_free[par]["sc"] = P.v("sc")
                if tb in (1, 3):
                    half = tb // 2
                    a, b2 = half * 1024, (half + 1) * 1024
                    evac_sc = P.v("sc")
                    P.on("vector", TS(S("convtmp")[:, a:b2],
                                      S("convrow")[:, a:b2],
                                      cw[:, 3 * 32 + frow:3 * 32 + frow + 1],
                                      None, ALU.mult),
                         wait=[("sc", evac_sc), ("sc", prev_silu[half]),
                               ("ld", ld0)],
                         inc=[("dv", 1)])
                    for k, sh in ((2, 1), (1, 2), (0, 3)):
                        lo = max(a, sh)
                        P.on("vector", STT(S("convtmp")[:, lo:b2],
                                           S("convrow")[:, lo - sh:b2 - sh],
                                           cw[:, k * 32 + frow:k * 32 + frow + 1],
                                           S("convtmp")[:, lo:b2],
                                           ALU.mult, ALU.add),
                             wait=[("dv", P.v("dv"))], inc=[("dv", 1)])
                    prev_conv[half] = P.v("dv")
                    P.on("scalar", SCPY(
                        S("ssmrow")[:, rbuf * T + a:rbuf * T + b2],
                        S("convtmp")[:, a:b2], func=AF.Silu,
                        bias=S("cbcol")[:, frow:frow + 1]),
                         wait=[("dv", prev_conv[half]),
                               ("ld", spill_free[rbuf])],
                         inc=[("sc", 1)])
                    prev_silu[half] = P.v("sc")
                    last_silu = P.v("sc")
            P.on("sync", DMA(dr["scr_ssm"][:, frow * T:(frow + 1) * T],
                             S("ssmrow")[:, rbuf * T:(rbuf + 1) * T]),
                 wait=[("sc", last_silu)], inc=[("ld", 16)])
            spill_free[rbuf] = P.v("ld")
        wpan_free[buf] = P.v("pe")
    ssm_ld = P.v("ld")

    buf = 0
    wd_v = load_panel(buf, 2 * DI, 288)
    for tb in range(4):
        pp, par, pe_v = acc_tile(buf, 288, 0, tb, 32, wd_v)
        P.on("vector", CPY(S("dtv")[:, tb * 512:(tb + 1) * 512],
                           pp[0:32, 0:512]),
             wait=[("pe", pe_v)], inc=[("dv", 1)])
        mm_free[par]["dv"] = P.v("dv")
    for name, fcol in (("bc_bf", 32), ("cc_bf", 160)):
        for tb in range(4):
            pp, par, pe_v = acc_tile(buf, 288, fcol, tb, 128, wd_v)
            P.on("scalar", SCPY(S(name)[:, tb * 512:(tb + 1) * 512], pp[:, 0:512]),
                 wait=[("pe", pe_v)], inc=[("sc", 1)])
            mm_free[par]["sc"] = P.v("sc")
    wpan_free[buf] = P.v("pe")
    tail_dv, tail_sc = P.v("dv"), P.v("sc")

    dts = S("dtv")[:, :]
    P.on("vector", TS(dts, dts, S("dtbcol")[:, 0:1], None, ALU.add),
         wait=[("dv", tail_dv), ("ld", ld0)], inc=[("dv", 1)])
    P.on("scalar", SCPY(S("t32a")[:], dts, func=AF.Abs),
         wait=[("dv", P.v("dv"))], inc=[("sc", 1)])
    P.on("scalar", SCPY(S("t32b")[:], S("t32a")[:], scale=-1.0, func=AF.Exp),
         wait=[("sc", P.v("sc"))], inc=[("sc", 1)])
    P.on("scalar", SCPY(S("t32a")[:], S("t32b")[:], bias=1.0, func=AF.Ln),
         wait=[("sc", P.v("sc"))], inc=[("sc", 1)])
    sp_sc = P.v("sc")
    P.on("vector", TS(dts, dts, 0.0, None, ALU.max),
         wait=[("sc", sp_sc)], inc=[("dv", 1)])
    P.on("vector", TT(dts, dts, S("t32a")[:], ALU.add),
         wait=[("dv", P.v("dv"))], inc=[("dv", 1)])
    P.on("vector", TS(S("t32a")[:], dts, S("acol")[:, 0:1], None, ALU.mult),
         wait=[("dv", P.v("dv"))], inc=[("dv", 1)])
    for ch in range(NCH):
        sl = slice(ch * QC, (ch + 1) * QC)
        P.on("vector", lambda e, sl=sl: e.tensor_tensor_scan(
            S("cumv")[:, sl], S("t32a")[:, sl], S("t32a")[:, sl], 0.0,
            op0=ALU.add, op1=ALU.bypass),
             wait=[("dv", P.v("dv"))], inc=[("dv", 1)])
    cum_dv = P.v("dv")
    P.on("scalar", SCPY(S("excum")[:], S("cumv")[:, :], func=AF.Exp),
         wait=[("dv", cum_dv)], inc=[("sc", 1)])
    P.on("vector", CPY(S("cumh")[:], S("cumv")[:]),
         wait=[("dv", cum_dv)], inc=[("dv", 1)])
    P.on("vector", TT(S("t32b")[:], S("cumv")[:], S("cumh")[:], ALU.subtract),
         wait=[("dv", P.v("dv"))], inc=[("dv", 1)])
    P.on("vector", CPY(S("cumm")[:], S("t32b")[:]),
         wait=[("dv", P.v("dv"))], inc=[("dv", 1)])
    P.on("vector", TT(S("t32b")[:], S("t32b")[:], S("cumm")[:], ALU.subtract),
         wait=[("dv", P.v("dv"))], inc=[("dv", 1)])
    P.on("vector", CPY(S("cuml")[:], S("t32b")[:]),
         wait=[("dv", P.v("dv"))], inc=[("dv", 1)])
    prep_dv, prep_sc = P.v("dv"), P.v("sc")

    for fcb in range(8):
        buf = fcb % 2
        wd_v = load_panel(buf, fcb * 512, 512)
        for ft in range(4):
            frow = fcb * 4 + ft
            rbuf = frow % 2
            grow = S("sgrow")[:, rbuf * T:(rbuf + 1) * T]
            for tb in range(4):
                pp, par, pe_v = acc_tile(buf, 512, ft * 128, tb, 128, wd_v)
                P.on("scalar", SCPY(grow[:, tb * 512:(tb + 1) * 512],
                                    pp[:, 0:512], func=AF.Silu),
                     wait=[("pe", pe_v), ("ld", sgrow_free[rbuf])],
                     inc=[("sc", 1)])
                mm_free[par]["sc"] = P.v("sc")
            P.on("sync", DMA(dr["scr_gate"][:, frow * T:(frow + 1) * T], grow),
                 wait=[("sc", P.v("sc"))], inc=[("ld", 16)])
            sgrow_free[rbuf] = P.v("ld")
        wpan_free[buf] = P.v("pe")
    gate_ld = P.v("ld")

    tp = mktap("cum", [32, T])
    if tp is not None:
        P.on("sync", DMA(tp, S("cumv")[:]),
             wait=[("sc", prep_sc), ("dv", prep_dv)], inc=[("st", 16)])
    tp = mktap("dt", [32, T])
    if tp is not None:
        P.on("sync", DMA(tp, S("dtv")[:]),
             wait=[("sc", prep_sc), ("dv", prep_dv)], inc=[("st", 16)])
    tp = mktap("bc", [128, T], BF16)
    if tp is not None:
        P.on("sync", DMA(tp, S("bc_bf")[:]), wait=[("sc", tail_sc)],
             inc=[("st", 16)])
    tp = mktap("cc", [128, T], BF16)
    if tp is not None:
        P.on("sync", DMA(tp, S("cc_bf")[:]), wait=[("sc", tail_sc)],
             inc=[("st", 16)])
    tp = mktap("scr_ssm", [128, NH * T], BF16)
    if tp is not None:
        P.on("sync", DMA(tp, dr["scr_ssm"]), wait=[("ld", ssm_ld)],
             inc=[("st", 16)])
    tp = mktap("scr_gate", [128, NH * T], BF16)
    if tp is not None:
        P.on("sync", DMA(tp, dr["scr_gate"]), wait=[("ld", gate_ld)],
             inc=[("st", 16)])

    ywin, xtch = S("ywin"), S("xtch")
    sst, sbf = S("s_st"), S("sbf")
    P.fence_all()
    for db in range(4):
        for kh in range(2):
            k0, k1 = kh * 16, kh * 16 + 16
            P.on("gpsimd", DMA(
                dr["scr_wo"].rearrange("p (db kt f) -> p db kt f",
                                       db=4, kt=KI)[:, db, k0:k1, :],
                dr["w_out"].rearrange("(kt p) f -> p kt f", p=128)[
                    :, k0:k1, db * 512:(db + 1) * 512]),
                 inc=[("wd", 16)])
    wo_wd = P.v("wd")
    for db in range(4):
        P.on("gpsimd", DMA(
            dr["scr_pw"].rearrange("p (db kt f) -> p db kt f",
                                   db=4, kt=KD)[:, db],
            dr["p_w"].rearrange("(kt p) f -> p kt f", p=128)[
                :, :, db * 512:(db + 1) * 512]),
             inc=[("wd", 16)])
    pw_wd = P.v("wd")
    P.on("vector", MEMS(sst[:], 0.0), inc=[("dv", 1)])
    P.on("vector", MEMS(sbf[:], 0.0), inc=[("dv", 1)])
    sinit_dv = P.v("dv")

    pair_ld = {}
    chunk_done_dv = {-1: 0}
    yspill_ld = {}
    gwin_ld = {}
    gmul_dv = [0]

    def load_pair(cp):
        gate_free = chunk_done_dv.get(2 * cp - 1, 0)
        xw = [("ld", ssm_ld), ("ld", yspill_ld.get(cp - 1, 0)),
              ("dv", gate_free)]
        for pc in range(4):
            h0, h1 = pc * 8, pc * 8 + 8
            for ci in range(2):
                src = dr["scr_ssm"].rearrange("p (h t) -> p h t", h=NH)[
                    :, h0:h1, (2 * cp + ci) * QC:(2 * cp + ci + 1) * QC]
                dst = ywin[:].rearrange("p (c h i) -> p c h i", c=2, h=NH)[
                    :, ci, h0:h1]
                P.on("sync", DMA(dst, src), wait=xw, inc=[("ld", 16)])
        pair_ld[cp] = P.v("ld")
        gw = [("ld", gate_ld), ("dv", gmul_dv[0])]
        for pc in range(4):
            h0, h1 = pc * 8, pc * 8 + 8
            for ci in range(2):
                src = dr["scr_gate"].rearrange("p (h t) -> p h t", h=NH)[
                    :, h0:h1, (2 * cp + ci) * QC:(2 * cp + ci + 1) * QC]
                dst = S("gwin")[:].rearrange("p (c h i) -> p c h i",
                                             c=2, h=NH)[:, ci, h0:h1]
                P.on("sync", DMA(dst, src), wait=gw, inc=[("ld", 16)])
        gwin_ld[cp] = P.v("ld")

    def yw(cpar, h):
        return ywin[:, cpar * NH * QC + h * QC:cpar * NH * QC + (h + 1) * QC]

    load_pair(0)
    fr_bc = fb_bc
    fr_cb = [0]
    fr_y = [0]
    fr_sl = [0]
    fr_x = [0]
    sbf_rd = {h: 0 for h in range(NH)}
    xt_last_sc = [A_tr * 0]
    prev = dict(s1=0, t3=0, t4=0, t5=0, d1=0, d4=0, d7=0, s2=0)
    prevu = [dict(s1=0, t3=0, t4=0, t5=0, d2=0, d4=0),
             dict(s1=0, t3=0, t4=0, t5=0, d2=0, d4=0)]
    chunk_rd = dict(cjlt=0, wcolt=0, cdbc=0, cbtri=0, bct=0)

    for c in range(NCH):
        cpar = c % 2
        csl = slice(c * QC, (c + 1) * QC)
        if c % 2 == 0 and c > 0:
            load_pair(c // 2)
        ld_need = pair_ld[c // 2]
        P.on("tensor", MM(p_cb[:, 0:128], S("bc_bf")[:, csl],
                          S("cc_bf")[:, csl], True, True),
             wait=[("sc", tail_sc), ("dv", fr_cb[0])], inc=[("pe", 1)])
        P.on("vector", TT(S("cbtri")[:], p_cb[:, 0:128], S("tri")[:], ALU.mult),
             wait=[("pe", P.v("pe")), ("dv", chunk_rd["cbtri"]), ("ld", ld0)],
             inc=[("dv", 1)])
        fr_cb[0] = P.v("dv")
        cbtri_dv = P.v("dv")
        P.on("tensor", TRN(p_tr[:, 0:128], S("bc_bf")[:, csl], S("iden_bf")[:]),
             wait=[("sc", xt_last_sc[0])], inc=[("pe", 1)])
        P.on("scalar", SCPY(S("bct_bf")[:], p_tr[:, 0:128]),
             wait=[("pe", P.v("pe")), ("dv", chunk_rd["bct"])], inc=[("sc", 1)])
        bct_sc = P.v("sc")
        P.on("scalar", SCPY(S("lnst")[:], S("dtv")[:, csl], func=AF.Ln),
             wait=[("sc", prep_sc), ("dv", prep_dv)], inc=[("sc", 1)])
        P.on("vector", TT(S("cjlst")[:], S("cumv")[:, csl], S("lnst")[:],
                          ALU.subtract),
             wait=[("sc", P.v("sc"))], inc=[("dv", 1)])
        cjl_dv = P.v("dv")
        P.on("vector", TS(S("wst")[:], S("cumv")[:, csl],
                          S("cumv")[:, c * QC + QC - 1:c * QC + QC],
                          None, ALU.subtract),
             inc=[("dv", 1)])
        P.on("scalar", SCPY(S("wst2")[:], S("wst")[:], scale=-1.0, func=AF.Exp),
             wait=[("dv", P.v("dv"))], inc=[("sc", 1)])
        P.on("vector", TT(S("wst")[:], S("wst2")[:], S("dtv")[:, csl],
                          ALU.mult),
             wait=[("sc", P.v("sc"))], inc=[("dv", 1)])
        w_dv = P.v("dv")
        P.on("tensor", TRN(p_x[:, 0:32], S("cjlst")[:], S("iden_f")[0:32, 0:32]),
             wait=[("dv", cjl_dv), ("dv", fr_x[0])], inc=[("pe", 1)])
        P.on("tensor", TRN(p_x[:, 32:64], S("wst")[:], S("iden_f")[0:32, 0:32]),
             wait=[("dv", w_dv)], inc=[("pe", 1)])
        P.on("tensor", TRN(p_x[0:1, 64:96],
                           S("cumv")[:, c * QC + QC - 1:c * QC + QC],
                           S("iden_f")[0:32, 0:32]),
             wait=[("dv", cum_dv)], inc=[("pe", 1)])
        pev = P.v("pe")
        P.on("vector", CPY(S("cjlt")[:], p_x[:, 0:32]),
             wait=[("pe", pev), ("dv", chunk_rd["cjlt"])], inc=[("dv", 1)])
        P.on("vector", CPY(S("wcolt")[:], p_x[:, 32:64]),
             wait=[("dv", chunk_rd["wcolt"])], inc=[("dv", 1)])
        P.on("scalar", SCPY(S("cdrow")[:], p_x[0:1, 64:96]),
             wait=[("pe", pev)], inc=[("sc", 1)])
        P.on("scalar", SCPY(S("cdrow2")[:], S("cdrow")[:], func=AF.Exp),
             wait=[("sc", P.v("sc"))], inc=[("sc", 1)])
        fr_x[0] = P.v("dv")
        cjl_ev = P.v("dv")
        P.on("tensor", MM(p_bc[:, 0:32], S("ones_f")[:], S("cdrow2")[:],
                          True, True),
             wait=[("sc", P.v("sc")), ("dv", fr_bc[0])], inc=[("pe", 1)])
        P.on("vector", CPY(S("cdbc")[:], p_bc[:, 0:32]),
             wait=[("pe", P.v("pe")), ("dv", chunk_rd["cdbc"])], inc=[("dv", 1)])
        fr_bc[0] = P.v("dv")
        shared_dv = P.v("dv")

        xt_sc = {}
        for bi in range(4):
            for j in range(8):
                h = bi * 8 + j
                P.on("tensor", TRN(p_tr[:, j * 128:(j + 1) * 128],
                                   yw(cpar, h), S("iden_bf")[:]),
                     wait=([("ld", ld_need), ("sc", bct_sc),
                            ("pe", prev["t5"])] if (bi == 0 and j == 0)
                           else ([("sc", xt_sc[bi - 1])] if j == 0 else [])),
                     inc=[("pe", 1)])
            pev = P.v("pe")
            P.on("scalar", SCPY(xtch[:, bi * 1024:(bi + 1) * 1024],
                                p_tr[:, 0:1024]),
                 wait=[("pe", pev)], inc=[("sc", 1)])
            xt_sc[bi] = P.v("sc")
        xt_last_sc[0] = xt_sc[3]

        stash = {}

        def stage1(h):
            up = h % 2
            usl = slice(up * 128, (up + 1) * 128)
            e_t = S("e_t")[:, usl]
            et2 = S("et2")[:, usl]
            dt_t = S("dt_t")[:, usl]
            mt_bf = S("mt_bf")[:, usl]
            ce_bf = S("ce_bf")[:, usl]
            bw_bf = S("bw_bf")[:, usl]
            prev2 = prevu[up]
            sel = S("sel32b")[:, h * 128:(h + 1) * 128]
            P.on("tensor", MM(p_bc[:, 0:128], sel, S("cumh")[:, csl],
                              True, False),
                 wait=[("dv", fr_bc[0]), ("dv", prep_dv)], inc=[("pe", 1)])
            P.on("tensor", MM(p_bc[:, 0:128], sel, S("cumm")[:, csl],
                              False, False), inc=[("pe", 1)])
            P.on("tensor", MM(p_bc[:, 0:128], sel, S("cuml")[:, csl],
                              False, True), inc=[("pe", 1)])
            t1 = P.v("pe")
            P.on("tensor", MM(p_cb[:, 0:128], sel, S("excum")[:, csl],
                              True, True),
                 wait=[("dv", fr_cb[0]), ("sc", prep_sc)], inc=[("pe", 1)])
            t2 = P.v("pe")
            P.on("vector", TS(e_t, p_bc[:, 0:128],
                              S("cjlt")[:, h:h + 1], CLAMP,
                              ALU.subtract, ALU.min),
                 wait=[("pe", t1), ("dv", cjl_ev), ("sc", prev2["s1"])],
                 inc=[("dv", 1)])
            fr_bc[0] = P.v("dv")
            d1 = P.v("dv")
            P.on("scalar", SCPY(et2, e_t, func=AF.Exp),
                 wait=[("dv", d1), ("dv", prev2["d2"])], inc=[("sc", 1)])
            s1 = P.v("sc")
            P.on("vector", TT(ce_bf, p_cb[:, 0:128], S("cc_bf")[:, csl],
                              ALU.mult),
                 wait=[("pe", t2), ("pe", prev2["t4"])], inc=[("dv", 1)])
            fr_cb[0] = P.v("dv")
            d3 = P.v("dv")
            P.on("vector", TS(bw_bf, S("bct_bf")[:],
                              S("wcolt")[:, h:h + 1], None, ALU.mult),
                 wait=[("sc", bct_sc), ("dv", shared_dv), ("pe", prev2["t5"])],
                 inc=[("dv", 1)])
            d6 = P.v("dv")
            P.on("vector", TT(mt_bf, et2, S("cbtri")[:], ALU.mult),
                 wait=[("sc", s1), ("dv", cbtri_dv), ("pe", prev2["t3"])],
                 inc=[("dv", 1)])
            d2 = P.v("dv")
            P.on("scalar", SCPY(dt_t, yw(cpar, h),
                                scale=S("dcol")[:, h:h + 1]),
                 wait=[("ld", ld_need), ("ld", ld0), ("dv", prev2["d4"])],
                 inc=[("sc", 1)])
            s2 = P.v("sc")
            stash[h] = dict(mt=mt_bf, ce=ce_bf, bw=bw_bf, dt_t=dt_t,
                            d2=d2, d3=d3, d6=d6, s1=s1, s2=s2, up=up)

        def stage2(h):
            st2 = stash.pop(h)
            up = st2["up"]
            xt = xtch[:, h * 128:(h + 1) * 128]
            P.on("tensor", MM(p_y[:, 0:128], xt, st2["mt"], True, False),
                 wait=[("dv", st2["d2"]), ("sc", xt_sc[h // 8]),
                       ("dv", fr_y[0])],
                 inc=[("pe", 1)])
            t3 = P.v("pe")
            P.on("tensor", MM(p_y[:, 0:128], sbf[:, h * 128:(h + 1) * 128],
                              st2["ce"], False, True),
                 wait=[("dv", st2["d3"]), ("sc", sbf_rd[h]),
                       ("dv", sinit_dv)],
                 inc=[("pe", 1)])
            t4 = P.v("pe")
            P.on("tensor", MM(p_sl[:, 0:128], st2["bw"], xt, True, True),
                 wait=[("dv", st2["d6"]), ("dv", fr_sl[0])], inc=[("pe", 1)])
            t5 = P.v("pe")
            P.on("vector", TT(yw(cpar, h), st2["dt_t"], p_y[:, 0:128],
                              ALU.add),
                 wait=[("pe", t4), ("sc", st2["s2"])], inc=[("dv", 1)])
            fr_y[0] = P.v("dv")
            d4 = P.v("dv")
            P.on("vector", STT(sst[:, h * 128:(h + 1) * 128],
                               sst[:, h * 128:(h + 1) * 128],
                               S("cdbc")[:, h:h + 1], p_sl[:, 0:128],
                               ALU.mult, ALU.add),
                 wait=[("pe", t5), ("dv", shared_dv), ("sc", sbf_rd[h])],
                 inc=[("dv", 1)])
            fr_sl[0] = P.v("dv")
            d7 = P.v("dv")
            P.on("scalar", SCPY(sbf[:, h * 128:(h + 1) * 128],
                                sst[:, h * 128:(h + 1) * 128]),
                 wait=[("dv", d7)], inc=[("sc", 1)])
            sbf_rd[h] = P.v("sc")
            prev["t5"] = t5
            prevu[up] = dict(s1=st2["s1"], t3=t3, t4=t4, t5=t5,
                             d2=st2["d2"], d4=d4)

        for h in range(NH):
            stage1(h)
            if h > 0:
                stage2(h - 1)
        stage2(NH - 1)
        chunk_done_dv[c] = P.v("dv")
        chunk_rd = dict(cjlt=P.v("dv"), wcolt=P.v("dv"), cdbc=P.v("dv"),
                        cbtri=P.v("dv"), bct=P.v("dv"))
        if c % 2 == 1:
            P.on("vector", TT(ywin[:], ywin[:], S("gwin")[:], ALU.mult),
                 wait=[("dv", P.v("dv")), ("ld", gwin_ld[c // 2])],
                 inc=[("dv", 1)])
            gmul_dv[0] = P.v("dv")
            for ci in range(2):
                P.on("sync", DMA(
                    dr["scr_y"][:, (c - 1 + ci) * NH * QC:(c + ci) * NH * QC],
                    ywin[:, ci * NH * QC:(ci + 1) * NH * QC]),
                     wait=[("dv", P.v("dv"))], inc=[("ld", 16)])
            yspill_ld[c // 2] = P.v("ld")
    y_ld = P.v("ld")
    C_pe = P.v("pe")

    tp = mktap("scr_y", [128, NH * T], BF16)
    if tp is not None:
        P.on("sync", DMA(tp, dr["scr_y"]), wait=[("ld", y_ld)], inc=[("st", 16)])

    P.fence_all()
    wpanD = S("wpanD")
    PAND = KI * 512
    wD_free = [C_pe, C_pe]
    ylhs = S("ylhs")
    ylhs_free = [C_pe]
    mmD = [dict(sc=0), dict(sc=0)]
    x0b_free = {0: 0, 1: 0}
    ptrD_free = [P.v("sc")]
    w1_free = {0: (0, 0, 0), 1: (0, 0, 0)}
    dstage_free = {q: 0 for q in range(4)}
    fwt_sc = 0

    def load_ylhs(tblk, slot_pe=None):
        for sidx in range(4):
            tt = tblk * 4 + sidx
            fpe = slot_pe[sidx] if slot_pe is not None else ylhs_free[0]
            P.on("sync", DMA(
                ylhs[:, sidx * KI * 128:(sidx + 1) * KI * 128],
                dr["scr_y"][:, tt * NH * QC:(tt + 1) * NH * QC]),
                 wait=[("ld", y_ld), ("pe", fpe)],
                 inc=[("wd", 16)])
        return P.v("wd")

    def yl(slot, ktf):
        return ylhs[:, slot * KI * 128 + ktf * 128:slot * KI * 128 + (ktf + 1) * 128]

    def load_wD(buf, d0):
        db = d0 // 512
        P.on("sync", DMA(
            wpanD[:, buf * PAND:(buf + 1) * PAND],
            dr["scr_wo"][:, db * KI * 512:(db + 1) * KI * 512]),
             wait=[("pe", wD_free[buf]), ("wd", wo_wd)], inc=[("wd", 16)])
        return P.v("wd")

    lnt2_free = [0, 0]
    w1b_free = {s: 0 for s in range(6)}
    pend_p2 = []
    pend_fw = []
    fwt_scL = [0]

    def flush_fw():
        for tt, fwdv, fpar in pend_fw:
            P.on("tensor", TRN(p_x[0:8, 128:256],
                               S("fw8")[:, fpar * 8:fpar * 8 + 8],
                               S("iden_f")[:]),
                 wait=[("dv", fwdv), ("sc", fwt_scL[0])], inc=[("pe", 1)])
            P.on("scalar", SCPY(S("fwt")[0:8, tt * 128:(tt + 1) * 128],
                                p_x[0:8, 128:256]),
                 wait=[("pe", P.v("pe"))], inc=[("sc", 1)])
            fwt_scL[0] = P.v("sc")
        del pend_fw[:]

    def emit_pass2(info):
        tt, slot, wbsc, w1b = info["tt"], info["slot"], info["wbsc"], info["w1b"]
        par = tt % 2
        flush_fw()
        w1t = S("w1tst")[:, par * DM:(par + 1) * DM]
        w1t_r = w1t.rearrange("p (kt i) -> p kt i", kt=KD)
        ftr, fld, fpe = w1_free[par]
        for bi in range(2):
            for j in range(8):
                kt = bi * 8 + j
                P.on("tensor", TRN(p_tr[:, j * 128:(j + 1) * 128],
                                   w1b[:, kt * 128:(kt + 1) * 128],
                                   S("iden_bf")[:]),
                     wait=([("sc", wbsc), ("sc", ptrD_free[0]),
                            ("ld", fld), ("pe", fpe)] if j == 0 else []),
                     inc=[("pe", 1)])
            pev = P.v("pe")
            P.on("scalar", lambda e, bi=bi, w1t_r=w1t_r: e.copy(
                w1t_r[:, bi * 8:(bi + 1) * 8, :],
                p_tr[:, 0:1024].rearrange("p (j t) -> p j t", j=8)),
                 wait=[("pe", pev)], inc=[("sc", 1)])
            ptrD_free[0] = P.v("sc")
        w1b_free[slot] = P.v("pe")
        trv = P.v("sc")
        P.on("sync", DMA(
            dr["scr_w1t"].rearrange("p (kt t) -> p kt t", kt=KD)[
                :, :, tt * 128:(tt + 1) * 128],
            w1t.rearrange("p (kt i) -> p kt i", kt=KD)),
             wait=[("sc", trv)], inc=[("ld", 16)])
        for kt in range(KD):
            P.on("tensor", MM(p_x[0:128, 0:8],
                              w1t[:, kt * 128:(kt + 1) * 128],
                              S("wr_bf")[:, kt * 8:(kt + 1) * 8],
                              kt == 0, kt == KD - 1),
                 wait=([("sc", trv), ("wd", wr_wd), ("dv", fr_x[0])]
                       if kt == 0 else []),
                 inc=[("pe", 1)] if kt == KD - 1 else ())
        pev = P.v("pe")
        w1_free[par] = (trv, P.v("ld"), pev)
        P.on("vector", TT(S("lg8")[:], p_x[:, 0:8], S("rb_b")[:], ALU.add),
             wait=[("pe", pev)], inc=[("dv", 1)])
        fr_x[0] = P.v("dv")
        P.on("vector", lambda e: e.max(out=S("m8")[:], in_=S("lg8")[:]),
             wait=[("dv", P.v("dv"))], inc=[("dv", 1)])
        P.on("vector", MEMS(S("mv8")[:], NEG), inc=[("dv", 1)])
        P.on("vector", CPY(S("mv8")[:, 0:2], S("m8")[:, 0:2]),
             wait=[("dv", P.v("dv"))], inc=[("dv", 1)])
        P.on("vector", lambda e: e.match_replace(
            out=S("mr8")[:], in_to_replace=S("mv8")[:],
            in_values=S("lg8")[:], imm_value=NEG),
             wait=[("dv", P.v("dv"))], inc=[("dv", 1)])
        P.on("vector", TT(S("rc")[:, 6:7], S("m8")[:, 0:1], S("m8")[:, 1:2],
                          ALU.subtract),
             wait=[("dv", P.v("dv"))], inc=[("dv", 1)])
        dvv = P.v("dv")
        P.on("scalar", SCPY(S("rc")[:, 7:8], S("rc")[:, 6:7],
                            func=AF.Sigmoid),
             wait=[("dv", dvv)], inc=[("sc", 1)])
        P.on("scalar", SCPY(S("rc")[:, 6:7], S("rc")[:, 6:7], scale=-1.0,
                            func=AF.Sigmoid),
             wait=[("sc", P.v("sc"))], inc=[("sc", 1)])
        rsc = P.v("sc")
        P.on("vector", TT(S("rc")[:, 7:8], S("rc")[:, 7:8], S("rc")[:, 6:7],
                          ALU.subtract),
             wait=[("sc", rsc)], inc=[("dv", 1)])
        fpar = tt % 2
        fw8 = S("fw8")[:, fpar * 8:fpar * 8 + 8]
        P.on("vector", TS(fw8, S("lg8")[:], S("m8")[:, 0:1], None,
                          ALU.is_equal),
             inc=[("dv", 1)])
        P.on("vector", TS(fw8, fw8, S("rc")[:, 7:8],
                          S("rc")[:, 6:7], ALU.mult, ALU.add),
             wait=[("dv", P.v("dv"))], inc=[("dv", 1)])
        P.on("vector", TT(S("mv8")[:], S("mr8")[:], S("lg8")[:],
                          ALU.not_equal),
             inc=[("dv", 1)])
        P.on("vector", TT(fw8, fw8, S("mv8")[:], ALU.mult),
             wait=[("dv", P.v("dv"))], inc=[("dv", 1)])
        pend_fw.append((tt, P.v("dv"), fpar))

    slotq_pe = {}
    pending = load_ylhs(0)
    pend_wd = {}
    for tblk in range(4):
        blk_wd = pending
        for db in range(4):
            buf = db % 2
            wd_v = pend_wd.pop((tblk, db), None)
            if wd_v is None:
                wd_v = load_wD(buf, db * 512)
            for q in range(4):
                tt = tblk * 4 + q
                par = (q + db) % 2
                pp = p_mm[par]
                for ktf in range(KI):
                    w = ([("wd", wd_v), ("wd", blk_wd),
                          ("sc", mmD[par]["sc"]),
                          ("sc", dstage_free[q])] if ktf == 0 else [])
                    P.on("tensor", MM(pp[:, 0:512], yl(q, ktf),
                                      wpanD[:, buf * PAND + ktf * 512:
                                            buf * PAND + (ktf + 1) * 512],
                                      ktf == 0, ktf == KI - 1),
                         wait=w, inc=[("pe", 1)] if ktf == KI - 1 else ())
                pe_v = P.v("pe")
                if db == 3:
                    slotq_pe[q] = pe_v
                P.on("scalar", SCPY(
                    S("dstage")[:, q * DM + db * 512:
                                q * DM + (db + 1) * 512],
                    pp[:, 0:512]),
                     wait=[("pe", pe_v)], inc=[("sc", 1)])
                mmD[par]["sc"] = P.v("sc")
            wD_free[buf] = P.v("pe")
        ylhs_free[0] = P.v("pe")
        evD_sc = P.v("sc")
        if tblk < 3:
            pend_wd[(tblk + 1, 0)] = load_wD(0, 0)
            pending = load_ylhs(tblk + 1, slot_pe=dict(slotq_pe))
            pend_wd[(tblk + 1, 1)] = load_wD(1, 512)
        for info in pend_p2:
            emit_pass2(info)
        pend_p2 = []
        for q in range(4):
            tt = tblk * 4 + q
            slot = tt % 6
            par = tt % 2
            stg = S("dstage")[:, q * DM:(q + 1) * DM]
            x0 = S("x0t2")[:, par * DM:(par + 1) * DM]
            P.on("sync", DMA(x0, dr["x_in"][tt * 128:(tt + 1) * 128, :]),
                 wait=[("dv", x0b_free[par])], inc=[("ld", 16)])
            ldv = P.v("ld")
            P.on("scalar", SCPY(S("lnt2")[:], x0,
                                bias=S("astats")[:, 2 * tt:2 * tt + 1],
                                scale=S("astats")[:, 2 * tt + 1:2 * tt + 2]),
                 wait=[("ld", ldv), ("dv", lnt2_free[0]),
                       ("ld", lnt2_free[1])],
                 inc=[("sc", 1)])
            P.on("vector", TT(stg, stg, S("lnt2")[:], ALU.add),
                 wait=[("sc", P.v("sc")), ("sc", evD_sc)], inc=[("dv", 1)])
            attn_dv = P.v("dv")
            dvv, cc = ln_stats(stg, par, LN_EPS, pre_wait=[("dv", attn_dv)])
            P.on("scalar", SCPY(S("lnt2")[:], stg, bias=S("rc")[:, cc + 1:cc + 2],
                                scale=S("rc")[:, cc:cc + 1]),
                 wait=[("dv", dvv)], inc=[("sc", 1)])
            stg_sc = P.v("sc")
            dstage_free[q] = stg_sc
            P.on("vector", TT(S("lnt2")[:], S("lnt2")[:], x0, ALU.add),
                 wait=[("sc", stg_sc)], inc=[("dv", 1)])
            x1dv = P.v("dv")
            x0b_free[par] = x1dv
            P.on("sync", DMA(dr["scr_x1"][tt * 128:(tt + 1) * 128, :],
                             S("lnt2")[:]),
                 wait=[("dv", x1dv)], inc=[("ld", 16)])
            x1_ld = P.v("ld")
            dvv, cc = ln_stats(S("lnt2"), par, LN_EPS,
                               pre_wait=[("dv", x1dv)])
            P.on("vector", CPY(S("x1stats")[:, 2 * tt:2 * tt + 2],
                               S("rc")[:, cc:cc + 2]),
                 wait=[("dv", dvv)], inc=[("dv", 1)])
            w1b = S("w1bf")[:, slot * DM:(slot + 1) * DM]
            P.on("scalar", SCPY(w1b, S("lnt2")[:],
                                bias=S("rc")[:, cc + 1:cc + 2],
                                scale=S("rc")[:, cc:cc + 1]),
                 wait=[("dv", dvv), ("ld", x1_ld),
                       ("pe", w1b_free[slot])],
                 inc=[("sc", 1)])
            wbsc = P.v("sc")
            lnt2_free = [P.v("dv"), x1_ld]
            pend_p2.append(dict(tt=tt, slot=slot, wbsc=wbsc, w1b=w1b))
    for info in pend_p2:
        emit_pass2(info)
    flush_fw()
    w1t_ld = P.v("ld")
    D_sc = P.v("sc")
    D_pe = P.v("pe")

    tp = mktap("x1", [T, DM])
    if tp is not None:
        P.on("sync", DMA(tp, dr["scr_x1"]), wait=[("ld", w1t_ld)],
             inc=[("st", 16)])
    tp = mktap("fwt", [8, T])
    if tp is not None:
        P.on("sync", DMA(tp, S("fwt")[:]), wait=[("sc", D_sc)], inc=[("st", 16)])
    tp = mktap("scr_w1t", [128, KD * T], BF16)
    if tp is not None:
        P.on("sync", DMA(tp, dr["scr_w1t"]), wait=[("ld", w1t_ld)],
             inc=[("st", 16)])

    web, w1blk, comb = S("web"), S("w1blk"), S("comb")
    PANE = KD * 512
    wE_free = [D_pe, D_pe]
    mmE = [dict(sc=0), dict(sc=0)]
    silu_free = [0, 0]
    comb_free = [0]
    w1blk_free = [D_pe]
    fwb_free = [0]

    wpanF = S("wpanF")
    PANF = KD * 512
    wF_free = [0, 0]
    mmF = [dict(sc=0, cb=0), dict(sc=0, cb=0)]
    cb_sc_map = {}
    fstage_free = {q: 0 for q in range(4)}
    fyt_free = [0]
    x1b_free = {0: 0, 1: 0}
    cbst_free = [0]

    def load_we(e_i, fcb, buf, fence_pe, fence_st):
        dst = web[:, buf * PANE:(buf + 1) * PANE].rearrange(
            "p (kt f) -> p kt f", kt=KD)
        src = dr["e_w"].rearrange("e (kt p) f -> e p kt f", p=128)[
            e_i][:, :, fcb * 512:(fcb + 1) * 512]
        for pc in range(4):
            k0, k1 = pc * 4, pc * 4 + 4
            P.on("gpsimd", DMA(dst[:, k0:k1], src[:, k0:k1]),
                 wait=[("pe", wE_free[buf]), ("pe", fence_pe),
                       ("st", fence_st)],
                 inc=[("wd", 16)])
        return P.v("wd")

    def load_wF(buf, d0, fence):
        db = d0 // 512
        P.on("sync", DMA(
            wpanF[:, buf * PANF:(buf + 1) * PANF],
            dr["scr_pw"][:, db * KD * 512:(db + 1) * KD * 512]),
             wait=[("pe", wF_free[buf]), ("pe", fence["pe"]),
                   ("dv", fence["dv"]), ("sc", fence["sc"]),
                   ("wd", pw_wd)],
             inc=[("wd", 16)])
        return P.v("wd")

    P.fence_all()
    P.on("sync", DMA(S("pbb")[0:1, :], dr["p_b"].rearrange("(o d) -> o d", o=1)),
         inc=[("ld", 16)])
    P.on("sync", DMA(S("nwb")[0:1, :], dr["n_w"].rearrange("(o d) -> o d", o=1)),
         inc=[("ld", 16)])
    ldv = P.v("ld")
    for nm in ("pbb", "nwb"):
        for b in range(4):
            P.on("tensor", MM(p_bc[:, 0:512], S("ones_f")[:],
                              S(nm)[0:1, b * 512:(b + 1) * 512], True, True),
                 wait=[("ld", ldv), ("dv", fr_bc[0])], inc=[("pe", 1)])
            P.on("vector", CPY(S(nm)[:, b * 512:(b + 1) * 512], p_bc[:, 0:512]),
                 wait=[("pe", P.v("pe"))], inc=[("dv", 1)])
            fr_bc[0] = P.v("dv")
    bb_dv = P.v("dv")

    for H in range(2):
        P.fence_all()
        fE = dict(pe=P.v("pe"), dv=P.v("dv"), sc=P.v("sc"), st=P.v("st"))
        P.on("sync", DMA(w1blk[:].rearrange("p (kt t) -> p kt t", kt=KD),
                         dr["scr_w1t"].rearrange("p (kt t) -> p kt t", kt=KD)[
                             :, :, H * TH:(H + 1) * TH]),
             wait=[("ld", w1t_ld), ("pe", w1blk_free[0]), ("pe", fE["pe"]),
                   ("dv", fE["dv"]), ("sc", fE["sc"]), ("st", fE["st"])],
             inc=[("ld", 16)])
        w1_ld2 = P.v("ld")
        P.on("vector", MEMS(comb[:], 0.0),
             wait=[("sc", comb_free[0])], inc=[("dv", 1)])
        cmdv = P.v("dv")
        wE_free = [fE["pe"], fE["pe"]]
        wd_first = load_we(0, 0, 0, fE["pe"], fE["st"])
        load_we(0, 1, 1, fE["pe"], fE["st"])
        wE_wd = {0: wd_first, 1: P.v("wd")}
        for e_i in range(NE):
            for bb in range(2):
                P.on("tensor", MM(p_bc[:, 0:512],
                                  S("sel8")[:, e_i * 128:(e_i + 1) * 128],
                                  S("fwt")[0:8, H * TH + bb * 512:
                                           H * TH + (bb + 1) * 512], True, True),
                     wait=[("sc", D_sc), ("dv", fr_bc[0]), ("dv", fwb_free[0])],
                     inc=[("pe", 1)])
                P.on("vector", CPY(S("fwb")[:, bb * 512:(bb + 1) * 512],
                                   p_bc[:, 0:512]),
                     wait=[("pe", P.v("pe")), ("sc", fE["sc"]),
                           ("st", fE["st"])],
                     inc=[("dv", 1)])
                fr_bc[0] = P.v("dv")
            fwb_dv = P.v("dv")
            for fcb in range(4):
                buf = fcb % 2
                wd_v = wE_wd[buf]
                for fi in range(4):
                    f = fcb * 4 + fi
                    for b in range(2):
                        par = (f * 2 + b) % 2
                        pp = p_mm[par]
                        for kt in range(KD):
                            w = ([("wd", wd_v), ("ld", w1_ld2),
                                  ("sc", mmE[par]["sc"])] if kt == 0 else [])
                            P.on("tensor", MM(
                                pp[:, 0:512],
                                web[:, buf * PANE + kt * 512 + fi * 128:
                                    buf * PANE + kt * 512 + fi * 128 + 128],
                                w1blk[:, kt * TH + b * 512:
                                      kt * TH + (b + 1) * 512],
                                kt == 0, kt == KD - 1),
                                 wait=w, inc=[("pe", 1)] if kt == KD - 1 else ())
                        pev = P.v("pe")
                        sl = S("silu1") if par == 0 else S("silu2")
                        P.on("scalar", SCPY(sl[:], pp[:, 0:512], func=AF.Silu,
                                            bias=S("ebcol")[:, e_i * 16 + f:
                                                            e_i * 16 + f + 1]),
                             wait=[("pe", pev), ("dv", silu_free[par]),
                                   ("ld", ld0), ("st", fE["st"])],
                             inc=[("sc", 1)])
                        mmE[par]["sc"] = P.v("sc")
                        ssc = P.v("sc")
                        P.on("vector", TT(S("prodt")[:], sl[:],
                                          S("fwb")[:, b * 512:(b + 1) * 512],
                                          ALU.mult),
                             wait=[("sc", ssc), ("dv", fwb_dv), ("dv", cmdv)],
                             inc=[("dv", 1)])
                        silu_free[par] = P.v("dv")
                        cslot = comb[:, f * TH + b * 512:f * TH + (b + 1) * 512]
                        P.on("vector", TT(cslot, cslot, S("prodt")[:], ALU.add),
                             wait=[("dv", P.v("dv"))], inc=[("dv", 1)])
                wE_free[buf] = P.v("pe")
                if fcb < 2:
                    wE_wd[buf] = load_we(e_i, fcb + 2, buf, fE["pe"], fE["st"])
                elif e_i < NE - 1:
                    wE_wd[buf] = load_we(e_i + 1, fcb - 2, buf, fE["pe"],
                                         fE["st"])
            fwb_free[0] = P.v("dv")
            if H == 0 and e_i == 0:
                tpx = mktap("sel8chk", [NE, NE * 128])
                if tpx is not None:
                    P.on("sync", DMA(tpx, S("sel8")[:]), wait=[("dv", P.v("dv"))],
                         inc=[("st", 16)])
                tpx = mktap("combE0", [128, KD * TH])
                if tpx is not None:
                    P.on("sync", DMA(tpx, comb[:]), wait=[("dv", P.v("dv"))],
                         inc=[("st", 16)])
                tpx = mktap("fwb0", [128, TH])
                if tpx is not None:
                    P.on("sync", DMA(tpx, S("fwb")[:]), wait=[("dv", P.v("dv"))],
                         inc=[("st", 16)])
        comb_dv = P.v("dv")
        if H == 0:
            tpx = mktap("comb0", [128, KD * TH])
            if tpx is not None:
                P.on("sync", DMA(tpx, comb[:]), wait=[("dv", comb_dv)],
                     inc=[("st", 16)])
        E_end_pe = P.v("pe")
        w1blk_free[0] = E_end_pe

        P.fence_all()
        fF = dict(pe=P.v("pe"), dv=P.v("dv"), sc=P.v("sc"), st=P.v("st"))
        wF_free = [fF["pe"], fF["pe"]]
        for tb4 in range(2):
            for db in range(4):
                buf = db % 2
                wd_v = load_wF(buf, db * 512, fF)
                for q in range(4):
                    ti = tb4 * 4 + q
                    tt = H * 8 + ti
                    slot = tt % 6
                    par = (q + db) % 2
                    pp = p_mm[par]
                    cb0 = q * KD * 128
                    if db == 0:
                        for kt in range(KD):
                            P.on("scalar", SCPY(
                                S("cbstage")[:, cb0 + kt * 128:
                                             cb0 + (kt + 1) * 128],
                                comb[:, kt * TH + ti * 128:
                                     kt * TH + ti * 128 + 128]),
                                 wait=([("dv", comb_dv),
                                        ("pe", cbst_free[0]),
                                        ("st", fF["st"])] if kt == 0 else []),
                                 inc=[("sc", 1)])
                        mmF[par]["cb"] = P.v("sc")
                        cbsc_by_ti = P.v("sc")
                        cb_sc_map[ti] = cbsc_by_ti
                    for kt in range(KD):
                        w = ([("wd", wd_v), ("sc", cb_sc_map[ti]),
                              ("dv", mmF[par]["sc"]),
                              ("st", fstage_free[q])] if kt == 0 else [])
                        P.on("tensor", MM(
                            pp[:, 0:512],
                            S("cbstage")[:, cb0 + kt * 128:cb0 + (kt + 1) * 128],
                            wpanF[:, buf * PANF + kt * 512:
                                  buf * PANF + (kt + 1) * 512],
                            kt == 0, kt == KD - 1),
                             wait=w, inc=[("pe", 1)] if kt == KD - 1 else ())
                    pev = P.v("pe")
                    P.on("vector", TT(
                        S("fstage")[:, q * DM + db * 512:
                                    q * DM + (db + 1) * 512],
                        pp[:, 0:512],
                        S("pbb")[:, db * 512:(db + 1) * 512], ALU.add),
                         wait=[("pe", pev), ("dv", fF["dv"]),
                               ("dv", bb_dv)],
                         inc=[("dv", 1)])
                    mmF[par]["sc"] = P.v("dv")
                wF_free[buf] = P.v("pe")
            cbst_free[0] = P.v("pe")
            evF_dv = P.v("dv")
            for q in range(4):
                ti = tb4 * 4 + q
                tt = H * 8 + ti
                slot = tt % 6
                par = tt % 2
                stg = S("fstage")[:, q * DM:(q + 1) * DM]
                x1 = S("x1t")[:, par * DM:(par + 1) * DM]
                P.on("sync", DMA(x1, dr["scr_x1"][tt * 128:(tt + 1) * 128, :]),
                     wait=[("ld", w1t_ld), ("dv", x1b_free[par]),
                           ("pe", fF["pe"]), ("dv", fF["dv"]),
                           ("sc", fF["sc"])],
                     inc=[("ld", 16)])
                ldv = P.v("ld")
                P.on("scalar", SCPY(S("fyt")[:], x1,
                                    bias=S("x1stats")[:, 2 * tt + 1:2 * tt + 2],
                                    scale=S("x1stats")[:, 2 * tt:2 * tt + 1]),
                     wait=[("ld", ldv), ("st", fF["st"]),
                           ("dv", fyt_free[0])],
                     inc=[("sc", 1)])
                w1sc = P.v("sc")
                P.on("vector", TT(stg, stg, S("fyt")[:], ALU.add),
                     wait=[("sc", w1sc), ("dv", evF_dv)],
                     inc=[("dv", 1)])
                ydv = P.v("dv")
                dvv, cc = ln_stats(stg, par, RMS_EPS, mean_sq=True,
                                   pre_wait=[("dv", ydv)])
                P.on("scalar", SCPY(S("fyt")[:], stg,
                                    scale=S("rc")[:, cc:cc + 1]),
                     wait=[("dv", dvv)], inc=[("sc", 1)])
                P.on("vector", TT(S("fyt")[:], S("fyt")[:], S("nwb")[:],
                                  ALU.mult),
                     wait=[("sc", P.v("sc"))], inc=[("dv", 1)])
                frdv = P.v("dv")
                dvv, cc = ln_stats(S("fyt"), par, LN_EPS,
                                   pre_wait=[("dv", frdv)])
                P.on("scalar", SCPY(stg, S("fyt")[:],
                                    bias=S("rc")[:, cc + 1:cc + 2],
                                    scale=S("rc")[:, cc:cc + 1]),
                     wait=[("dv", dvv)], inc=[("sc", 1)])
                P.on("vector", TT(stg, stg, x1, ALU.add),
                     wait=[("sc", P.v("sc"))], inc=[("dv", 1)])
                odv = P.v("dv")
                x1b_free[par] = odv
                fyt_free[0] = odv
                P.on("sync", DMA(dr["out"][tt * 128:(tt + 1) * 128, :], stg),
                     wait=[("dv", odv)], inc=[("st", 16)])
                fstage_free[q] = P.v("st")


    P.on("sync", None, wait=[("st", P.v("st"))])


def kernel(**inputs):
    xs = np.ascontiguousarray(inputs["L0_attn_x"]).astype(np.float32, copy=False)
    iden = np.eye(128, dtype=np.float32)
    consts = {
        "c_iden_bf": iden.astype(ml_dtypes.bfloat16),
        "c_iden_f": iden,
        "c_tri": np.triu(np.ones((128, 128), np.float32)),
        "c_sel32": np.repeat(np.eye(NH, dtype=np.float32), 128, axis=1),
        "c_sel32b": np.repeat(np.eye(NH, dtype=np.float32), 128,
                              axis=1).astype(ml_dtypes.bfloat16),
        "c_sel8": np.repeat(np.eye(NE, dtype=np.float32), 128, axis=1),
        "c_ones": np.ones((1, 128), np.float32),
    }
    wmap = {
        "w_in": inputs["L0_attn_in_proj_w"], "conv_w": inputs["L0_attn_conv_weight"],
        "conv_b": inputs["L0_attn_conv_bias"], "a_log": inputs["L0_attn_A_log"],
        "dt_bias": inputs["L0_attn_dt_bias"], "d_vec": inputs["L0_attn_D"],
        "w_out": inputs["L0_attn_out_proj_w"], "nsc": inputs["L0_attn_norm_scale"],
        "r_w": inputs["L0_ffn_router_w"], "r_b": inputs["L0_ffn_router_b"],
        "e_w": inputs["L0_ffn_expert_w"], "e_b": inputs["L0_ffn_expert_b"],
        "p_w": inputs["L0_ffn_proj_w"], "p_b": inputs["L0_ffn_proj_b"],
        "n_w": inputs["L0_ffn_norm_w"],
    }
    wmap = {k: np.ascontiguousarray(v, dtype=np.float32) for k, v in wmap.items()}
    nc, _ = build()
    in_maps = []
    for b in range(8):
        m = dict(consts)
        m.update(wmap)
        m["x_in"] = np.ascontiguousarray(xs[b])
        in_maps.append(m)
    res = run_bass_kernel_spmd(nc, in_maps, core_ids=list(range(8)))
    return np.stack([res.results[b]["out"] for b in range(8)], axis=0)
